# version 02
# baseline (speedup 1.0000x reference)
import contextlib
import numpy as np
import ml_dtypes

import concourse.bass as bass
import concourse.mybir as mybir
from concourse.bass_utils import run_bass_kernel_spmd

F32 = mybir.dt.float32
BF16 = mybir.dt.bfloat16
AF = mybir.ActivationFunctionType
ALU = mybir.AluOpType
AX = mybir.AxisListType
BF = ml_dtypes.bfloat16

NCORES = 8
D = 1024
TTOK = 16384
TCORE = 2048
NT = TCORE // 128
NC_D = D // 128


class SelfSync:

    def __init__(self, eng, sem, dummy=None):
        self.eng, self.sem, self.n = eng, sem, 0
        self.dummy = dummy

    def flush(self, inst):
        self.n += 1
        inst.then_inc(self.sem, 1)
        self.eng.wait_ge(self.sem, self.n)

    def barrier(self):
        self.n += 1
        self.eng.memset(self.dummy, 0.0).then_inc(self.sem, 1)
        self.eng.wait_ge(self.sem, self.n)


class Phase:

    def __init__(self, nc, prefix=""):
        self.nc, self.prefix = nc, prefix
        self.stack = contextlib.ExitStack()
        self.sems = []

    def __enter__(self):
        self.stack.__enter__()
        return self

    def sbuf(self, name, shape, dtype):
        return self.stack.enter_context(
            self.nc.sbuf_tensor(self.prefix + name, shape, dtype))

    def psum(self, name, shape, dtype):
        return self.stack.enter_context(
            self.nc.psum_tensor(self.prefix + name, shape, dtype))

    def sem(self, name):
        h = self.stack.enter_context(self.nc.semaphore(self.prefix + name))
        self.sems.append(h)
        return h

    def __exit__(self, *exc):
        if exc[0] is None and self.sems:
            sems = self.sems
            with self.nc.Block() as blk:
                @blk.gpsimd
                def _(g):
                    for h in sems:
                        g.sem_clear(range(h.num, h.num + 1))
        return self.stack.__exit__(*exc)



def _ln_stats(v, s, x_sb, sq0, sq1, sums, sumsq, mean, var, w0, w1, eps,
              sSq, sVc, ssync, n_tiles=NT, width=D):
    for ti in range(n_tiles):
        v.reduce_sum(sums[:, ti:ti + 1], x_sb[:, ti, :], AX.X)
    for ti in range(n_tiles):
        v.wait_ge(sSq, ti + 1)
        inst = v.reduce_sum(sumsq[:, ti:ti + 1], (sq0, sq1)[ti % 2][:, :], AX.X)
        inst.then_inc(sVc, 1)
    ssync.barrier()
    v.tensor_scalar(mean[:, :], sums[:, :], 1.0 / width, None, ALU.mult)
    inst = v.tensor_scalar(w0[:, :], sumsq[:, :], 1.0 / width, None, ALU.mult)
    ssync.flush(inst)
    inst = v.tensor_mul(w1[:, :], mean[:, :], mean[:, :])
    ssync.flush(inst)
    return v.tensor_sub(var[:, :], w0[:, :], w1[:, :])


def _ln_stats_scalar(s, x_sb, sq0, sq1, sSq, sVc, n_tiles=NT):
    for ti in range(n_tiles):
        if ti >= 2:
            s.wait_ge(sVc, ti - 1)
        s.activation((sq0, sq1)[ti % 2][:, :], x_sb[:, ti, :],
                     AF.Square).then_inc(sSq, 1)


def build_launch_A(dbg_xn_tm=False):
    nc = bass.Bass("TRN2", target_bir_lowering=False, debug=False,
                   detect_race_conditions=False)
    x_in = nc.dram_tensor("x", [TCORE, D], F32, kind="ExternalInput")
    wgate_in = nc.dram_tensor("wgate", [D, 7], F32, kind="ExternalInput")
    wst_in = nc.dram_tensor("wst", [D, D], F32, kind="ExternalInput")
    wet_in = nc.dram_tensor("wet", [D, D], F32, kind="ExternalInput")
    bias_in = nc.dram_tensor("bias", [1, D], F32, kind="ExternalInput")
    ones1_in = nc.dram_tensor("ones1", [1, 128], F32, kind="ExternalInput")
    ident_in = nc.dram_tensor("ident", [128, 128], F32, kind="ExternalInput")
    xnfm_out = nc.dram_tensor("xn_fm", [D, TCORE], BF16, kind="ExternalOutput")
    g_out = nc.dram_tensor("g", [TCORE, D], BF16, kind="ExternalOutput")

    emit_A(nc, "A_", x_in, wgate_in, wst_in, wet_in, bias_in, ones1_in, ident_in, xnfm_out, g_out, dbg_xn_tm=dbg_xn_tm)
    return nc


def emit_A(nc, PFX, x_in, wgate_in, wst_in, wet_in, bias_in, ones1_in, ident_in, xnfm_out, g_out, dbg_xn_tm=False):
    with Phase(nc, PFX) as ph:
        x_sb = ph.sbuf("x_sb", [128, NT, D], F32)
        xn_tm = ph.sbuf("xn_tm", [128, NT, D], BF16)
        xn_fm = ph.sbuf("xnf_sb", [128, NC_D, TCORE], BF16)
        g_sb = ph.sbuf("g_sb", [128, NT, D], BF16)
        wgate_sb = ph.sbuf("wgate_sb", [128, NC_D, 7], BF16)
        wst_sb = ph.sbuf("wst_sb", [128, NC_D, D], BF16)
        wet_sb = ph.sbuf("wet_sb", [128, NC_D, D], BF16)
        bias_sb = ph.sbuf("bias_sb", [1, D], BF16)
        ones1_sb = ph.sbuf("ones1_sb", [1, 128], BF16)
        ident_sb = ph.sbuf("ident_sb", [128, 128], BF16)
        eps_sb = ph.sbuf("eps_sb", [128, 1], F32)
        sdum_sb = ph.sbuf("sdum_sb", [128, 2], F32)
        sq0_sb = ph.sbuf("sq0_sb", [128, D], F32)
        sq1_sb = ph.sbuf("sq1_sb", [128, D], F32)
        sums_sb = ph.sbuf("sums_sb", [128, NT], F32)
        sumsq_sb = ph.sbuf("sumsq_sb", [128, NT], F32)
        mean_sb = ph.sbuf("mean_sb", [128, NT], F32)
        var_sb = ph.sbuf("var_sb", [128, NT], F32)
        w0_sb = ph.sbuf("w0_sb", [128, NT], F32)
        w1_sb = ph.sbuf("w1_sb", [128, NT], F32)
        std_sb = ph.sbuf("std_sb", [128, NT], F32)
        rstd_sb = ph.sbuf("rstd_sb", [128, NT], F32)
        lg_sb = ph.sbuf("lg_sb", [128, NT, 7], F32)
        fw_sb = ph.sbuf("fw_sb", [128, NT, 12], F32)
        e2_sb = ph.sbuf("e2_sb", [128, NT, 2], F32)
        fd_sb = ph.sbuf("fd_sb", [128, NT], F32)
        fr_sb = ph.sbuf("fr_sb", [128, NT], F32)
        fac_sb = ph.sbuf("fac_sb", [128, NT], F32)
        t_sb0 = sq0_sb[:, 0:512]
        t_sb1 = sq0_sb[:, 512:1024]
        s_sb0 = sq1_sb[:, 0:512]
        s_sb1 = sq1_sb[:, 512:1024]

        pt0 = ph.psum("pt0", [128, 1024], BF16)
        pt1 = ph.psum("pt1", [128, 1024], BF16)
        pl0 = ph.psum("pl0", [128, 512], F32)
        pl1 = ph.psum("pl1", [128, 512], F32)
        ps0 = ph.psum("ps0", [128, 512], F32)
        ps1 = ph.psum("ps1", [128, 512], F32)
        pe0 = ph.psum("pe0", [128, 512], F32)
        pe1 = ph.psum("pe1", [128, 512], F32)

        ld_x = ph.sem("ld_x")
        ld_w = ph.sem("ld_w")
        sSq = ph.sem("sSq")
        sSc = ph.sem("sSc")
        sVc = ph.sem("sVc")
        sVx = ph.sem("sVx")
        s2a = ph.sem("s2a")
        s2b = ph.sem("s2b")
        s3t = ph.sem("s3t")
        s3e = ph.sem("s3e")
        s4t = ph.sem("s4t")
        s4v = ph.sem("s4v")
        s4e = ph.sem("s4e")
        s5t = ph.sem("s5t")
        s5s = ph.sem("s5s")
        s5v = ph.sem("s5v")
        s5e = ph.sem("s5e")
        st = ph.sem("st")

        with nc.Block() as block:
            @block.sync
            def _(sync):
                sync.dma_start(out=x_sb[:, :, :],
                               in_=x_in.ap().rearrange("(T p) d -> p T d", p=128)).then_inc(ld_x, 16)

            @block.gpsimd
            def _(g):
                g.dma_start(out=wgate_sb[:, :, :],
                            in_=wgate_in.ap().rearrange("(c p) d -> p c d", p=128)).then_inc(ld_w, 16)
                g.dma_start(out=wst_sb[:, :, :],
                            in_=wst_in.ap().rearrange("(c p) d -> p c d", p=128)).then_inc(ld_w, 16)
                g.dma_start(out=wet_sb[:, :, :],
                            in_=wet_in.ap().rearrange("(c p) d -> p c d", p=128)).then_inc(ld_w, 16)
                g.dma_start(out=bias_sb[:, :], in_=bias_in[:, :]).then_inc(ld_w, 16)
                g.dma_start(out=ones1_sb[:, :], in_=ones1_in[:, :]).then_inc(ld_w, 16)
                g.dma_start(out=ident_sb[:, :], in_=ident_in[:, :]).then_inc(ld_w, 16)

        with nc.Block() as block:
            @block.vector
            def _(v):
                ssync = SelfSync(v, sVx, dummy=sdum_sb[:, 0:1])
                v.memset(eps_sb[:, :], 1e-5)
                v.wait_ge(ld_x, 16)
                inst = _ln_stats(v, None, x_sb, sq0_sb, sq1_sb, sums_sb, sumsq_sb,
                                 mean_sb, var_sb, w0_sb, w1_sb, eps_sb, sSq, sVc, ssync)
                inst.then_inc(s2a, 1)
                v.wait_ge(s2b, 1)
                inst = v.reciprocal(rstd_sb[:, :], std_sb[:, :])
                ssync.flush(inst)
                for ti in range(NT):
                    v.tensor_scalar(xn_tm[:, ti, :], x_sb[:, ti, :],
                                    mean_sb[:, ti:ti + 1], rstd_sb[:, ti:ti + 1],
                                    ALU.subtract, ALU.mult)

            @block.scalar
            def _(s):
                s.wait_ge(ld_x, 16)
                _ln_stats_scalar(s, x_sb, sq0_sb, sq1_sb, sSq, sVc)
                s.wait_ge(s2a, 1)
                s.activation(std_sb[:, :], var_sb[:, :], AF.Sqrt,
                             bias=eps_sb[:, 0:1]).then_inc(s2b, 1)

        with nc.Block() as block:
            @block.tensor
            def _(t):
                t.wait_ge(ld_w, 96)
                k = 0
                for ci in range(NC_D):
                    for tg in range(NT // 4):
                        pb = (pt0, pt1)[k % 2]
                        if k >= 2:
                            t.wait_ge(s3e, k - 1)
                        inst = None
                        for j in range(4):
                            ti = 4 * tg + j
                            inst = t.transpose(pb[:, 128 * j:128 * j + 128],
                                               xn_tm[:, ti, 128 * ci:128 * ci + 128],
                                               ident_sb[:, :])
                        inst.then_inc(s3t, 1)
                        k += 1

            @block.scalar
            def _(s):
                k = 0
                for ci in range(NC_D):
                    for tg in range(NT // 4):
                        pb = (pt0, pt1)[k % 2]
                        s.wait_ge(s3t, k + 1)
                        s.copy(xn_fm[:, ci, 512 * tg:512 * tg + 512],
                               pb[:, 0:512]).then_inc(s3e, 1)
                        k += 1

        with nc.Block() as block:
            @block.tensor
            def _(t):
                for ti in range(NT):
                    pb = (pl0, pl1)[ti % 2]
                    if ti >= 2:
                        t.wait_ge(s4v, ti - 1)
                    inst = None
                    for ci in range(NC_D):
                        inst = t.matmul(pb[:, 0:7], xn_fm[:, ci, 128 * ti:128 * ti + 128],
                                        wgate_sb[:, ci, :], start=(ci == 0),
                                        stop=(ci == NC_D - 1))
                    inst.then_inc(s4t, 1)

            @block.vector
            def _(v):
                ssync = SelfSync(v, s4e, dummy=sdum_sb[:, 1:2])
                inst = None
                for ti in range(NT):
                    pb = (pl0, pl1)[ti % 2]
                    v.wait_ge(s4t, ti + 1)
                    inst = v.tensor_copy(lg_sb[:, ti, :], pb[:, 0:7])
                    inst.then_inc(s4v, 1)
                ssync.barrier()
                c = lambda j: lg_sb[:, :, j]
                v.tensor_tensor(fw_sb[:, :, 0], c(0), c(1), ALU.max)
                inst = v.tensor_tensor(fw_sb[:, :, 1], c(0), c(1), ALU.min)
                a, b = 0, 1
                nxt = 2
                for j in range(2, 7):
                    tm = nxt; na = nxt + 1; nb = nxt + 2
                    nxt += 3 if j < 6 else 0
                    if nxt > 9:
                        nxt = 2
                    ssync.flush(inst)
                    v.tensor_tensor(fw_sb[:, :, tm], fw_sb[:, :, a], c(j), ALU.min)
                    inst = v.tensor_tensor(fw_sb[:, :, na], fw_sb[:, :, a], c(j),
                                           ALU.max)
                    ssync.flush(inst)
                    inst = v.tensor_tensor(fw_sb[:, :, nb], fw_sb[:, :, b],
                                           fw_sb[:, :, tm], ALU.max)
                    a, b = na, nb
                ssync.flush(inst)
                v.tensor_copy(fw_sb[:, :, 10], fw_sb[:, :, a])
                v.tensor_copy(fw_sb[:, :, 11],
                              fw_sb[:, :, b]).then_inc(s4v, 1)
                v.wait_ge(sSq, NT + 1)
                inst = v.tensor_add(fd_sb[:, :], e2_sb[:, :, 0], e2_sb[:, :, 1])
                ssync.flush(inst)
                inst = v.reciprocal(fr_sb[:, :], fd_sb[:, :])
                ssync.flush(inst)
                v.tensor_mul(fac_sb[:, :], fd_sb[:, :], fr_sb[:, :])

            @block.scalar
            def _(s):
                s.wait_ge(s4v, NT + 1)
                s.activation(e2_sb[:, :, :], fw_sb[:, :, 10:12],
                             AF.Exp).then_inc(sSq, 1)

        with nc.Block() as block:
            @block.tensor
            def _(t):
                k = 0
                for ti in range(NT):
                    for dh in range(2):
                        psb = (ps0, ps1)[k % 2]
                        peb = (pe0, pe1)[k % 2]
                        if k >= 2:
                            t.wait_ge(s5e, k - 1)
                        for ci in range(NC_D):
                            t.matmul(psb[:, :], xn_fm[:, ci, 128 * ti:128 * ti + 128],
                                     wst_sb[:, ci, 512 * dh:512 * dh + 512],
                                     start=(ci == 0), stop=False)
                        t.matmul(psb[:, :], ones1_sb[:, :], bias_sb[:, 512 * dh:512 * dh + 512],
                                 start=False, stop=True)
                        inst = None
                        for ci in range(NC_D):
                            inst = t.matmul(peb[:, :], xn_fm[:, ci, 128 * ti:128 * ti + 128],
                                            wet_sb[:, ci, 512 * dh:512 * dh + 512],
                                            start=(ci == 0), stop=(ci == NC_D - 1))
                        inst.then_inc(s5t, 1)
                        k += 1

            @block.scalar
            def _(s):
                k = 0
                for ti in range(NT):
                    for dh in range(2):
                        psb = (ps0, ps1)[k % 2]
                        ssb = (s_sb0, s_sb1)[k % 2]
                        tsb = (t_sb0, t_sb1)[k % 2]
                        s.wait_ge(s5t, k + 1)
                        s.copy(ssb, psb[:, :]).then_inc(s5s, 1)
                        s.wait_ge(s5v, k + 1)
                        s.activation(g_sb[:, ti, 512 * dh:512 * dh + 512], tsb,
                                     AF.Silu).then_inc(s5e, 1)
                        k += 1

            @block.vector
            def _(v):
                k = 0
                for ti in range(NT):
                    for dh in range(2):
                        peb = (pe0, pe1)[k % 2]
                        ssb = (s_sb0, s_sb1)[k % 2]
                        tsb = (t_sb0, t_sb1)[k % 2]
                        v.wait_ge(s5s, k + 1)
                        v.scalar_tensor_tensor(tsb, peb[:, :], fac_sb[:, ti:ti + 1],
                                               ssb, ALU.mult, ALU.add).then_inc(s5v, 1)
                        k += 1

        with nc.Block() as block:
            @block.sync
            def _(sync):
                sync.dma_start(out=xnfm_out.ap().rearrange("(c p) t -> p c t", p=128),
                               in_=xn_fm[:, :, :]).then_inc(st, 16)
                sync.dma_start(out=g_out.ap().rearrange("(T p) d -> p T d", p=128),
                               in_=(xn_tm if dbg_xn_tm else g_sb)[:, :, :]).then_inc(st, 16)
                sync.wait_ge(st, 32)
    return nc


def _prep_A_inputs(inputs):
    x = np.asarray(inputs["L0_attn_x"], np.float32).reshape(TTOK, D)
    wst = np.ascontiguousarray(np.asarray(inputs["L0_ffn_W_shared"], np.float32).T)
    wet = np.ascontiguousarray(np.asarray(inputs["L0_ffn_W_expert"], np.float32).T)
    wgate = np.ascontiguousarray(np.asarray(inputs["L0_ffn_W_gate"], np.float32))
    bias = np.ascontiguousarray(np.asarray(inputs["L0_ffn_bias"], np.float32)[None, :])
    ones1 = np.ones((1, 128), np.float32)
    ident = np.eye(128, dtype=np.float32)
    maps = []
    for c in range(NCORES):
        maps.append(dict(x=np.ascontiguousarray(x[TCORE * c:TCORE * (c + 1)]),
                         wgate=wgate, wst=wst, wet=wet, bias=bias,
                         ones1=ones1, ident=ident))
    return maps





SEQB = 8192
NTB = SEQB // 128
NGR = NTB // 4


def _plan_B():
    p = dict(st={}, dg={}, pv={}, t={}, su={}, pr={}, qke={}, ve={},
             et_t={}, steps={})
    nSt = nDg = nPv = nT = nSu = nPr = nQKe = nVe = nEtT = 0
    for pair in range(4):
        for part in range(2):
            for gq in range(16):
                nQKe += 1
                p["qke"][(pair, part, gq)] = nQKe
        for vt in range(NTB):
            nVe += 1
            p["ve"][(pair, vt)] = nVe
        for g in range(NGR - 1, -1, -1):
            nT += 1
            p["t"][(pair, g)] = nT
            nSu += 1
            p["su"][(pair, g)] = nSu
            for j in range(0, 4 * g + 4):
                nSt += 1
                p["st"][(pair, g, j)] = nSt
                diag = j >= 4 * g
                tcons = any(4 * g + r > j for r in range(4))
                if tcons:
                    nEtT += 1
                    p["et_t"][(pair, g, j)] = nEtT
                if diag:
                    r = j - 4 * g
                    nDg += 1
                    p["dg"][(pair, g, r)] = nDg
                    nPv += 1
                    p["pv"][(pair, g, r)] = nPv
                p["steps"][nSt] = (pair, g, j, diag, tcons)
            for r in range(4):
                nPr += 1
                p["pr"][(pair, g, r)] = nPr
    p["nSt"] = nSt
    return p


def build_launch_B():
    nc = bass.Bass("TRN2", target_bir_lowering=False, debug=False,
                   detect_race_conditions=False)
    xh_in = nc.dram_tensor("xh", [128, TTOK], BF16, kind="ExternalInput")
    wqkv_in = nc.dram_tensor("wqkv", [128, 192], F32, kind="ExternalInput")
    bqkv_in = nc.dram_tensor("bqkvc", [64, 3], F32, kind="ExternalInput")
    bv_in = nc.dram_tensor("bvrow", [1, 64], F32, kind="ExternalInput")
    ones1_in = nc.dram_tensor("ones1", [1, 128], F32, kind="ExternalInput")
    onec_in = nc.dram_tensor("onec", [128, 1], F32, kind="ExternalInput")
    msl_in = nc.dram_tensor("msl", [128, 128], mybir.dt.uint8, kind="ExternalInput")
    onet_in = nc.dram_tensor("onet", [128, 128], F32, kind="ExternalInput")
    s_out = nc.dram_tensor("s", [TTOK, 128], BF16, kind="ExternalOutput")


    emit_B(nc, "B_", xh_in, wqkv_in, bqkv_in, bv_in, ones1_in, onec_in, msl_in, onet_in, s_out)
    return nc


def emit_B(nc, PFX, xh_in, wqkv_in, bqkv_in, bv_in, ones1_in, onec_in, msl_in, onet_in, s_out,
           xng=None, wqkvL_in=None):
    Pl = _plan_B()
    gathered = xng is not None
    with Phase(nc, PFX) as ph:
        if gathered:
            xb_sb0 = ph.sbuf("xb_sb0", [128, NC_D, 512], BF16)
            xb_sb1 = ph.sbuf("xb_sb1", [128, NC_D, 512], BF16)
            wL_sb = ph.sbuf("wL_sb", [128, NC_D, 384], BF16)
            ld_b = ph.sem("ld_b")
        else:
            xh_sb = ph.sbuf("xh_sb", [128, 2, SEQB], BF16)
        qk_sb = ph.sbuf("qk_sb", [64, 2 * SEQB], BF16)
        v1_sb = ph.sbuf("v1_sb", [128, NTB, 65], BF16)
        et0_sb = ph.sbuf("et0_sb", [128, 512], BF16)
        et1_sb = ph.sbuf("et1_sb", [128, 512], BF16)
        etd_sb = ph.sbuf("etd_sb", [128, 128], BF16)
        s_sb = ph.sbuf("s_sb", [128, 128, 128], BF16)
        wqkv_sb = ph.sbuf("wqkv_sb", [128, 192], BF16)
        bqkv_sb = ph.sbuf("bqkv_sb", [64, 3], F32)
        bv_sb = ph.sbuf("bv_sb", [1, 64], BF16)
        ones1_sb = ph.sbuf("ones1_sb", [1, 128], BF16)
        ones1f_sb = ph.sbuf("ones1f_sb", [1, 128], F32)
        onec_sb = ph.sbuf("onec_sb", [128, 1], BF16)
        msl_sb = ph.sbuf("msl_sb", [128, 128], mybir.dt.uint8)
        onet_sb = ph.sbuf("onet_sb", [128, 128], BF16)
        suf_sb = ph.sbuf("suf_sb", [1, 8, 65], F32)
        trow_sb = ph.sbuf("trow_sb", [1, 260], F32)
        rz_sb = ph.sbuf("rz_sb", [128, 8], F32)
        sdum_sb = ph.sbuf("sdumB_sb", [128, 2], F32)

        pst0 = ph.psum("pst0", [128, 512], F32)
        pst1 = ph.psum("pst1", [128, 512], F32)
        pp = [ph.psum("pp%d" % i, [128, 512], F32) for i in range(4)]
        pT = ph.psum("pT", [128, 512], F32)

        ld = ph.sem("ld")
        ldx = ph.sem("ldx")
        sQKe = ph.sem("sQKe")
        sVe = ph.sem("sVe")
        sPj = ph.sem("sPj")
        sSt = ph.sem("sSt")
        sEx = ph.sem("sEx")
        sEt = ph.sem("sEt")
        sDg = ph.sem("sDg")
        sPv = ph.sem("sPv")
        sSu = ph.sem("sSu")
        sT = ph.sem("sT")
        sPr = ph.sem("sPr")
        sVx = ph.sem("sVxB")
        st = ph.sem("st")

        with nc.Block() as block:
            @block.sync
            def _(sync):
                if not gathered:
                    sync.dma_start(out=xh_sb[:, 0, :], in_=xh_in[:, 0:SEQB]).then_inc(ldx, 16)
                    sync.dma_start(out=xh_sb[:, 1, :], in_=xh_in[:, SEQB:TTOK]).then_inc(ldx, 16)

            @block.gpsimd
            def _(g):
                if gathered:
                    g.dma_start(out=wL_sb[:, :, :],
                                in_=wqkvL_in.ap().rearrange("(c p) d -> p c d", p=128)).then_inc(ld, 16)
                else:
                    g.dma_start(out=wqkv_sb[:, :], in_=wqkv_in[:, :]).then_inc(ld, 16)
                g.dma_start(out=bqkv_sb[:, :], in_=bqkv_in[:, :]).then_inc(ld, 16)
                g.dma_start(out=bv_sb[:, :], in_=bv_in[:, :]).then_inc(ld, 16)
                g.dma_start(out=ones1_sb[:, :], in_=ones1_in[:, :]).then_inc(ld, 16)
                g.dma_start(out=ones1f_sb[:, :], in_=ones1_in[:, :]).then_inc(ld, 16)
                g.dma_start(out=onec_sb[:, :], in_=onec_in[:, :]).then_inc(ld, 16)
                g.dma_start(out=msl_sb[:, :], in_=msl_in[:, :]).then_inc(ld, 16)
                g.dma_start(out=onet_sb[:, :], in_=onet_in[:, :]).then_inc(ld, 16)

        vsync_state = [0]
        for hl in range(2):
            for b in range(2):
                pair = 2 * hl + b
                hrow = 64 * hl

                with nc.Block() as block:
                    if gathered:
                        @block.sync
                        def _(sync, hl=hl, b=b, pair=pair):
                            for blk in range(16):
                                n = 16 * pair + blk
                                r = 4 * b + blk // 4
                                co = 512 * (blk % 4)
                                xb = (xb_sb0, xb_sb1)[n % 2]
                                if n >= 2:
                                    pm2 = (n - 2) // 16
                                    bm2 = (n - 2) % 16
                                    sync.wait_ge(sPj, 96 * pm2 + 6 * (bm2 + 1))
                                sync.dma_start(
                                    out=xb[:, :, :],
                                    in_=xng[1024 * r:1024 * (r + 1),
                                            co:co + 512].rearrange(
                                                "(c p) t -> p c t", p=128)
                                ).then_inc(ld_b, 16)

                        @block.tensor
                        def _(t, hl=hl, b=b, pair=pair):
                            if pair == 0:
                                t.wait_ge(ld, 128)
                            k = 0
                            for blk in range(16):
                                n = 16 * pair + blk
                                xb = (xb_sb0, xb_sb1)[n % 2]
                                t.wait_ge(ld_b, 16 * (n + 1))
                                for wc in (0, 128):
                                    pb = (pst0, pst1)[k % 2]
                                    if k >= 2 or pair > 0:
                                        t.wait_ge(sQKe, 32 * pair
                                                  + (k - 1 if k >= 2 else 0))
                                    inst = None
                                    for ci in range(NC_D):
                                        inst = t.matmul(
                                            pb[0:64, :],
                                            wL_sb[:, ci,
                                                  wc + 64 * hl:wc + 64 * hl + 64],
                                            xb[:, ci, :],
                                            start=(ci == 0), stop=(ci == NC_D - 1))
                                    inst.then_inc(sPj, 1)
                                    k += 1
                                for tt in range(4):
                                    vt = 4 * blk + tt
                                    pb = (pp[0], pp[1])[vt % 2]
                                    if vt >= 2:
                                        t.wait_ge(sVe, Pl["ve"][(pair, vt - 2)])
                                    for ci in range(NC_D):
                                        t.matmul(pb[:, 0:64],
                                                 xb[:, ci, 128 * tt:128 * tt + 128],
                                                 wL_sb[:, ci, 256 + 64 * hl:
                                                       320 + 64 * hl],
                                                 start=(ci == 0), stop=False)
                                    inst = t.matmul(pb[:, 0:64], ones1_sb[:, :],
                                                    bv_sb[:, :], start=False,
                                                    stop=True)
                                    inst.then_inc(sPj, 1)

                        @block.scalar
                        def _(s, pair=pair):
                            base = pair * 96
                            for blk in range(16):
                                for half, bcol in ((0, 0), (1, 1)):
                                    k = 2 * blk + half
                                    pb = (pst0, pst1)[k % 2]
                                    s.wait_ge(sPj, base + 6 * blk + half + 1)
                                    s.activation(
                                        qk_sb[0:64, SEQB * half + 512 * blk:
                                              SEQB * half + 512 * blk + 512],
                                        pb[0:64, :], AF.Identity,
                                        bias=bqkv_sb[:, bcol:bcol + 1]
                                    ).then_inc(sQKe, 1)

                        @block.vector
                        def _(v, pair=pair):
                            base = pair * 96
                            v.memset(v1_sb[:, :, 64:65], 1.0)
                            for blk in range(16):
                                for tt in range(4):
                                    vt = 4 * blk + tt
                                    pb = (pp[0], pp[1])[vt % 2]
                                    v.wait_ge(sPj, base + 6 * blk + 2 + tt + 1)
                                    v.tensor_copy(v1_sb[:, vt, 0:64],
                                                  pb[:, 0:64]).then_inc(sVe, 1)
                    else:
                        @block.tensor
                        def _(t, hl=hl, b=b, pair=pair, hrow=hrow):
                            if pair == 0:
                                t.wait_ge(ld, 128)
                                t.wait_ge(ldx, 32)
                            k = 0
                            for part, wc, _bc in ((0, 0, 0), (64, 64, 1)):
                                for gq in range(16):
                                    pb = (pst0, pst1)[k % 2]
                                    if k >= 2:
                                        t.wait_ge(sQKe, Pl["qke"][(pair, (k - 2) // 16,
                                                                  (k - 2) % 16)])
                                    inst = t.matmul(
                                        pb[0:64, :],
                                        wqkv_sb[hrow:hrow + 64, wc:wc + 64],
                                        xh_sb[hrow:hrow + 64, b,
                                              512 * gq:512 * gq + 512],
                                        start=True, stop=True)
                                    inst.then_inc(sPj, 1)
                                    k += 1
                            for vt in range(NTB):
                                pb = (pp[0], pp[1])[vt % 2]
                                if vt >= 2:
                                    t.wait_ge(sVe, Pl["ve"][(pair, vt - 2)])
                                t.matmul(pb[:, 0:64],
                                         xh_sb[hrow:hrow + 64, b,
                                               128 * vt:128 * vt + 128],
                                         wqkv_sb[hrow:hrow + 64, 128:192],
                                         start=True, stop=False)
                                inst = t.matmul(pb[:, 0:64], ones1_sb[:, :],
                                                bv_sb[:, :],
                                                start=False, stop=True)
                                inst.then_inc(sPj, 1)

                        @block.scalar
                        def _(s, pair=pair):
                            base = pair * 96
                            for half, bcol in ((0, 0), (1, 1)):
                                for gq in range(16):
                                    k = half * 16 + gq
                                    pb = (pst0, pst1)[k % 2]
                                    s.wait_ge(sPj, base + k + 1)
                                    s.activation(qk_sb[0:64, SEQB * half + 512 * gq:
                                                       SEQB * half + 512 * gq + 512],
                                                 pb[0:64, :], AF.Identity,
                                                 bias=bqkv_sb[:, bcol:bcol + 1]
                                                 ).then_inc(sQKe, 1)

                        @block.vector
                        def _(v, pair=pair):
                            base = pair * 96
                            v.memset(v1_sb[:, :, 64:65], 1.0)
                            for vt in range(NTB):
                                pb = (pp[0], pp[1])[vt % 2]
                                v.wait_ge(sPj, base + 32 + vt + 1)
                                v.tensor_copy(v1_sb[:, vt, 0:64],
                                              pb[:, 0:64]).then_inc(sVe, 1)

                with nc.Block() as block:
                    @block.tensor
                    def _(t, hl=hl, b=b, pair=pair):
                        for g in range(NGR - 1, -1, -1):
                            inst = None
                            for r, j in enumerate(range(4 * g + 3, 4 * g - 1, -1)):
                                inst = t.matmul(pT[0:1, 65 * r:65 * r + 65],
                                                onec_sb[:, :], v1_sb[:, j, :],
                                                start=True, stop=True)
                            inst.then_inc(sT, 1)
                            if g < NGR - 1:
                                t.wait_ge(sPr, Pl["pr"][(pair, g + 1, 3)])
                            t.wait_ge(sSu, Pl["su"][(pair, g)])
                            for r in range(4):
                                srow = suf_sb[0:1, (4, 3, 2) [r] if r < 3 else 5, :]
                                t.matmul(pp[r][:, 0:65], ones1f_sb[:, :], srow,
                                         start=True, stop=False)
                            for j in range(0, 4 * g + 4):
                                pb = (pst0, pst1)[j % 2]
                                me = Pl["st"][(pair, g, j)]
                                if me >= 3:
                                    t.wait_ge(sEx, me - 2)
                                inst = t.matmul(
                                    pb[:, :],
                                    qk_sb[0:64, SEQB + 128 * j:SEQB + 128 * j + 128],
                                    qk_sb[0:64, 512 * g:512 * g + 512],
                                    start=True, stop=True)
                                inst.then_inc(sSt, 1)
                                t.wait_ge(sEx, me)
                                et = (et0_sb, et1_sb)[j % 2]
                                last = None
                                for r in range(4):
                                    if 4 * g + r > j:
                                        last = t.matmul(pp[r][:, 0:65],
                                                        et[:, 128 * r:128 * r + 128],
                                                        v1_sb[:, j, :],
                                                        start=False, stop=False)
                                diag = j >= 4 * g
                                if (pair, g, j) in Pl["et_t"] and not diag:
                                    last.then_inc(sEt, 1)
                                if diag:
                                    r = j - 4 * g
                                    if last is not None:
                                        last.then_inc(sEt, 1)
                                    t.wait_ge(sDg, Pl["dg"][(pair, g, r)])
                                    t.matmul(pp[r][:, 0:65], etd_sb[:, :],
                                             v1_sb[:, j, :], start=False,
                                             stop=True).then_inc(sPv, 1)

                    @block.scalar
                    def _(s, pair=pair):
                        for g in range(NGR - 1, -1, -1):
                            for j in range(0, 4 * g + 4):
                                me = Pl["st"][(pair, g, j)]
                                s.wait_ge(sSt, me)
                                if me >= 3:
                                    opair, og, oj, odiag, otcons = Pl["steps"][me - 2]
                                    if otcons:
                                        s.wait_ge(sEt, Pl["et_t"][(opair, og, oj)])
                                    if odiag:
                                        s.wait_ge(sDg, Pl["dg"][(opair, og,
                                                                oj - 4 * og)])
                                pb = (pst0, pst1)[j % 2]
                                s.activation((et0_sb, et1_sb)[j % 2][:, :], pb[:, :],
                                             AF.Exp, scale=0.125).then_inc(sEx, 1)

                    @block.vector
                    def _(v, hl=hl, b=b, pair=pair):
                        ssync = SelfSync(v, sVx, dummy=sdum_sb[:, 0:1])
                        ssync.n = vsync_state[0]
                        run = 0
                        if pair == 0:
                            v.memset(suf_sb[0:1, 0, :], 0.0)
                        else:
                            v.memset(suf_sb[0:1, run, :], 0.0)
                        for g in range(NGR - 1, -1, -1):
                            v.wait_ge(sT, Pl["t"][(pair, g)])
                            inst = v.tensor_copy(trow_sb[0:1, :], pT[0:1, 0:260])
                            ssync.flush(inst)
                            inst = v.tensor_add(suf_sb[0:1, 2, :],
                                                suf_sb[0:1, run, :],
                                                trow_sb[0:1, 0:65])
                            ssync.flush(inst)
                            inst = v.tensor_add(suf_sb[0:1, 3, :], suf_sb[0:1, 2, :],
                                                trow_sb[0:1, 65:130])
                            ssync.flush(inst)
                            inst = v.tensor_add(suf_sb[0:1, 4, :], suf_sb[0:1, 3, :],
                                                trow_sb[0:1, 130:195])
                            ssync.flush(inst)
                            v.tensor_copy(suf_sb[0:1, 5, :], suf_sb[0:1, run, :])
                            inst = v.tensor_add(suf_sb[0:1, 1 - run, :],
                                                suf_sb[0:1, 4, :],
                                                trow_sb[0:1, 195:260])
                            inst.then_inc(sSu, 1)
                            run = 1 - run
                            for j in range(4 * g, 4 * g + 4):
                                r = j - 4 * g
                                et = (et0_sb, et1_sb)[j % 2]
                                v.wait_ge(sEx, Pl["st"][(pair, g, j)])
                                v.select(etd_sb[:, :], msl_sb[:, :],
                                         et[:, 128 * r:128 * r + 128],
                                         onet_sb[:, :]).then_inc(sDg, 1)
                                v.wait_ge(sPv, Pl["pv"][(pair, g, r)])
                                inst = v.reciprocal(rz_sb[:, r:r + 1],
                                                    pp[r][:, 64:65])
                                ssync.flush(inst)
                                gti = NTB * b + 4 * g + r
                                v.tensor_scalar(s_sb[:, gti, 64 * hl:64 * hl + 64],
                                                pp[r][:, 0:64], rz_sb[:, r:r + 1],
                                                None, ALU.mult).then_inc(sPr, 1)
                        vsync_state[0] = ssync.n

        with nc.Block() as block:
            @block.sync
            def _(sync):
                sync.dma_start(out=s_out.ap().rearrange("(T p) c -> p T c", p=128),
                               in_=s_sb[:, :, :]).then_inc(st, 16)
                sync.wait_ge(st, 16)
    return nc


def _prep_B_inputs(xnfm_by_core, inputs):
    xn_full = np.concatenate(xnfm_by_core, axis=1)
    w1 = np.asarray(inputs["L0_attn_Wqkv"], np.float32)
    wqkv = np.ascontiguousarray(np.concatenate([w1, w1], axis=0))
    bq = np.asarray(inputs["L0_attn_bqkv"], np.float32)
    bqkvc = np.stack([bq[0:64], bq[64:128], bq[128:192]], axis=1)
    bvrow = np.ascontiguousarray(bq[None, 128:192])
    ones1 = np.ones((1, 128), np.float32)
    onec = np.ones((128, 1), np.float32)
    jj, ii = np.meshgrid(np.arange(128), np.arange(128), indexing="ij")
    msl = (jj < ii).astype(np.uint8)
    onet = np.ones((128, 128), np.float32)
    maps = []
    for c in range(NCORES):
        maps.append(dict(xh=np.ascontiguousarray(xn_full[128 * c:128 * (c + 1)]),
                         wqkv=wqkv, bqkvc=np.ascontiguousarray(bqkvc),
                         bvrow=bvrow, ones1=ones1, onec=onec, msl=msl, onet=onet))
    return maps


def build_launch_C1():
    nc = bass.Bass("TRN2", target_bir_lowering=False, debug=False,
                   detect_race_conditions=False)
    x_in = nc.dram_tensor("x", [TCORE, D], F32, kind="ExternalInput")
    sfm_in = nc.dram_tensor("sfm", [D, TCORE], BF16, kind="ExternalInput")
    g_in = nc.dram_tensor("g", [TCORE, D], BF16, kind="ExternalInput")
    wr_in = nc.dram_tensor("wr", [D, D], F32, kind="ExternalInput")
    br_in = nc.dram_tensor("br", [1, D], F32, kind="ExternalInput")
    ones1_in = nc.dram_tensor("ones1", [1, 128], F32, kind="ExternalInput")
    ident_in = nc.dram_tensor("ident", [128, 128], F32, kind="ExternalInput")
    x1_out = nc.dram_tensor("x1", [TCORE, D], F32, kind="ExternalOutput")
    xn1fm_out = nc.dram_tensor("xn1_fm", [D, TCORE], BF16, kind="ExternalOutput")

    emit_C1(nc, "C1_", x_in, sfm_in, g_in, wr_in, br_in, ones1_in, ident_in, x1_out, xn1fm_out)
    return nc


def emit_C1(nc, PFX, x_in, sfm_in, g_in, wr_in, br_in, ones1_in, ident_in, x1_out, xn1fm_out, s_t_in=None):
    with Phase(nc, PFX) as ph:
        x1_sb = ph.sbuf("x1_sb", [128, NT, D], F32)
        sfm_sb = ph.sbuf("sfm_sb", [128, NC_D, TCORE], BF16)
        g_sb = ph.sbuf("g_sb", [128, NT, D], BF16)
        xn_tm = ph.sbuf("xn_tm", [128, NT, D], BF16)
        wr_sb = ph.sbuf("wr_sb", [128, NC_D, D], BF16)
        br_sb = ph.sbuf("br_sb", [1, D], BF16)
        ones1_sb = ph.sbuf("ones1_sb", [1, 128], BF16)
        ident_sb = ph.sbuf("ident_sb", [128, 128], BF16)
        eps_sb = ph.sbuf("eps_sb", [128, 1], F32)
        sdum_sb = ph.sbuf("sdum_sb", [128, 2], F32)
        sq0_sb = ph.sbuf("sq0_sb", [128, D], F32)
        sq1_sb = ph.sbuf("sq1_sb", [128, D], F32)
        sums_sb = ph.sbuf("sums_sb", [128, NT], F32)
        sumsq_sb = ph.sbuf("sumsq_sb", [128, NT], F32)
        mean_sb = ph.sbuf("mean_sb", [128, NT], F32)
        var_sb = ph.sbuf("var_sb", [128, NT], F32)
        w0_sb = ph.sbuf("w0_sb", [128, NT], F32)
        w1_sb = ph.sbuf("w1_sb", [128, NT], F32)
        std_sb = ph.sbuf("std_sb", [128, NT], F32)
        rstd_sb = ph.sbuf("rstd_sb", [128, NT], F32)
        xt_sb0 = ph.sbuf("xt_sb0", [128, 512], F32)
        xt_sb1 = ph.sbuf("xt_sb1", [128, 512], F32)
        rg_sb0 = ph.sbuf("rg_sb0", [128, 512], F32)
        rg_sb1 = ph.sbuf("rg_sb1", [128, 512], F32)
        fmst_sb0 = ph.sbuf("fmst_sb0", [128, 512], BF16)
        fmst_sb1 = ph.sbuf("fmst_sb1", [128, 512], BF16)

        pr0 = ph.psum("pr0", [128, 512], F32)
        pr1 = ph.psum("pr1", [128, 512], F32)
        pt0 = ph.psum("pt0", [128, 1024], BF16)
        pt1 = ph.psum("pt1", [128, 1024], BF16)

        ld_x = ph.sem("ld_x")
        ld_w = ph.sem("ld_w")
        ld_t = ph.sem("ld_t")
        sR = ph.sem("sR")
        sC = ph.sem("sC")
        sSq = ph.sem("sSq")
        sVc = ph.sem("sVc")
        sVx = ph.sem("sVx")
        s2a = ph.sem("s2a")
        s2b = ph.sem("s2b")
        s3t = ph.sem("s3t")
        s3e = ph.sem("s3e")
        sFm = ph.sem("sFm")
        st = ph.sem("st")

        with nc.Block() as block:
            @block.sync
            def _(sync):
                if s_t_in is not None:
                    for i in range(8):
                        sync.dma_start(out=sfm_sb[:, i, :],
                                       in_=s_t_in[TCORE * i:TCORE * (i + 1), :],
                                       transpose=True).then_inc(ld_x, 16)
                else:
                    sync.dma_start(out=sfm_sb[:, :, :],
                                   in_=sfm_in.ap().rearrange("(c p) t -> p c t", p=128)).then_inc(ld_x, 16)
                sync.dma_start(out=g_sb[:, :, :],
                               in_=g_in.ap().rearrange("(T p) d -> p T d", p=128)).then_inc(ld_x, 16)

            @block.gpsimd
            def _(g):
                g.dma_start(out=wr_sb[:, :, :],
                            in_=wr_in.ap().rearrange("(c p) d -> p c d", p=128)).then_inc(ld_w, 16)
                g.dma_start(out=br_sb[:, :], in_=br_in[:, :]).then_inc(ld_w, 16)
                g.dma_start(out=ones1_sb[:, :], in_=ones1_in[:, :]).then_inc(ld_w, 16)
                g.dma_start(out=ident_sb[:, :], in_=ident_in[:, :]).then_inc(ld_w, 16)

        xr = x_in.ap().rearrange("(T p) d -> p T d", p=128)
        with nc.Block() as block:
            @block.sync
            def _(sync):
                k = 0
                for ti in range(NT):
                    for dh in range(2):
                        xtb = (xt_sb0, xt_sb1)[k % 2]
                        if k >= 2:
                            sync.wait_ge(sC, k - 1)
                        sync.dma_start(out=xtb[:, :],
                                       in_=xr[:, ti, 512 * dh:512 * dh + 512]).then_inc(ld_t, 16)
                        k += 1

            @block.tensor
            def _(t):
                t.wait_ge(ld_x, 144 if s_t_in is not None else 32)
                t.wait_ge(ld_w, 64)
                k = 0
                for ti in range(NT):
                    for dh in range(2):
                        pb = (pr0, pr1)[k % 2]
                        if k >= 2:
                            t.wait_ge(sC, k - 1)
                        for ci in range(NC_D):
                            t.matmul(pb[:, :], sfm_sb[:, ci, 128 * ti:128 * ti + 128],
                                     wr_sb[:, ci, 512 * dh:512 * dh + 512],
                                     start=(ci == 0), stop=False)
                        inst = t.matmul(pb[:, :], ones1_sb[:, :],
                                        br_sb[:, 512 * dh:512 * dh + 512],
                                        start=False, stop=True)
                        inst.then_inc(sR, 1)
                        k += 1

            @block.vector
            def _(v):
                k = 0
                for ti in range(NT):
                    for dh in range(2):
                        pb = (pr0, pr1)[k % 2]
                        rgb = (rg_sb0, rg_sb1)[k % 2]
                        xtb = (xt_sb0, xt_sb1)[k % 2]
                        v.wait_ge(sR, k + 1)
                        v.wait_ge(ld_t, 16 * (k + 1))
                        v.tensor_mul(rgb[:, :], pb[:, :],
                                     g_sb[:, ti, 512 * dh:512 * dh + 512])
                        v.tensor_add(x1_sb[:, ti, 512 * dh:512 * dh + 512],
                                     xtb[:, :], rgb[:, :]).then_inc(sC, 1)
                        k += 1

        with nc.Block() as block:
            @block.vector
            def _(v):
                ssync = SelfSync(v, sVx, dummy=sdum_sb[:, 0:1])
                v.memset(eps_sb[:, :], 1e-5)
                inst = _ln_stats(v, None, x1_sb, sq0_sb, sq1_sb, sums_sb, sumsq_sb,
                                 mean_sb, var_sb, w0_sb, w1_sb, eps_sb, sSq, sVc, ssync)
                inst.then_inc(s2a, 1)
                v.wait_ge(s2b, 1)
                inst = v.reciprocal(rstd_sb[:, :], std_sb[:, :])
                ssync.flush(inst)
                for ti in range(NT):
                    v.tensor_scalar(xn_tm[:, ti, :], x1_sb[:, ti, :],
                                    mean_sb[:, ti:ti + 1], rstd_sb[:, ti:ti + 1],
                                    ALU.subtract, ALU.mult)

            @block.scalar
            def _(s):
                _ln_stats_scalar(s, x1_sb, sq0_sb, sq1_sb, sSq, sVc)
                s.wait_ge(s2a, 1)
                s.activation(std_sb[:, :], var_sb[:, :], AF.Sqrt,
                             bias=eps_sb[:, 0:1]).then_inc(s2b, 1)

        with nc.Block() as block:
            @block.tensor
            def _(t):
                k = 0
                for ci in range(NC_D):
                    for tg in range(NT // 4):
                        pb = (pt0, pt1)[k % 2]
                        if k >= 2:
                            t.wait_ge(s3e, k - 1)
                        inst = None
                        for j in range(4):
                            ti = 4 * tg + j
                            inst = t.transpose(pb[:, 128 * j:128 * j + 128],
                                               xn_tm[:, ti, 128 * ci:128 * ci + 128],
                                               ident_sb[:, :])
                        inst.then_inc(s3t, 1)
                        k += 1

            @block.scalar
            def _(s):
                k = 0
                for ci in range(NC_D):
                    for tg in range(NT // 4):
                        pb = (pt0, pt1)[k % 2]
                        fst = (fmst_sb0, fmst_sb1)[k % 2]
                        s.wait_ge(s3t, k + 1)
                        if k >= 2:
                            s.wait_ge(sFm, 16 * (k - 1))
                        s.copy(fst[:, :], pb[:, 0:512]).then_inc(s3e, 1)
                        k += 1

            @block.sync
            def _(sync):
                k = 0
                for ci in range(NC_D):
                    for tg in range(NT // 4):
                        fst = (fmst_sb0, fmst_sb1)[k % 2]
                        sync.wait_ge(s3e, k + 1)
                        sync.dma_start(out=xn1fm_out[128 * ci:128 * ci + 128,
                                                     512 * tg:512 * tg + 512],
                                       in_=fst[:, :]).then_inc(sFm, 16)
                        k += 1

        with nc.Block() as block:
            @block.sync
            def _(sync):
                sync.dma_start(out=x1_out.ap().rearrange("(T p) d -> p T d", p=128),
                               in_=x1_sb[:, :, :]).then_inc(st, 16)
                sync.wait_ge(st, 16)
    return nc


LN16 = 2.772588722239781


def build_launch_C2():
    nc = bass.Bass("TRN2", target_bir_lowering=False, debug=False,
                   detect_race_conditions=False)
    xn1_in = nc.dram_tensor("xn1f", [D, TTOK], BF16, kind="ExternalInput")
    wk_in = nc.dram_tensor("wk2h", [D, 128], F32, kind="ExternalInput")
    wv_in = nc.dram_tensor("wv2h", [D, 128], F32, kind="ExternalInput")
    wt_in = nc.dram_tensor("wT", [128, 256], F32, kind="ExternalInput")
    onec_in = nc.dram_tensor("onec", [128, 1], F32, kind="ExternalInput")
    kv_out = nc.dram_tensor("kv4", [1024, 65], F32, kind="ExternalOutput")

    NSTEP = 32

    emit_C2(nc, "C2_", xn1_in, wk_in, wv_in, wt_in, onec_in, kv_out)
    return nc


def emit_C2(nc, PFX, xn1_in, wk_in, wv_in, wt_in, onec_in, kv_out, gathered=False, dup_out=False):
    NSTEP = 32
    with Phase(nc, PFX) as ph:
        xb_sb0 = ph.sbuf("xb_sb0", [128, NC_D, 512], BF16)
        xb_sb1 = ph.sbuf("xb_sb1", [128, NC_D, 512], BF16)
        wk_sb = ph.sbuf("wk_sb", [128, NC_D, 128], BF16)
        wv_sb = ph.sbuf("wv_sb", [128, NC_D, 128], BF16)
        wt_sb = ph.sbuf("wt_sb", [128, 256], BF16)
        onec_sb = ph.sbuf("onec_sb", [128, 1], BF16)
        onecf_sb = ph.sbuf("onecf_sb", [128, 1], F32)
        kf_sb = ph.sbuf("kf_sb", [64, 512], BF16)
        sqk_sb = ph.sbuf("sqk_sb", [64, 512], F32)
        nk_sb = ph.sbuf("nk_sb", [128, 4], F32)
        mln_sb = ph.sbuf("mln_sb", [128, 1], F32)
        pe_sb = ph.sbuf("pe_sb", [128, 2, 256], F32)
        pk_sb = ph.sbuf("pk_sb", [128, 2, 256], BF16)
        v1g_sb = ph.sbuf("v1g_sb", [128, 2, 65], BF16)
        kvo_sb = ph.sbuf("kvo_sb", [128, 8, 65], F32)

        pkv0 = ph.psum("pkv0", [128, 512], F32)
        pkv1 = ph.psum("pkv1", [128, 512], F32)
        pK = ph.psum("pK", [128, 512], F32)
        pP0 = ph.psum("pP0", [128, 512], F32)
        pP1 = ph.psum("pP1", [128, 512], F32)
        pN = ph.psum("pN", [128, 512], F32)
        pV = ph.psum("pV", [128, 512], F32)

        ld_w = ph.sem("ld_w")
        ld_t = ph.sem("ld_t")
        sK = ph.sem("sK")
        sSq2 = ph.sem("sSq2")
        sNm = ph.sem("sNm")
        sNe = ph.sem("sNe")
        sPk = ph.sem("sPk")
        sCl = ph.sem("sCl")
        sEx = ph.sem("sEx")
        sVm = ph.sem("sVm")
        sVe = ph.sem("sVe")
        sKv = ph.sem("sKv")
        sKvE = ph.sem("sKvE")
        st = ph.sem("st")

        with nc.Block() as block:
            @block.gpsimd
            def _(g):
                g.dma_start(out=wk_sb[:, :, :],
                            in_=wk_in.ap().rearrange("(c p) d -> p c d", p=128)).then_inc(ld_w, 16)
                g.dma_start(out=wv_sb[:, :, :],
                            in_=wv_in.ap().rearrange("(c p) d -> p c d", p=128)).then_inc(ld_w, 16)
                g.dma_start(out=wt_sb[:, :], in_=wt_in[:, :]).then_inc(ld_w, 16)
                g.dma_start(out=onec_sb[:, :], in_=onec_in[:, :]).then_inc(ld_w, 16)
                g.dma_start(out=onecf_sb[:, :], in_=onec_in[:, :]).then_inc(ld_w, 16)

        if not gathered:
            xr = xn1_in.ap().rearrange("(c p) t -> p c t", p=128)
        with nc.Block() as block:
            @block.sync
            def _(sync):
                for n in range(2 * NSTEP):
                    hl, g = divmod(n, NSTEP)
                    xb = (xb_sb0, xb_sb1)[n % 2]
                    if n >= 2:
                        sync.wait_ge(sVm, 4 * (n - 1))
                    if gathered:
                        r, co = g // 4, 512 * (g % 4)
                        ap = xn1_in[1024 * r:1024 * (r + 1),
                                    co:co + 512].rearrange("(c p) t -> p c t", p=128)
                    else:
                        ap = xr[:, :, 512 * g:512 * g + 512]
                    sync.dma_start(out=xb[:, :, :], in_=ap).then_inc(ld_t, 16)

            @block.tensor
            def _(t):
                t.wait_ge(ld_w, 80)
                for n in range(2 * NSTEP):
                    hl, g = divmod(n, NSTEP)
                    b = g // 16
                    xb = (xb_sb0, xb_sb1)[n % 2]
                    t.wait_ge(ld_t, 16 * (n + 1))
                    if n >= 1:
                        t.wait_ge(sSq2, n)
                    inst = None
                    for ci in range(NC_D):
                        inst = t.matmul(pK[0:64, :], wk_sb[:, ci, 64 * hl:64 * hl + 64],
                                        xb[:, ci, :], start=(ci == 0), stop=(ci == NC_D - 1))
                    inst.then_inc(sK, 1)
                    t.wait_ge(sSq2, n + 1)
                    if n >= 1:
                        t.wait_ge(sNe, n)
                    inst = None
                    for tt in range(4):
                        inst = t.matmul(pN[:, tt:tt + 1],
                                        sqk_sb[:, 128 * tt:128 * tt + 128],
                                        onecf_sb[0:64, :], start=True, stop=True)
                    inst.then_inc(sNm, 1)
                    for tt in range(4):
                        pp = (pP0, pP1)[tt % 2]
                        if 4 * n + tt >= 2:
                            t.wait_ge(sCl, 4 * n + tt - 1)
                        inst = t.matmul(pp[:, 0:256],
                                        kf_sb[:, 128 * tt:128 * tt + 128],
                                        wt_sb[0:64, :], start=True, stop=True)
                        inst.then_inc(sPk, 1)
                        if 4 * n + tt >= 1:
                            t.wait_ge(sVe, 4 * n + tt)
                        inst = None
                        for ci in range(NC_D):
                            inst = t.matmul(pV[:, 0:64], xb[:, ci, 128 * tt:128 * tt + 128],
                                            wv_sb[:, ci, 64 * hl:64 * hl + 64],
                                            start=(ci == 0), stop=(ci == NC_D - 1))
                        inst.then_inc(sVm, 1)
                        t.wait_ge(sEx, 4 * n + tt + 1)
                        t.wait_ge(sVe, 4 * n + tt + 1)
                        first = (g % 16 == 0 and tt == 0)
                        lastt = (g % 16 == 15 and tt == 3)
                        if first and n > 0:
                            t.wait_ge(sKvE, 2 * (n // 16))
                        for mc in range(2):
                            inst = t.matmul((pkv0, pkv1)[mc][:, 0:65],
                                            pk_sb[:, tt % 2, 128 * mc:128 * mc + 128],
                                            v1g_sb[:, tt % 2, :],
                                            start=first, stop=lastt)
                            inst.then_inc(sKv, 1)

            @block.scalar
            def _(s):
                for n in range(2 * NSTEP):
                    s.wait_ge(sK, n + 1)
                    if n >= 1:
                        s.wait_ge(sPk, 4 * n)
                        s.wait_ge(sNm, n)
                    s.copy(kf_sb[:, :], pK[0:64, :])
                    s.activation(sqk_sb[:, :], kf_sb[:, :], AF.Square).then_inc(sSq2, 1)
                    s.wait_ge(sNm, n + 1)
                    if n >= 1:
                        s.wait_ge(sCl, 4 * n)
                    s.activation(nk_sb[:, 0:4], pN[:, 0:4], AF.Identity,
                                 bias=mln_sb[:, 0:1], scale=-0.5).then_inc(sNe, 1)
                    for tt in range(4):
                        s.wait_ge(sCl, 4 * n + tt + 1)
                        s.activation(pk_sb[:, tt % 2, :], pe_sb[:, tt % 2, :],
                                     AF.Exp).then_inc(sEx, 1)

            @block.vector
            def _(v):
                v.memset(mln_sb[:, :], -LN16)
                v.memset(v1g_sb[:, :, 64:65], 1.0)
                for n in range(2 * NSTEP):
                    hl, g = divmod(n, NSTEP)
                    for tt in range(4):
                        pp = (pP0, pP1)[tt % 2]
                        v.wait_ge(sPk, 4 * n + tt + 1)
                        v.wait_ge(sNe, n + 1)
                        if 4 * n + tt >= 2:
                            v.wait_ge(sEx, 4 * n + tt - 1)
                        v.tensor_scalar(pe_sb[:, tt % 2, :], pp[:, 0:256],
                                        nk_sb[:, tt:tt + 1], -60.0,
                                        ALU.add, ALU.max).then_inc(sCl, 1)
                        v.wait_ge(sVm, 4 * n + tt + 1)
                        if 4 * n + tt >= 2:
                            v.wait_ge(sKv, 2 * (4 * n + tt - 1))
                        v.tensor_copy(v1g_sb[:, tt % 2, 0:64],
                                      pV[:, 0:64]).then_inc(sVe, 1)
                    if g % 16 == 15:
                        bidx = 2 * hl + g // 16
                        v.wait_ge(sKv, 2 * (4 * n + 4))
                        v.tensor_copy(kvo_sb[:, 4 * hl + 2 * (g // 16), :],
                                      pkv0[:, 0:65])
                        v.tensor_copy(kvo_sb[:, 4 * hl + 2 * (g // 16) + 1, :],
                                      pkv1[:, 0:65]).then_inc(sKvE, 2)

        with nc.Block() as block:
            @block.sync
            def _(sync):
                if dup_out:
                    nst = 0
                    kvr = kv_out.ap().rearrange("(R p) c -> p R c", p=128)
                    for j in range(8):
                        b = j // 4
                        for hl in range(2):
                            sync.dma_start(
                                out=kvr[:, 4 * j + 2 * hl:4 * j + 2 * hl + 2, :],
                                in_=kvo_sb[:, 4 * hl + 2 * b:4 * hl + 2 * b + 2, :]
                            ).then_inc(st, 16)
                            nst += 16
                    sync.wait_ge(st, nst)
                else:
                    sync.dma_start(out=kv_out.ap().rearrange("(r p) c -> p r c",
                                                             p=128),
                                   in_=kvo_sb[:, :, :]).then_inc(st, 16)
                    sync.wait_ge(st, 16)
    return nc


def build_launch_C3a():
    nc = bass.Bass("TRN2", target_bir_lowering=False, debug=False,
                   detect_race_conditions=False)
    xn1_in = nc.dram_tensor("xn1fm", [D, TCORE], BF16, kind="ExternalInput")
    kvb_in = nc.dram_tensor("kvb", [4096, 65], F32, kind="ExternalInput")
    wq_in = nc.dram_tensor("wq", [D, D], F32, kind="ExternalInput")
    wt_in = nc.dram_tensor("wT", [128, 256], F32, kind="ExternalInput")
    wrt_in = nc.dram_tensor("wrt", [D, 8], F32, kind="ExternalInput")
    wgt_in = nc.dram_tensor("wgt", [D, 8], F32, kind="ExternalInput")
    onec_in = nc.dram_tensor("onec", [128, 1], F32, kind="ExternalInput")
    ones1_in = nc.dram_tensor("ones1", [1, 128], F32, kind="ExternalInput")
    s1_out = nc.dram_tensor("s1", [TCORE, D], BF16, kind="ExternalOutput")
    cw_out = nc.dram_tensor("cw", [TCORE, 8], F32, kind="ExternalOutput")

    emit_C3a(nc, "C3a_", xn1_in, kvb_in, wq_in, wt_in, wrt_in, wgt_in, onec_in, ones1_in, s1_out, cw_out)
    return nc


def emit_C3a(nc, PFX, xn1_in, kvb_in, wq_in, wt_in, wrt_in, wgt_in, onec_in, ones1_in, s1_out, cw_out, kv_a2a=False):
    with Phase(nc, PFX) as ph:
        xn1_sb = ph.sbuf("xn1_sb", [128, NC_D, TCORE], BF16)
        q_sb = ph.sbuf("q_sb", [128, NC_D, TCORE], BF16)
        s1_sb = ph.sbuf("s1_sb", [128, NT, D], BF16)
        kv_sb = ph.sbuf("kv_sb", [128, 32, 65], BF16)
        wq_sb = ph.sbuf("wq_sb", [128, NC_D, D], BF16)
        wt_sb = ph.sbuf("wt_sb", [128, 256], BF16)
        wrg_sb = ph.sbuf("wrg_sb", [128, NC_D, 16], BF16)
        onecf_sb = ph.sbuf("onecf_sb", [128, 1], F32)
        ones1f_sb = ph.sbuf("ones1f_sb", [1, 128], F32)
        sqq_sb = ph.sbuf("sqq_sb", [128, 2048], F32)
        nq_sb = ph.sbuf("nq_sb", [1, 2048], F32)
        pqe_sb = ph.sbuf("pqe_sb", [128, 2, 512], F32)
        pq_sb = ph.sbuf("pq_sb", [128, 2, 2048], BF16)
        rz_sb = ph.sbuf("rz_sb", [128, NT], F32)
        zc_sb = ph.sbuf("zc_sb", [128, NT], F32)
        lgr_sb = ph.sbuf("lgr_sb", [128, NT, 8], F32)
        lgg_sb = ph.sbuf("lgg_sb", [128, NT, 8], F32)
        er_sb = ph.sbuf("er_sb", [128, NT, 8], F32)
        eg_sb = ph.sbuf("eg_sb", [128, NT, 8], F32)
        cw_sb = ph.sbuf("cw_sb", [128, NT, 8], F32)
        rw_sb = ph.sbuf("rw_sb", [128, NT, 12], F32)
        sdum_sb = ph.sbuf("sdum_sb", [128, 2], F32)

        pq0 = ph.psum("pq0", [128, 512], F32)
        pq1 = ph.psum("pq1", [128, 512], F32)
        pn0 = ph.psum("pn0", [128, 512], F32)
        pn1 = ph.psum("pn1", [128, 512], F32)
        po0 = ph.psum("po0", [128, 512], F32)
        po1 = ph.psum("po1", [128, 512], F32)
        plg = ph.psum("plg", [128, 512], F32)

        ld_x = ph.sem("ld_x")
        ld_w = ph.sem("ld_w")
        sQm = ph.sem("sQm")
        sQe = ph.sem("sQe")
        sSqm = ph.sem("sSqm")
        sNm = ph.sem("sNm")
        sNe = ph.sem("sNe")
        sPm = ph.sem("sPm")
        sCl = ph.sem("sCl")
        sEx = ph.sem("sEx")
        sOm = ph.sem("sOm")
        sOe = ph.sem("sOe")
        sLm = ph.sem("sLm")
        sLv = ph.sem("sLv")
        sVx = ph.sem("sVx")
        sSc = ph.sem("sSc")
        st = ph.sem("st")

        with nc.Block() as block:
            @block.sync
            def _(sync):
                sync.dma_start(out=xn1_sb[:, :, :],
                               in_=xn1_in.ap().rearrange("(c p) t -> p c t", p=128)).then_inc(ld_x, 16)

            @block.gpsimd
            def _(g):
                g.dma_start(out=kv_sb[:, :, :],
                            in_=kvb_in.ap().rearrange("(r p) c -> p r c", p=128)).then_inc(ld_w, 16)
                g.dma_start(out=wq_sb[:, :, :],
                            in_=wq_in.ap().rearrange("(c p) d -> p c d", p=128)).then_inc(ld_w, 16)
                g.dma_start(out=wt_sb[:, :], in_=wt_in[:, :]).then_inc(ld_w, 16)
                g.dma_start(out=wrg_sb[:, :, 0:8],
                            in_=wrt_in.ap().rearrange("(c p) d -> p c d", p=128)).then_inc(ld_w, 16)
                g.dma_start(out=wrg_sb[:, :, 8:16],
                            in_=wgt_in.ap().rearrange("(c p) d -> p c d", p=128)).then_inc(ld_w, 16)
                g.dma_start(out=onecf_sb[:, :], in_=onec_in[:, :]).then_inc(ld_w, 16)
                g.dma_start(out=ones1f_sb[:, :], in_=ones1_in[:, :]).then_inc(ld_w, 16)

        with nc.Block() as block:
            @block.tensor
            def _(t):
                t.wait_ge(ld_x, 16)
                t.wait_ge(ld_w, 112)
                k = 0
                for dc in range(NC_D):
                    for tg in range(4):
                        pb = (pq0, pq1)[k % 2]
                        if k >= 2:
                            t.wait_ge(sQe, k - 1)
                        inst = None
                        for ci in range(NC_D):
                            inst = t.matmul(pb[:, :],
                                            wq_sb[:, ci, 128 * dc:128 * dc + 128],
                                            xn1_sb[:, ci, 512 * tg:512 * tg + 512],
                                            start=(ci == 0), stop=(ci == NC_D - 1))
                        inst.then_inc(sQm, 1)
                        k += 1

            @block.scalar
            def _(s):
                k = 0
                for dc in range(NC_D):
                    for tg in range(4):
                        pb = (pq0, pq1)[k % 2]
                        s.wait_ge(sQm, k + 1)
                        s.copy(q_sb[:, dc, 512 * tg:512 * tg + 512],
                               pb[:, :]).then_inc(sQe, 1)
                        k += 1

        for h in range(16):
            dc = h // 2
            base = 64 * (h % 2)
            with nc.Block() as block:
                @block.tensor
                def _(t, h=h, dc=dc, base=base):
                    t.wait_ge(sSqm, h + 1)
                    for tg in range(4):
                        pnb = (pn0, pn1)[tg % 2]
                        if 4 * h + tg >= 2:
                            t.wait_ge(sNe, 4 * h + tg - 1)
                        inst = t.matmul(pnb[0:1, 0:512],
                                        onecf_sb[base:base + 64, :],
                                        sqq_sb[base:base + 64, 512 * tg:512 * tg + 512],
                                        start=True, stop=True)
                        inst.then_inc(sNm, 1)
                    k = 0
                    for mc in range(2):
                        for tg in range(4):
                            pb = (pq0, pq1)[k % 2]
                            if 8 * h + k >= 2:
                                t.wait_ge(sCl, 8 * h + k - 1)
                            t.wait_ge(sNe, 4 * h + tg + 1)
                            t.matmul(pb[:, :],
                                     wt_sb[base:base + 64, 128 * mc:128 * mc + 128],
                                     q_sb[base:base + 64, dc, 512 * tg:512 * tg + 512],
                                     start=True, stop=False)
                            inst = t.matmul(pb[:, :], ones1f_sb[:, :],
                                            nq_sb[:, 512 * tg:512 * tg + 512],
                                            start=False, stop=True)
                            inst.then_inc(sPm, 1)
                            k += 1
                    for tt in range(NT):
                        pb = (po0, po1)[tt % 2]
                        if 16 * h + tt >= 2:
                            t.wait_ge(sOe, 16 * h + tt - 1)
                        inst = None
                        for mc in range(2):
                            t.wait_ge(sEx, 8 * h + 4 * mc + tt // 4 + 1)
                            kvr = (4 * (h // 2) + 2 * (h % 2) + mc) if kv_a2a \
                                else (2 * h + mc)
                            inst = t.matmul(pb[:, 0:65],
                                            pq_sb[:, mc, 128 * tt:128 * tt + 128],
                                            kv_sb[:, kvr, :],
                                            start=(mc == 0), stop=(mc == 1))
                        inst.then_inc(sOm, 1)

                @block.scalar
                def _(s, h=h, dc=dc, base=base):
                    if h >= 1:
                        s.wait_ge(sNm, 4 * h)
                    s.activation(sqq_sb[base:base + 64, :],
                                 q_sb[base:base + 64, dc, :],
                                 AF.Square).then_inc(sSqm, 1)
                    for tg in range(4):
                        pnb = (pn0, pn1)[tg % 2]
                        s.wait_ge(sNm, 4 * h + tg + 1)
                        if h >= 1:
                            s.wait_ge(sPm, 8 * (h - 1) + 4 + tg + 1)
                        s.activation(nq_sb[0:1, 512 * tg:512 * tg + 512],
                                     pnb[0:1, 0:512], AF.Identity,
                                     scale=-0.5).then_inc(sNe, 1)
                    if h >= 1:
                        s.wait_ge(sOm, 16 * h)
                    for k in range(8):
                        mc, tg = divmod(k, 4)
                        s.wait_ge(sCl, 8 * h + k + 1)
                        s.activation(pq_sb[:, mc, 512 * tg:512 * tg + 512],
                                     pqe_sb[:, k % 2, :], AF.Exp).then_inc(sEx, 1)

                @block.vector
                def _(v, h=h, dc=dc, base=base):
                    for k in range(8):
                        pb = (pq0, pq1)[k % 2]
                        v.wait_ge(sPm, 8 * h + k + 1)
                        if 8 * h + k >= 2:
                            v.wait_ge(sEx, 8 * h + k - 1)
                        v.tensor_scalar(pqe_sb[:, k % 2, :], pb[:, :],
                                        -LN16, -60.0,
                                        ALU.add, ALU.max).then_inc(sCl, 1)
                    ssync = SelfSync(v, sVx, dummy=sdum_sb[:, 0:1])
                    ssync.n = 2 * 16 * h
                    for tt in range(NT):
                        pb = (po0, po1)[tt % 2]
                        v.wait_ge(sOm, 16 * h + tt + 1)
                        inst = v.tensor_scalar(zc_sb[:, tt:tt + 1], pb[:, 64:65],
                                               1e-6, None, ALU.add)
                        ssync.flush(inst)
                        inst = v.reciprocal(rz_sb[:, tt:tt + 1], zc_sb[:, tt:tt + 1])
                        ssync.flush(inst)
                        v.tensor_scalar(s1_sb[:, tt, 64 * h:64 * h + 64],
                                        pb[:, 0:64], rz_sb[:, tt:tt + 1],
                                        None, ALU.mult).then_inc(sOe, 1)

        with nc.Block() as block:
            @block.tensor
            def _(t):
                for ti in range(NT):
                    if ti >= 1:
                        t.wait_ge(sLv, ti)
                    inst = None
                    for ci in range(NC_D):
                        inst = t.matmul(plg[:, 0:16],
                                        xn1_sb[:, ci, 128 * ti:128 * ti + 128],
                                        wrg_sb[:, ci, :], start=(ci == 0),
                                        stop=(ci == NC_D - 1))
                    inst.then_inc(sLm, 1)

            @block.vector
            def _(v):
                ssync = SelfSync(v, sVx, dummy=sdum_sb[:, 1:2])
                ssync.n = 2 * 16 * 16
                for ti in range(NT):
                    v.wait_ge(sLm, ti + 1)
                    v.tensor_copy(lgr_sb[:, ti, :], plg[:, 0:8])
                    v.tensor_copy(lgg_sb[:, ti, :], plg[:, 8:16]).then_inc(sLv, 1)
                ssync.barrier()
                v.reduce_max(rw_sb[:, :, 0], lgr_sb[:, :, :], AX.X, negate=True)
                inst = v.reduce_max(rw_sb[:, :, 1], lgg_sb[:, :, :], AX.X, negate=True)
                ssync.flush(inst)
                inst = v.tensor_scalar(rw_sb[:, :, 2], rw_sb[:, :, 1], 10.0, None,
                                       ALU.mult)
                ssync.flush(inst)
                v.nop().then_inc(sLv, 1)
                v.wait_ge(sSc, 32)
                inst = v.reduce_sum(rw_sb[:, :, 3], er_sb[:, :, :], AX.X)
                inst2 = v.reduce_sum(rw_sb[:, :, 4], eg_sb[:, :, :], AX.X)
                ssync.flush(inst2)
                v.reciprocal(rw_sb[:, :, 5], rw_sb[:, :, 3])
                inst = v.reciprocal(rw_sb[:, :, 6], rw_sb[:, :, 4])
                ssync.flush(inst)
                for ti in range(NT):
                    v.tensor_scalar(er_sb[:, ti, :], er_sb[:, ti, :],
                                    rw_sb[:, ti, 5:6], None, ALU.mult)
                    v.tensor_scalar(eg_sb[:, ti, :], eg_sb[:, ti, :],
                                    rw_sb[:, ti, 6:7], None, ALU.mult)
                ssync.barrier()
                inst = v.tensor_mul(cw_sb[:, :, :], er_sb[:, :, :], eg_sb[:, :, :])
                ssync.flush(inst)
                inst = v.reduce_sum(rw_sb[:, :, 7], cw_sb[:, :, :], AX.X)
                ssync.flush(inst)
                inst = v.tensor_scalar(rw_sb[:, :, 8], rw_sb[:, :, 7], 1e-9, None,
                                       ALU.add)
                ssync.flush(inst)
                inst = v.reciprocal(rw_sb[:, :, 9], rw_sb[:, :, 8])
                ssync.flush(inst)
                for ti in range(NT):
                    v.tensor_scalar(cw_sb[:, ti, :], cw_sb[:, ti, :],
                                    rw_sb[:, ti, 9:10], None, ALU.mult)

            @block.scalar
            def _(s):
                s.wait_ge(sLv, NT + 1)
                for ti in range(NT):
                    s.activation(er_sb[:, ti, :], lgr_sb[:, ti, :], AF.Exp,
                                 bias=rw_sb[:, ti, 0:1]).then_inc(sSc, 1)
                for ti in range(NT):
                    s.activation(eg_sb[:, ti, :], lgg_sb[:, ti, :], AF.Exp,
                                 bias=rw_sb[:, ti, 2:3], scale=10.0).then_inc(sSc, 1)

        with nc.Block() as block:
            @block.sync
            def _(sync):
                sync.dma_start(out=s1_out.ap().rearrange("(T p) d -> p T d", p=128),
                               in_=s1_sb[:, :, :]).then_inc(st, 16)
                sync.dma_start(out=cw_out.ap().rearrange("(T p) d -> p T d", p=128),
                               in_=cw_sb[:, :, :]).then_inc(st, 16)
                sync.wait_ge(st, 32)
    return nc


def build_launch_C3b():
    nc = bass.Bass("TRN2", target_bir_lowering=False, debug=False,
                   detect_race_conditions=False)
    xn1_in = nc.dram_tensor("xn1fm", [D, TCORE], BF16, kind="ExternalInput")
    x1_in = nc.dram_tensor("x1", [TCORE, D], F32, kind="ExternalInput")
    s1_in = nc.dram_tensor("s1", [TCORE, D], BF16, kind="ExternalInput")
    cw_in = nc.dram_tensor("cw", [TCORE, 8], F32, kind="ExternalInput")
    lin_in = nc.dram_tensor("lin", [8, 32, 128, 3072], F32, kind="ExternalInput")
    x2_out = nc.dram_tensor("x2", [TCORE, D], F32, kind="ExternalOutput")

    HT = 1024
    NTT = HT // 128

    emit_C3b(nc, "C3b_", xn1_in, x1_in, s1_in, cw_in, lin_in, x2_out)
    return nc


def emit_C3b(nc, PFX, xn1_in, x1_in, s1_in, cw_in, lin_in, x2_out):
    HT = 1024
    NTT = HT // 128
    with Phase(nc, PFX) as ph:
        xn1_sb = ph.sbuf("xn1_sb", [128, NC_D, TCORE], BF16)
        s1_sb = ph.sbuf("s1_sb", [128, NT, D], BF16)
        cw_sb = ph.sbuf("cw_sb", [128, NT, 8], F32)
        h_sb = ph.sbuf("h_sb", [128, 32, HT], BF16)
        acc_sb = ph.sbuf("acc_sb", [128, NTT, D], F32)
        wbuf_sb = ph.sbuf("wbuf_sb", [128, 2, 2048], BF16)
        wdb_sb = ph.sbuf("wdb_sb", [128, 2, 512], BF16)
        sg_sb0 = ph.sbuf("sg_sb0", [128, 512], BF16)
        sg_sb1 = ph.sbuf("sg_sb1", [128, 512], BF16)
        g1_sb0 = ph.sbuf("g1_sb0", [128, D], BF16)
        g1_sb1 = ph.sbuf("g1_sb1", [128, D], BF16)
        x1t_sb0 = ph.sbuf("x1t_sb0", [128, D], F32)
        x1t_sb1 = ph.sbuf("x1t_sb1", [128, D], F32)
        sg1_sb0 = ph.sbuf("sg1_sb0", [128, D], F32)
        sg1_sb1 = ph.sbuf("sg1_sb1", [128, D], F32)
        out_sb0 = ph.sbuf("out_sb0", [128, D], F32)
        out_sb1 = ph.sbuf("out_sb1", [128, D], F32)

        p8 = [ph.psum("p8_%d" % i, [128, 512], F32) for i in range(8)]

        ld_x = ph.sem("ld_x")
        ld_t = ph.sem("ld_t")
        ld_d = ph.sem("ld_d")
        ld_1 = ph.sem("ld_1")
        sGm = ph.sem("sGm")
        sUm = ph.sem("sUm")
        sSi = ph.sem("sSi")
        sGU = ph.sem("sGU")
        sDm = ph.sem("sDm")
        sDe = ph.sem("sDe")
        sG1 = ph.sem("sG1")
        sCv = ph.sem("sCv")
        sCm = ph.sem("sCm")
        st = ph.sem("st")

        with nc.Block() as block:
            @block.sync
            def _(sync):
                sync.dma_start(out=xn1_sb[:, :, :],
                               in_=xn1_in.ap().rearrange("(c p) t -> p c t", p=128)).then_inc(ld_x, 16)
                sync.dma_start(out=s1_sb[:, :, :],
                               in_=s1_in.ap().rearrange("(T p) d -> p T d", p=128)).then_inc(ld_x, 16)
                sync.dma_start(out=cw_sb[:, :, :],
                               in_=cw_in.ap().rearrange("(T p) d -> p T d", p=128)).then_inc(ld_x, 16)

        nqs = [0]
        nwd = [0]
        nx1 = [0]

        for hf in range(2):
            q = 8 * hf
            with nc.Block() as block:
                @block.gpsimd
                def _(g, hf=hf):
                    for e in range(8):
                        q = 8 * hf + e
                        for fs in range(32):
                            k = 32 * q + fs
                            if k >= 2:
                                g.wait_ge(sGm, 2 * (k - 1))
                            g.dma_start(out=wbuf_sb[:, fs % 2, :],
                                        in_=lin_in[e, fs, :, 0:2048]).then_inc(ld_t, 16)
                        for dh in range(2):
                            for fs in range(32):
                                kd = 64 * q + 32 * dh + fs
                                if kd >= 2:
                                    g.wait_ge(sDm, kd - 1)
                                g.dma_start(out=wdb_sb[:, fs % 2, :],
                                            in_=lin_in[e, fs, :,
                                                       2048 + 512 * dh:2560 + 512 * dh]
                                            ).then_inc(ld_d, 16)

                @block.sync
                def _(sync, hf=hf):
                    for tt in range(NTT):
                        k = NTT * hf + tt
                        if k >= 2:
                            sync.wait_ge(sCv, k - 1)
                            tiP = k - 2
                            sync.dma_start(
                                out=x2_out.ap().rearrange("(T p) d -> p T d",
                                                          p=128)[:, tiP, :],
                                in_=(out_sb0, out_sb1)[tiP % 2][:, :]).then_inc(st, 16)
                        sync.dma_start(
                            out=(x1t_sb0, x1t_sb1)[tt % 2][:, :],
                            in_=x1_in.ap().rearrange("(T p) d -> p T d", p=128)[:, k, :]
                        ).then_inc(ld_1, 16)
                    for tt in (NTT - 2, NTT - 1):
                        k = NTT * hf + tt
                        sync.wait_ge(sCv, k + 1)
                        sync.dma_start(
                            out=x2_out.ap().rearrange("(T p) d -> p T d", p=128)[:, k, :],
                            in_=(out_sb0, out_sb1)[tt % 2][:, :]).then_inc(st, 16)
                    sync.wait_ge(st, 16 * NTT * (hf + 1))

                @block.tensor
                def _(t, hf=hf):
                    if hf == 0:
                        t.wait_ge(ld_x, 48)
                    for e in range(8):
                        q = 8 * hf + e
                        for fs in range(32):
                            k = 32 * q + fs
                            t.wait_ge(ld_t, 16 * (k + 1))
                            for tg in range(2):
                                kk = 2 * k + tg
                                pg = p8[(kk % 2) * 2]
                                pu = p8[(kk % 2) * 2 + 1]
                                if kk >= 2:
                                    t.wait_ge(sSi, kk - 1)
                                    t.wait_ge(sGU, kk - 1)
                                if fs == 0 and tg == 0 and q > 0:
                                    t.wait_ge(sDe, 16 * q)
                                for ci in range(NC_D):
                                    t.matmul(pg[:, :], wbuf_sb[:, fs % 2,
                                                               128 * ci:128 * ci + 128],
                                             xn1_sb[:, ci, HT * hf + 512 * tg:
                                                    HT * hf + 512 * tg + 512],
                                             start=(ci == 0), stop=(ci == NC_D - 1))
                                inst = None
                                for ci in range(NC_D):
                                    inst = t.matmul(pu[:, :],
                                                    wbuf_sb[:, fs % 2,
                                                            1024 + 128 * ci:1152 + 128 * ci],
                                                    xn1_sb[:, ci, HT * hf + 512 * tg:
                                                           HT * hf + 512 * tg + 512],
                                                    start=(ci == 0), stop=(ci == NC_D - 1))
                                inst.then_inc(sGm, 1)
                        t.wait_ge(sGU, 64 * (q + 1))
                        t.wait_ge(sSi, 64 * (q + 1))
                        for dh in range(2):
                            if dh == 1:
                                t.wait_ge(sDe, 16 * q + 8)
                            for fs in range(32):
                                kd = 64 * q + 32 * dh + fs
                                t.wait_ge(ld_d, 16 * (kd + 1))
                                inst = None
                                for tt in range(NTT):
                                    inst = t.matmul(p8[tt][:, :],
                                                    h_sb[:, fs, 128 * tt:128 * tt + 128],
                                                    wdb_sb[:, fs % 2, :],
                                                    start=(fs == 0), stop=(fs == 31))
                                inst.then_inc(sDm, 1)

                @block.scalar
                def _(s, hf=hf):
                    for e in range(8):
                        q = 8 * hf + e
                        for fs in range(32):
                            for tg in range(2):
                                kk = 2 * (32 * q + fs) + tg
                                pg = p8[(kk % 2) * 2]
                                sg = (sg_sb0, sg_sb1)[kk % 2]
                                s.wait_ge(sGm, kk + 1)
                                if kk >= 2:
                                    s.wait_ge(sGU, kk - 1)
                                s.activation(sg[:, :], pg[:, :],
                                             AF.Silu).then_inc(sSi, 1)
                    for tt in range(NTT):
                        k = NTT * hf + tt
                        g1 = (g1_sb0, g1_sb1)[tt % 2]
                        s.wait_ge(sDe, 16 * 8 * (hf + 1))
                        if k >= 2:
                            s.wait_ge(sCv, k - 1)
                        s.activation(g1[:, :], acc_sb[:, tt, :],
                                     AF.Silu).then_inc(sG1, 1)

                @block.vector
                def _(v, hf=hf):
                    if hf == 0:
                        v.wait_ge(ld_x, 48)
                    for e in range(8):
                        q = 8 * hf + e
                        for fs in range(32):
                            for tg in range(2):
                                kk = 2 * (32 * q + fs) + tg
                                pu = p8[(kk % 2) * 2 + 1]
                                sg = (sg_sb0, sg_sb1)[kk % 2]
                                v.wait_ge(sSi, kk + 1)
                                v.tensor_mul(h_sb[:, fs, 512 * tg:512 * tg + 512],
                                             sg[:, :], pu[:, :]).then_inc(sGU, 1)
                        for dh in range(2):
                            for tt in range(NTT):
                                v.wait_ge(sDm, 64 * q + 32 * dh + 32)
                                ti = NTT * hf + tt
                                if e == 0:
                                    v.tensor_scalar(acc_sb[:, tt, 512 * dh:512 * dh + 512],
                                                    p8[tt][:, :], cw_sb[:, ti, 0:1],
                                                    None, ALU.mult).then_inc(sDe, 1)
                                else:
                                    v.scalar_tensor_tensor(
                                        acc_sb[:, tt, 512 * dh:512 * dh + 512],
                                        p8[tt][:, :], cw_sb[:, ti, e:e + 1],
                                        acc_sb[:, tt, 512 * dh:512 * dh + 512],
                                        ALU.mult, ALU.add).then_inc(sDe, 1)
                    for tt in range(NTT):
                        k = NTT * hf + tt
                        ti = NTT * hf + tt
                        g1 = (g1_sb0, g1_sb1)[tt % 2]
                        sg1 = (sg1_sb0, sg1_sb1)[tt % 2]
                        x1t = (x1t_sb0, x1t_sb1)[tt % 2]
                        outb = (out_sb0, out_sb1)[tt % 2]
                        v.wait_ge(sG1, k + 1)
                        v.wait_ge(ld_1, 16 * (k + 1))
                        if k >= 2:
                            v.wait_ge(st, 16 * (k - 1))
                        inst = v.tensor_mul(sg1[:, :], s1_sb[:, ti, :], g1[:, :])
                        inst.then_inc(sCm, 1)
                        v.wait_ge(sCm, k + 1)
                        v.tensor_add(outb[:, :], x1t[:, :], sg1[:, :]).then_inc(sCv, 1)

    return nc




_NC_CACHE = {}


def _nc(name, builder):
    if name not in _NC_CACHE:
        _NC_CACHE[name] = builder()
    return _NC_CACHE[name]


def _pack_lin(wg, wu, wd):
    lin = np.empty((8, 32, 128, 3072), np.float32)
    for e in range(8):
        ge = np.asarray(wg[e], np.float32).reshape(8, 128, 4096)
        ue = np.asarray(wu[e], np.float32).reshape(8, 128, 4096)
        de = np.asarray(wd[e], np.float32)
        for fs in range(32):
            lin[e, fs, :, 0:1024] = ge[:, :, 128 * fs:128 * fs + 128].transpose(
                1, 0, 2).reshape(128, 1024)
            lin[e, fs, :, 1024:2048] = ue[:, :, 128 * fs:128 * fs + 128].transpose(
                1, 0, 2).reshape(128, 1024)
            lin[e, fs, :, 2048:3072] = de[128 * fs:128 * fs + 128, :]
    return lin


def kernel(**inputs):
    return kernel_fused(**inputs)


def kernel_unfused(**inputs):
    cores = list(range(NCORES))
    ones1 = np.ones((1, 128), np.float32)
    onec = np.ones((128, 1), np.float32)
    ident = np.eye(128, dtype=np.float32)

    resA = run_bass_kernel_spmd(_nc("A", build_launch_A), _prep_A_inputs(inputs),
                                core_ids=cores)
    xnfm = [r["xn_fm"] for r in resA.results]
    g0 = [r["g"] for r in resA.results]

    resB = run_bass_kernel_spmd(_nc("B", build_launch_B),
                                _prep_B_inputs(xnfm, inputs), core_ids=cores)
    sB = [r["s"] for r in resB.results]

    x = np.asarray(inputs["L0_attn_x"], np.float32).reshape(TTOK, D)
    wr = np.ascontiguousarray(np.asarray(inputs["L0_attn_Wr"], np.float32))
    br = np.ascontiguousarray(np.asarray(inputs["L0_attn_br"], np.float32)[None, :])
    mapsC1 = []
    for c in cores:
        s_own = np.concatenate([sB[i][TCORE * c:TCORE * (c + 1), :] for i in cores],
                               axis=1)
        mapsC1.append(dict(x=np.ascontiguousarray(x[TCORE * c:TCORE * (c + 1)]),
                           sfm=np.ascontiguousarray(s_own.T), g=g0[c], wr=wr, br=br,
                           ones1=ones1, ident=ident))
    resC1 = run_bass_kernel_spmd(_nc("C1", build_launch_C1), mapsC1, core_ids=cores)
    x1 = [r["x1"] for r in resC1.results]
    xn1fm = [r["xn1_fm"] for r in resC1.results]

    xn1_full = np.ascontiguousarray(np.concatenate(xn1fm, axis=1))
    wk = np.asarray(inputs["L1_attn_Wk"], np.float32)
    wv = np.asarray(inputs["L1_attn_Wv"], np.float32)
    omega = np.asarray(inputs["L1_attn_omega"], np.float32)
    wTd = np.ascontiguousarray(np.concatenate([omega.T, omega.T], 0))
    mapsC2 = []
    for c in cores:
        mapsC2.append(dict(xn1f=xn1_full,
                           wk2h=np.ascontiguousarray(wk[:, 128 * c:128 * c + 128]),
                           wv2h=np.ascontiguousarray(wv[:, 128 * c:128 * c + 128]),
                           wT=wTd, onec=onec))
    resC2 = run_bass_kernel_spmd(_nc("C2", build_launch_C2), mapsC2, core_ids=cores)
    kv4 = [r["kv4"] for r in resC2.results]

    wq = np.ascontiguousarray(np.asarray(inputs["L1_attn_Wq"], np.float32))
    wrt = np.ascontiguousarray(np.asarray(inputs["L1_ffn_w_router"], np.float32))
    wgt = np.ascontiguousarray(np.asarray(inputs["L1_ffn_w_gating"], np.float32))
    mapsC3a = []
    for c in cores:
        b = c // 4
        kvb = np.concatenate(
            [kv4[h // 2][512 * (h % 2) + 256 * b + 128 * mc:
                         512 * (h % 2) + 256 * b + 128 * mc + 128, :]
             for h in range(16) for mc in range(2)], axis=0)
        mapsC3a.append(dict(xn1fm=xn1fm[c], kvb=np.ascontiguousarray(kvb), wq=wq,
                            wT=wTd, wrt=wrt, wgt=wgt, onec=onec, ones1=ones1))
    resC3a = run_bass_kernel_spmd(_nc("C3a", build_launch_C3a), mapsC3a,
                                  core_ids=cores)
    s1 = [r["s1"] for r in resC3a.results]
    cw = [r["cw"] for r in resC3a.results]

    lin = _pack_lin(inputs["L1_ffn_w_gate"], inputs["L1_ffn_w_up"],
                    inputs["L1_ffn_w_down"])
    mapsC3b = [dict(xn1fm=xn1fm[c], x1=x1[c], s1=s1[c], cw=cw[c], lin=lin)
               for c in cores]
    resC3b = run_bass_kernel_spmd(_nc("C3b", build_launch_C3b), mapsC3b,
                                  core_ids=cores)
    out = np.concatenate([r["x2"] for r in resC3b.results], axis=0)
    return np.ascontiguousarray(out.reshape(2, SEQB, D), dtype=np.float32)


from concourse.aot_env import allow_ici_all_to_all


def _emit_cc(nc, PFX, kind, src_t, dst_t):
    with Phase(nc, PFX) as ph:
        cc = ph.sem("cc")
        with nc.Block() as block:
            @block.gpsimd
            def _(g):
                if kind == "AllToAll":
                    with allow_ici_all_to_all():
                        g.collective_compute(kind, ALU.bypass,
                                             replica_groups=[list(range(NCORES))],
                                             ins=[src_t.ap()],
                                             outs=[dst_t.ap()]).then_inc(cc)
                else:
                    g.collective_compute(kind, ALU.bypass,
                                         replica_groups=[list(range(NCORES))],
                                         ins=[src_t.ap()],
                                         outs=[dst_t.ap()]).then_inc(cc)
                g.wait_ge(cc, 1)


def build_fused():
    nc = bass.Bass("TRN2", target_bir_lowering=False, debug=False,
                   detect_race_conditions=False)
    x_in = nc.dram_tensor("x", [TCORE, D], F32, kind="ExternalInput")
    wgate_in = nc.dram_tensor("wgate", [D, 7], F32, kind="ExternalInput")
    wst_in = nc.dram_tensor("wst", [D, D], F32, kind="ExternalInput")
    wet_in = nc.dram_tensor("wet", [D, D], F32, kind="ExternalInput")
    bias_in = nc.dram_tensor("bias", [1, D], F32, kind="ExternalInput")
    ones1_in = nc.dram_tensor("ones1", [1, 128], F32, kind="ExternalInput")
    ident_in = nc.dram_tensor("ident", [128, 128], F32, kind="ExternalInput")
    wqkvL_in = nc.dram_tensor("wqkvL", [D, 384], F32, kind="ExternalInput")
    bqkv_in = nc.dram_tensor("bqkvc", [64, 3], F32, kind="ExternalInput")
    bv_in = nc.dram_tensor("bvrow", [1, 64], F32, kind="ExternalInput")
    onec_in = nc.dram_tensor("onec", [128, 1], F32, kind="ExternalInput")
    msl_in = nc.dram_tensor("msl", [128, 128], mybir.dt.uint8, kind="ExternalInput")
    onet_in = nc.dram_tensor("onet", [128, 128], F32, kind="ExternalInput")
    wr_in = nc.dram_tensor("wr", [D, D], F32, kind="ExternalInput")
    br_in = nc.dram_tensor("br", [1, D], F32, kind="ExternalInput")
    wk_in = nc.dram_tensor("wk2h", [D, 128], F32, kind="ExternalInput")
    wv_in = nc.dram_tensor("wv2h", [D, 128], F32, kind="ExternalInput")
    wt_in = nc.dram_tensor("wT", [128, 256], F32, kind="ExternalInput")
    wq_in = nc.dram_tensor("wq", [D, D], F32, kind="ExternalInput")
    wrt_in = nc.dram_tensor("wrt", [D, 8], F32, kind="ExternalInput")
    wgt_in = nc.dram_tensor("wgt", [D, 8], F32, kind="ExternalInput")
    lin_in = nc.dram_tensor("lin", [8, 32, 128, 3072], F32, kind="ExternalInput")
    x2_out = nc.dram_tensor("x2", [TCORE, D], F32, kind="ExternalOutput")
    xn_c = nc.dram_tensor("xn_c", [D, TCORE], BF16)
    xng = nc.dram_tensor("xng", [8 * D, TCORE], BF16)
    gd = nc.dram_tensor("gd", [TCORE, D], BF16)
    s_c = nc.dram_tensor("s_c", [TTOK, 128], BF16)
    s_t = nc.dram_tensor("s_t", [TTOK, 128], BF16)
    x1_d = nc.dram_tensor("x1_d", [TCORE, D], F32)
    xn1_c = nc.dram_tensor("xn1_c", [D, TCORE], BF16)
    xn1g = nc.dram_tensor("xn1g", [8 * D, TCORE], BF16)
    kv_c = nc.dram_tensor("kv_c", [4096, 65], F32)
    kv_t = nc.dram_tensor("kv_t", [4096, 65], F32)
    s1_d = nc.dram_tensor("s1_d", [TCORE, D], BF16)
    cw_d = nc.dram_tensor("cw_d", [TCORE, 8], F32)

    emit_A(nc, "A_", x_in, wgate_in, wst_in, wet_in, bias_in, ones1_in, ident_in,
           xn_c, gd)
    _emit_cc(nc, "G1_", "AllGather", xn_c, xng)
    emit_B(nc, "B_", None, None, bqkv_in, bv_in, ones1_in, onec_in, msl_in,
           onet_in, s_c, xng=xng, wqkvL_in=wqkvL_in)
    _emit_cc(nc, "T1_", "AllToAll", s_c, s_t)
    emit_C1(nc, "C1_", x_in, None, gd, wr_in, br_in, ones1_in, ident_in,
            x1_d, xn1_c, s_t_in=s_t)
    _emit_cc(nc, "G2_", "AllGather", xn1_c, xn1g)
    emit_C2(nc, "C2_", xn1g, wk_in, wv_in, wt_in, onec_in, kv_c,
            gathered=True, dup_out=True)
    _emit_cc(nc, "T2_", "AllToAll", kv_c, kv_t)
    emit_C3a(nc, "C3a_", xn1_c, kv_t, wq_in, wt_in, wrt_in, wgt_in, onec_in,
             ones1_in, s1_d, cw_d, kv_a2a=True)
    emit_C3b(nc, "C3b_", xn1_c, x1_d, s1_d, cw_d, lin_in, x2_out)
    return nc


def _prep_fused_inputs(inputs):
    x = np.asarray(inputs["L0_attn_x"], np.float32).reshape(TTOK, D)
    wst = np.ascontiguousarray(np.asarray(inputs["L0_ffn_W_shared"], np.float32).T)
    wet = np.ascontiguousarray(np.asarray(inputs["L0_ffn_W_expert"], np.float32).T)
    wgate = np.ascontiguousarray(np.asarray(inputs["L0_ffn_W_gate"], np.float32))
    bias = np.ascontiguousarray(np.asarray(inputs["L0_ffn_bias"], np.float32)[None, :])
    ones1 = np.ones((1, 128), np.float32)
    onec = np.ones((128, 1), np.float32)
    ident = np.eye(128, dtype=np.float32)
    jj, ii = np.meshgrid(np.arange(128), np.arange(128), indexing="ij")
    msl = (jj < ii).astype(np.uint8)
    onet = np.ones((128, 128), np.float32)
    wqkv = np.asarray(inputs["L0_attn_Wqkv"], np.float32)
    bq = np.asarray(inputs["L0_attn_bqkv"], np.float32)
    bqkvc = np.ascontiguousarray(np.stack([bq[0:64], bq[64:128], bq[128:192]], 1))
    bvrow = np.ascontiguousarray(bq[None, 128:192])
    wr = np.ascontiguousarray(np.asarray(inputs["L0_attn_Wr"], np.float32))
    br = np.ascontiguousarray(np.asarray(inputs["L0_attn_br"], np.float32)[None, :])
    wk = np.asarray(inputs["L1_attn_Wk"], np.float32)
    wv = np.asarray(inputs["L1_attn_Wv"], np.float32)
    omega = np.asarray(inputs["L1_attn_omega"], np.float32)
    wTd = np.ascontiguousarray(np.concatenate([omega.T, omega.T], 0))
    wq = np.ascontiguousarray(np.asarray(inputs["L1_attn_Wq"], np.float32))
    wrt = np.ascontiguousarray(np.asarray(inputs["L1_ffn_w_router"], np.float32))
    wgt = np.ascontiguousarray(np.asarray(inputs["L1_ffn_w_gating"], np.float32))
    lin = _pack_lin(inputs["L1_ffn_w_gate"], inputs["L1_ffn_w_up"],
                    inputs["L1_ffn_w_down"])
    maps = []
    for c in range(NCORES):
        wqkvL = np.zeros((D, 384), np.float32)
        for hl in range(2):
            h = 2 * c + hl
            wqkvL[64 * h:64 * h + 64, 64 * hl:64 * hl + 64] = wqkv[:, 0:64]
            wqkvL[64 * h:64 * h + 64, 128 + 64 * hl:128 + 64 * hl + 64] = wqkv[:, 64:128]
            wqkvL[64 * h:64 * h + 64, 256 + 64 * hl:256 + 64 * hl + 64] = wqkv[:, 128:192]
        maps.append(dict(
            x=np.ascontiguousarray(x[TCORE * c:TCORE * (c + 1)]),
            wgate=wgate, wst=wst, wet=wet, bias=bias, ones1=ones1, ident=ident,
            wqkvL=wqkvL, bqkvc=bqkvc, bvrow=bvrow, onec=onec, msl=msl, onet=onet,
            wr=wr, br=br,
            wk2h=np.ascontiguousarray(wk[:, 128 * c:128 * c + 128]),
            wv2h=np.ascontiguousarray(wv[:, 128 * c:128 * c + 128]),
            wT=wTd, wq=wq, wrt=wrt, wgt=wgt, lin=lin))
    return maps


def kernel_fused(**inputs):
    res = run_bass_kernel_spmd(_nc("F", build_fused), _prep_fused_inputs(inputs),
                               core_ids=list(range(NCORES)))
    out = np.concatenate([r["x2"] for r in res.results], axis=0)
    return np.ascontiguousarray(out.reshape(2, SEQB, D), dtype=np.float32)


# version 06
# speedup vs baseline: 1.3483x; 1.3483x over previous
import contextlib
import numpy as np
import ml_dtypes

import concourse.bass as bass
import concourse.mybir as mybir
from concourse.bass_utils import run_bass_kernel_spmd

F32 = mybir.dt.float32
BF16 = mybir.dt.bfloat16
AF = mybir.ActivationFunctionType
ALU = mybir.AluOpType
AX = mybir.AxisListType
BF = ml_dtypes.bfloat16

NCORES = 8
D = 1024
TTOK = 16384
TCORE = 2048
NT = TCORE // 128
NC_D = D // 128


class SelfSync:

    def __init__(self, eng, sem, dummy=None):
        self.eng, self.sem, self.n = eng, sem, 0
        self.dummy = dummy

    def flush(self, inst):
        self.n += 1
        inst.then_inc(self.sem, 1)
        self.eng.wait_ge(self.sem, self.n)

    def barrier(self):
        self.n += 1
        self.eng.memset(self.dummy, 0.0).then_inc(self.sem, 1)
        self.eng.wait_ge(self.sem, self.n)


class Phase:

    def __init__(self, nc, prefix=""):
        self.nc, self.prefix = nc, prefix
        self.stack = contextlib.ExitStack()
        self.sems = []

    def __enter__(self):
        self.stack.__enter__()
        return self

    def sbuf(self, name, shape, dtype):
        return self.stack.enter_context(
            self.nc.sbuf_tensor(self.prefix + name, shape, dtype))

    def psum(self, name, shape, dtype):
        return self.stack.enter_context(
            self.nc.psum_tensor(self.prefix + name, shape, dtype))

    def sem(self, name):
        h = self.stack.enter_context(self.nc.semaphore(self.prefix + name))
        self.sems.append(h)
        return h

    def __exit__(self, *exc):
        if exc[0] is None and self.sems:
            sems = self.sems
            with self.nc.Block() as blk:
                @blk.gpsimd
                def _(g):
                    for h in sems:
                        g.sem_clear(range(h.num, h.num + 1))
        return self.stack.__exit__(*exc)



def _ln_stats(v, s, x_sb, sq0, sq1, sums, sumsq, mean, var, w0, w1, eps,
              sSq, sVc, ssync, n_tiles=NT, width=D):
    for ti in range(n_tiles):
        v.reduce_sum(sums[:, ti:ti + 1], x_sb[:, ti, :], AX.X)
    for ti in range(n_tiles):
        v.wait_ge(sSq, ti + 1)
        inst = v.reduce_sum(sumsq[:, ti:ti + 1], (sq0, sq1)[ti % 2][:, :], AX.X)
        inst.then_inc(sVc, 1)
    ssync.barrier()
    v.tensor_scalar(mean[:, :], sums[:, :], 1.0 / width, None, ALU.mult)
    inst = v.tensor_scalar(w0[:, :], sumsq[:, :], 1.0 / width, None, ALU.mult)
    ssync.flush(inst)
    inst = v.tensor_mul(w1[:, :], mean[:, :], mean[:, :])
    ssync.flush(inst)
    return v.tensor_sub(var[:, :], w0[:, :], w1[:, :])


def _ln_stats_scalar(s, x_sb, sq0, sq1, sSq, sVc, n_tiles=NT):
    for ti in range(n_tiles):
        if ti >= 2:
            s.wait_ge(sVc, ti - 1)
        s.activation((sq0, sq1)[ti % 2][:, :], x_sb[:, ti, :],
                     AF.Square).then_inc(sSq, 1)


def build_launch_A(dbg_xn_tm=False):
    nc = bass.Bass("TRN2", target_bir_lowering=False, debug=False,
                   detect_race_conditions=False)
    x_in = nc.dram_tensor("x", [TCORE, D], F32, kind="ExternalInput")
    wgate_in = nc.dram_tensor("wgate", [D, 7], F32, kind="ExternalInput")
    wst_in = nc.dram_tensor("wst", [D, D], F32, kind="ExternalInput")
    wet_in = nc.dram_tensor("wet", [D, D], F32, kind="ExternalInput")
    bias_in = nc.dram_tensor("bias", [1, D], F32, kind="ExternalInput")
    ones1_in = nc.dram_tensor("ones1", [1, 128], F32, kind="ExternalInput")
    ident_in = nc.dram_tensor("ident", [128, 128], F32, kind="ExternalInput")
    xnfm_out = nc.dram_tensor("xn_fm", [D, TCORE], BF16, kind="ExternalOutput")
    g_out = nc.dram_tensor("g", [TCORE, D], BF16, kind="ExternalOutput")

    emit_A(nc, "A_", x_in, wgate_in, wst_in, wet_in, bias_in, ones1_in, ident_in, xnfm_out, g_out, dbg_xn_tm=dbg_xn_tm)
    return nc


def emit_A(nc, PFX, x_in, wgate_in, wst_in, wet_in, bias_in, ones1_in, ident_in, xnfm_out, g_out, dbg_xn_tm=False):
    with Phase(nc, PFX) as ph:
        x_sb = ph.sbuf("x_sb", [128, NT, D], F32)
        xn_tm = ph.sbuf("xn_tm", [128, NT, D], BF16)
        xn_fm = ph.sbuf("xnf_sb", [128, NC_D, TCORE], BF16)
        g_sb = ph.sbuf("g_sb", [128, NT, D], BF16)
        wgate_sb = ph.sbuf("wgate_sb", [128, NC_D, 7], BF16)
        wst_sb = ph.sbuf("wst_sb", [128, NC_D, D], BF16)
        wet_sb = ph.sbuf("wet_sb", [128, NC_D, D], BF16)
        bias_sb = ph.sbuf("bias_sb", [1, D], BF16)
        ones1_sb = ph.sbuf("ones1_sb", [1, 128], BF16)
        ident_sb = ph.sbuf("ident_sb", [128, 128], BF16)
        eps_sb = ph.sbuf("eps_sb", [128, 1], F32)
        sdum_sb = ph.sbuf("sdum_sb", [128, 2], F32)
        sq0_sb = ph.sbuf("sq0_sb", [128, D], F32)
        sq1_sb = ph.sbuf("sq1_sb", [128, D], F32)
        sums_sb = ph.sbuf("sums_sb", [128, NT], F32)
        sumsq_sb = ph.sbuf("sumsq_sb", [128, NT], F32)
        mean_sb = ph.sbuf("mean_sb", [128, NT], F32)
        var_sb = ph.sbuf("var_sb", [128, NT], F32)
        w0_sb = ph.sbuf("w0_sb", [128, NT], F32)
        w1_sb = ph.sbuf("w1_sb", [128, NT], F32)
        std_sb = ph.sbuf("std_sb", [128, NT], F32)
        rstd_sb = ph.sbuf("rstd_sb", [128, NT], F32)
        lg_sb = ph.sbuf("lg_sb", [128, NT, 7], F32)
        fw_sb = ph.sbuf("fw_sb", [128, NT, 12], F32)
        e2_sb = ph.sbuf("e2_sb", [128, NT, 2], F32)
        fd_sb = ph.sbuf("fd_sb", [128, NT], F32)
        fr_sb = ph.sbuf("fr_sb", [128, NT], F32)
        fac_sb = ph.sbuf("fac_sb", [128, NT], F32)
        t_sb0 = sq0_sb[:, 0:512]
        t_sb1 = sq0_sb[:, 512:1024]
        s_sb0 = sq1_sb[:, 0:512]
        s_sb1 = sq1_sb[:, 512:1024]

        pt0 = ph.psum("pt0", [128, 1024], BF16)
        pt1 = ph.psum("pt1", [128, 1024], BF16)
        pl0 = ph.psum("pl0", [128, 512], F32)
        pl1 = ph.psum("pl1", [128, 512], F32)
        ps0 = ph.psum("ps0", [128, 512], F32)
        ps1 = ph.psum("ps1", [128, 512], F32)
        pe0 = ph.psum("pe0", [128, 512], F32)
        pe1 = ph.psum("pe1", [128, 512], F32)

        ld_x = ph.sem("ld_x")
        ld_w = ph.sem("ld_w")
        sSq = ph.sem("sSq")
        sSc = ph.sem("sSc")
        sVc = ph.sem("sVc")
        sVx = ph.sem("sVx")
        s2a = ph.sem("s2a")
        s2b = ph.sem("s2b")
        s3t = ph.sem("s3t")
        s3e = ph.sem("s3e")
        s4t = ph.sem("s4t")
        s4v = ph.sem("s4v")
        s4e = ph.sem("s4e")
        s5t = ph.sem("s5t")
        s5s = ph.sem("s5s")
        s5v = ph.sem("s5v")
        s5e = ph.sem("s5e")
        st = ph.sem("st")

        with nc.Block() as block:
            @block.sync
            def _(sync):
                sync.dma_start(out=x_sb[:, :, :],
                               in_=x_in.ap().rearrange("(T p) d -> p T d", p=128)).then_inc(ld_x, 16)

            @block.gpsimd
            def _(g):
                g.dma_start(out=wgate_sb[:, :, :],
                            in_=wgate_in.ap().rearrange("(c p) d -> p c d", p=128)).then_inc(ld_w, 16)
                g.dma_start(out=wst_sb[:, :, :],
                            in_=wst_in.ap().rearrange("(c p) d -> p c d", p=128)).then_inc(ld_w, 16)
                g.dma_start(out=wet_sb[:, :, :],
                            in_=wet_in.ap().rearrange("(c p) d -> p c d", p=128)).then_inc(ld_w, 16)
                g.dma_start(out=bias_sb[:, :], in_=bias_in[:, :]).then_inc(ld_w, 16)
                g.dma_start(out=ones1_sb[:, :], in_=ones1_in[:, :]).then_inc(ld_w, 16)
                g.dma_start(out=ident_sb[:, :], in_=ident_in[:, :]).then_inc(ld_w, 16)

        with nc.Block() as block:
            @block.vector
            def _(v):
                ssync = SelfSync(v, sVx, dummy=sdum_sb[:, 0:1])
                v.memset(eps_sb[:, :], 1e-5)
                v.wait_ge(ld_x, 16)
                inst = _ln_stats(v, None, x_sb, sq0_sb, sq1_sb, sums_sb, sumsq_sb,
                                 mean_sb, var_sb, w0_sb, w1_sb, eps_sb, sSq, sVc, ssync)
                inst.then_inc(s2a, 1)
                v.wait_ge(s2b, 1)
                inst = v.reciprocal(rstd_sb[:, :], std_sb[:, :])
                ssync.flush(inst)
                for ti in range(NT):
                    v.tensor_scalar(xn_tm[:, ti, :], x_sb[:, ti, :],
                                    mean_sb[:, ti:ti + 1], rstd_sb[:, ti:ti + 1],
                                    ALU.subtract, ALU.mult)

            @block.scalar
            def _(s):
                s.wait_ge(ld_x, 16)
                _ln_stats_scalar(s, x_sb, sq0_sb, sq1_sb, sSq, sVc)
                s.wait_ge(s2a, 1)
                s.activation(std_sb[:, :], var_sb[:, :], AF.Sqrt,
                             bias=eps_sb[:, 0:1]).then_inc(s2b, 1)

        with nc.Block() as block:
            @block.tensor
            def _(t):
                t.wait_ge(ld_w, 96)
                k = 0
                for ci in range(NC_D):
                    for tg in range(NT // 4):
                        pb = (pt0, pt1)[k % 2]
                        if k >= 2:
                            t.wait_ge(s3e, k - 1)
                        inst = None
                        for j in range(4):
                            ti = 4 * tg + j
                            inst = t.transpose(pb[:, 128 * j:128 * j + 128],
                                               xn_tm[:, ti, 128 * ci:128 * ci + 128],
                                               ident_sb[:, :])
                        inst.then_inc(s3t, 1)
                        k += 1

            @block.scalar
            def _(s):
                k = 0
                for ci in range(NC_D):
                    for tg in range(NT // 4):
                        pb = (pt0, pt1)[k % 2]
                        s.wait_ge(s3t, k + 1)
                        s.copy(xn_fm[:, ci, 512 * tg:512 * tg + 512],
                               pb[:, 0:512]).then_inc(s3e, 1)
                        k += 1

        with nc.Block() as block:
            @block.tensor
            def _(t):
                for ti in range(NT):
                    pb = (pl0, pl1)[ti % 2]
                    if ti >= 2:
                        t.wait_ge(s4v, ti - 1)
                    inst = None
                    for ci in range(NC_D):
                        inst = t.matmul(pb[:, 0:7], xn_fm[:, ci, 128 * ti:128 * ti + 128],
                                        wgate_sb[:, ci, :], start=(ci == 0),
                                        stop=(ci == NC_D - 1))
                    inst.then_inc(s4t, 1)

            @block.vector
            def _(v):
                ssync = SelfSync(v, s4e, dummy=sdum_sb[:, 1:2])
                inst = None
                for ti in range(NT):
                    pb = (pl0, pl1)[ti % 2]
                    v.wait_ge(s4t, ti + 1)
                    inst = v.tensor_copy(lg_sb[:, ti, :], pb[:, 0:7])
                    inst.then_inc(s4v, 1)
                ssync.barrier()
                c = lambda j: lg_sb[:, :, j]
                v.tensor_tensor(fw_sb[:, :, 0], c(0), c(1), ALU.max)
                inst = v.tensor_tensor(fw_sb[:, :, 1], c(0), c(1), ALU.min)
                a, b = 0, 1
                nxt = 2
                for j in range(2, 7):
                    tm = nxt; na = nxt + 1; nb = nxt + 2
                    nxt += 3 if j < 6 else 0
                    if nxt > 9:
                        nxt = 2
                    ssync.flush(inst)
                    v.tensor_tensor(fw_sb[:, :, tm], fw_sb[:, :, a], c(j), ALU.min)
                    inst = v.tensor_tensor(fw_sb[:, :, na], fw_sb[:, :, a], c(j),
                                           ALU.max)
                    ssync.flush(inst)
                    inst = v.tensor_tensor(fw_sb[:, :, nb], fw_sb[:, :, b],
                                           fw_sb[:, :, tm], ALU.max)
                    a, b = na, nb
                ssync.flush(inst)
                v.tensor_copy(fw_sb[:, :, 10], fw_sb[:, :, a])
                v.tensor_copy(fw_sb[:, :, 11],
                              fw_sb[:, :, b]).then_inc(s4v, 1)
                v.wait_ge(sSq, NT + 1)
                inst = v.tensor_add(fd_sb[:, :], e2_sb[:, :, 0], e2_sb[:, :, 1])
                ssync.flush(inst)
                inst = v.reciprocal(fr_sb[:, :], fd_sb[:, :])
                ssync.flush(inst)
                v.tensor_mul(fac_sb[:, :], fd_sb[:, :], fr_sb[:, :])

            @block.scalar
            def _(s):
                s.wait_ge(s4v, NT + 1)
                s.activation(e2_sb[:, :, :], fw_sb[:, :, 10:12],
                             AF.Exp).then_inc(sSq, 1)

        with nc.Block() as block:
            @block.tensor
            def _(t):
                k = 0
                for ti in range(NT):
                    for dh in range(2):
                        psb = (ps0, ps1)[k % 2]
                        peb = (pe0, pe1)[k % 2]
                        if k >= 2:
                            t.wait_ge(s5e, k - 1)
                        for ci in range(NC_D):
                            t.matmul(psb[:, :], xn_fm[:, ci, 128 * ti:128 * ti + 128],
                                     wst_sb[:, ci, 512 * dh:512 * dh + 512],
                                     start=(ci == 0), stop=False)
                        t.matmul(psb[:, :], ones1_sb[:, :], bias_sb[:, 512 * dh:512 * dh + 512],
                                 start=False, stop=True)
                        inst = None
                        for ci in range(NC_D):
                            inst = t.matmul(peb[:, :], xn_fm[:, ci, 128 * ti:128 * ti + 128],
                                            wet_sb[:, ci, 512 * dh:512 * dh + 512],
                                            start=(ci == 0), stop=(ci == NC_D - 1))
                        inst.then_inc(s5t, 1)
                        k += 1

            @block.scalar
            def _(s):
                k = 0
                for ti in range(NT):
                    for dh in range(2):
                        psb = (ps0, ps1)[k % 2]
                        ssb = (s_sb0, s_sb1)[k % 2]
                        tsb = (t_sb0, t_sb1)[k % 2]
                        s.wait_ge(s5t, k + 1)
                        s.copy(ssb, psb[:, :]).then_inc(s5s, 1)
                        s.wait_ge(s5v, k + 1)
                        s.activation(g_sb[:, ti, 512 * dh:512 * dh + 512], tsb,
                                     AF.Silu).then_inc(s5e, 1)
                        k += 1

            @block.vector
            def _(v):
                k = 0
                for ti in range(NT):
                    for dh in range(2):
                        peb = (pe0, pe1)[k % 2]
                        ssb = (s_sb0, s_sb1)[k % 2]
                        tsb = (t_sb0, t_sb1)[k % 2]
                        v.wait_ge(s5s, k + 1)
                        v.scalar_tensor_tensor(tsb, peb[:, :], fac_sb[:, ti:ti + 1],
                                               ssb, ALU.mult, ALU.add).then_inc(s5v, 1)
                        k += 1

        with nc.Block() as block:
            @block.sync
            def _(sync):
                sync.dma_start(out=xnfm_out.ap().rearrange("(c p) t -> p c t", p=128),
                               in_=xn_fm[:, :, :]).then_inc(st, 16)
                sync.dma_start(out=g_out.ap().rearrange("(T p) d -> p T d", p=128),
                               in_=(xn_tm if dbg_xn_tm else g_sb)[:, :, :]).then_inc(st, 16)
                sync.wait_ge(st, 32)
    return nc


def _prep_A_inputs(inputs):
    x = np.asarray(inputs["L0_attn_x"], np.float32).reshape(TTOK, D)
    wst = np.ascontiguousarray(np.asarray(inputs["L0_ffn_W_shared"], np.float32).T)
    wet = np.ascontiguousarray(np.asarray(inputs["L0_ffn_W_expert"], np.float32).T)
    wgate = np.ascontiguousarray(np.asarray(inputs["L0_ffn_W_gate"], np.float32))
    bias = np.ascontiguousarray(np.asarray(inputs["L0_ffn_bias"], np.float32)[None, :])
    ones1 = np.ones((1, 128), np.float32)
    ident = np.eye(128, dtype=np.float32)
    maps = []
    for c in range(NCORES):
        maps.append(dict(x=np.ascontiguousarray(x[TCORE * c:TCORE * (c + 1)]),
                         wgate=wgate, wst=wst, wet=wet, bias=bias,
                         ones1=ones1, ident=ident))
    return maps





SEQB = 8192
NTB = SEQB // 128
NGR = NTB // 4


def _plan_B():
    p = dict(st={}, dg={}, pv={}, t={}, su={}, pr={}, qke={}, ve={},
             et_t={}, steps={})
    nSt = nDg = nPv = nT = nSu = nPr = nQKe = nVe = nEtT = 0
    for pair in range(4):
        for part in range(2):
            for gq in range(16):
                nQKe += 1
                p["qke"][(pair, part, gq)] = nQKe
        for vt in range(NTB):
            nVe += 1
            p["ve"][(pair, vt)] = nVe
        for g in range(NGR - 1, -1, -1):
            nT += 1
            p["t"][(pair, g)] = nT
            nSu += 1
            p["su"][(pair, g)] = nSu
            for j in range(0, 4 * g + 4):
                nSt += 1
                p["st"][(pair, g, j)] = nSt
                diag = j >= 4 * g
                tcons = any(4 * g + r > j for r in range(4))
                if tcons:
                    nEtT += 1
                    p["et_t"][(pair, g, j)] = nEtT
                if diag:
                    r = j - 4 * g
                    nDg += 1
                    p["dg"][(pair, g, r)] = nDg
                    nPv += 1
                    p["pv"][(pair, g, r)] = nPv
                p["steps"][nSt] = (pair, g, j, diag, tcons)
            for r in range(4):
                nPr += 1
                p["pr"][(pair, g, r)] = nPr
    p["nSt"] = nSt
    return p


def build_launch_B():
    nc = bass.Bass("TRN2", target_bir_lowering=False, debug=False,
                   detect_race_conditions=False)
    xh_in = nc.dram_tensor("xh", [128, TTOK], BF16, kind="ExternalInput")
    wqkv_in = nc.dram_tensor("wqkv", [128, 192], F32, kind="ExternalInput")
    bqkv_in = nc.dram_tensor("bqkvc", [64, 3], F32, kind="ExternalInput")
    bv_in = nc.dram_tensor("bvrow", [1, 64], F32, kind="ExternalInput")
    ones1_in = nc.dram_tensor("ones1", [1, 128], F32, kind="ExternalInput")
    onec_in = nc.dram_tensor("onec", [128, 1], F32, kind="ExternalInput")
    msl_in = nc.dram_tensor("msl", [128, 128], mybir.dt.uint8, kind="ExternalInput")
    onet_in = nc.dram_tensor("onet", [128, 128], F32, kind="ExternalInput")
    s_out = nc.dram_tensor("s", [TTOK, 128], BF16, kind="ExternalOutput")


    emit_B(nc, "B_", xh_in, wqkv_in, bqkv_in, bv_in, ones1_in, onec_in, msl_in, onet_in, s_out)
    return nc


def emit_B(nc, PFX, xh_in, wqkv_in, bqkv_in, bv_in, ones1_in, onec_in, msl_in, onet_in, s_out,
           xng=None, wqkvL_in=None):
    Pl = _plan_B()
    gathered = xng is not None
    with Phase(nc, PFX) as ph:
        if gathered:
            xb_sb0 = ph.sbuf("xb_sb0", [128, NC_D, 512], BF16)
            xb_sb1 = ph.sbuf("xb_sb1", [128, NC_D, 512], BF16)
            wL_sb = ph.sbuf("wL_sb", [128, NC_D, 384], BF16)
            ld_b = ph.sem("ld_b")
        else:
            xh_sb = ph.sbuf("xh_sb", [128, 2, SEQB], BF16)
        qk_sb = ph.sbuf("qk_sb", [64, 2 * SEQB], BF16)
        v1_sb = ph.sbuf("v1_sb", [128, NTB, 65], BF16)
        et0_sb = ph.sbuf("et0_sb", [128, 512], BF16)
        et1_sb = ph.sbuf("et1_sb", [128, 512], BF16)
        etd_sb = ph.sbuf("etd_sb", [128, 128], BF16)
        s_sb = ph.sbuf("s_sb", [128, 128, 128], BF16)
        wqkv_sb = ph.sbuf("wqkv_sb", [128, 192], BF16)
        bqkv_sb = ph.sbuf("bqkv_sb", [64, 3], F32)
        bv_sb = ph.sbuf("bv_sb", [1, 64], BF16)
        ones1_sb = ph.sbuf("ones1_sb", [1, 128], BF16)
        ones1f_sb = ph.sbuf("ones1f_sb", [1, 128], F32)
        onec_sb = ph.sbuf("onec_sb", [128, 1], BF16)
        msl_sb = ph.sbuf("msl_sb", [128, 128], mybir.dt.uint8)
        onet_sb = ph.sbuf("onet_sb", [128, 128], BF16)
        suf_sb = ph.sbuf("suf_sb", [1, 8, 65], F32)
        trow_sb = ph.sbuf("trow_sb", [1, 260], F32)
        rz_sb = ph.sbuf("rz_sb", [128, 8], F32)
        sdum_sb = ph.sbuf("sdumB_sb", [128, 2], F32)

        pst = [ph.psum("pst%d" % i, [128, 512], F32) for i in range(3)]
        pst0, pst1 = pst[0], pst[1]
        pp = [ph.psum("pp%d" % i, [128, 512], F32) for i in range(4)]
        pT = ph.psum("pT", [128, 512], F32)

        ld = ph.sem("ld")
        ldx = ph.sem("ldx")
        sQKe = ph.sem("sQKe")
        sVe = ph.sem("sVe")
        sPj = ph.sem("sPj")
        sSt = ph.sem("sSt")
        sEx = ph.sem("sEx")
        sEt = ph.sem("sEt")
        sDg = ph.sem("sDg")
        sPv = ph.sem("sPv")
        sSu = ph.sem("sSu")
        sT = ph.sem("sT")
        sPr = ph.sem("sPr")
        sVx = ph.sem("sVxB")
        st = ph.sem("st")

        with nc.Block() as block:
            @block.sync
            def _(sync):
                if not gathered:
                    sync.dma_start(out=xh_sb[:, 0, :], in_=xh_in[:, 0:SEQB]).then_inc(ldx, 16)
                    sync.dma_start(out=xh_sb[:, 1, :], in_=xh_in[:, SEQB:TTOK]).then_inc(ldx, 16)

            @block.gpsimd
            def _(g):
                if gathered:
                    g.dma_start(out=wL_sb[:, :, :],
                                in_=wqkvL_in.ap().rearrange("(c p) d -> p c d", p=128)).then_inc(ld, 16)
                else:
                    g.dma_start(out=wqkv_sb[:, :], in_=wqkv_in[:, :]).then_inc(ld, 16)
                g.dma_start(out=bqkv_sb[:, :], in_=bqkv_in[:, :]).then_inc(ld, 16)
                g.dma_start(out=bv_sb[:, :], in_=bv_in[:, :]).then_inc(ld, 16)
                g.dma_start(out=ones1_sb[:, :], in_=ones1_in[:, :]).then_inc(ld, 16)
                g.dma_start(out=ones1f_sb[:, :], in_=ones1_in[:, :]).then_inc(ld, 16)
                g.dma_start(out=onec_sb[:, :], in_=onec_in[:, :]).then_inc(ld, 16)
                g.dma_start(out=msl_sb[:, :], in_=msl_in[:, :]).then_inc(ld, 16)
                g.dma_start(out=onet_sb[:, :], in_=onet_in[:, :]).then_inc(ld, 16)

        vsync_state = [0]
        for hl in range(2):
            for b in range(2):
                pair = 2 * hl + b
                hrow = 64 * hl

                with nc.Block() as block:
                    if gathered:
                        @block.sync
                        def _(sync, hl=hl, b=b, pair=pair):
                            for blk in range(16):
                                n = 16 * pair + blk
                                r = 4 * b + blk // 4
                                co = 512 * (blk % 4)
                                xb = (xb_sb0, xb_sb1)[n % 2]
                                if n >= 2:
                                    pm2 = (n - 2) // 16
                                    bm2 = (n - 2) % 16
                                    sync.wait_ge(sPj, 96 * pm2 + 6 * (bm2 + 1))
                                sync.dma_start(
                                    out=xb[:, :, :],
                                    in_=xng[1024 * r:1024 * (r + 1),
                                            co:co + 512].rearrange(
                                                "(c p) t -> p c t", p=128)
                                ).then_inc(ld_b, 16)

                        @block.tensor
                        def _(t, hl=hl, b=b, pair=pair):
                            if pair == 0:
                                t.wait_ge(ld, 128)
                            k = 0
                            for blk in range(16):
                                n = 16 * pair + blk
                                xb = (xb_sb0, xb_sb1)[n % 2]
                                t.wait_ge(ld_b, 16 * (n + 1))
                                for wc in (0, 128):
                                    pb = (pst0, pst1)[k % 2]
                                    if k >= 2 or pair > 0:
                                        t.wait_ge(sQKe, 32 * pair
                                                  + (k - 1 if k >= 2 else 0))
                                    inst = None
                                    for ci in range(NC_D):
                                        inst = t.matmul(
                                            pb[0:64, :],
                                            wL_sb[:, ci,
                                                  wc + 64 * hl:wc + 64 * hl + 64],
                                            xb[:, ci, :],
                                            start=(ci == 0), stop=(ci == NC_D - 1))
                                    inst.then_inc(sPj, 1)
                                    k += 1
                                for tt in range(4):
                                    vt = 4 * blk + tt
                                    pb = (pp[0], pp[1])[vt % 2]
                                    if vt >= 2:
                                        t.wait_ge(sVe, Pl["ve"][(pair, vt - 2)])
                                    for ci in range(NC_D):
                                        t.matmul(pb[:, 0:64],
                                                 xb[:, ci, 128 * tt:128 * tt + 128],
                                                 wL_sb[:, ci, 256 + 64 * hl:
                                                       320 + 64 * hl],
                                                 start=(ci == 0), stop=False)
                                    inst = t.matmul(pb[:, 0:64], ones1_sb[:, :],
                                                    bv_sb[:, :], start=False,
                                                    stop=True)
                                    inst.then_inc(sPj, 1)

                        @block.scalar
                        def _(s, pair=pair):
                            base = pair * 96
                            for blk in range(16):
                                for half, bcol in ((0, 0), (1, 1)):
                                    k = 2 * blk + half
                                    pb = (pst0, pst1)[k % 2]
                                    s.wait_ge(sPj, base + 6 * blk + half + 1)
                                    s.activation(
                                        qk_sb[0:64, SEQB * half + 512 * blk:
                                              SEQB * half + 512 * blk + 512],
                                        pb[0:64, :], AF.Identity,
                                        bias=bqkv_sb[:, bcol:bcol + 1]
                                    ).then_inc(sQKe, 1)

                        @block.vector
                        def _(v, pair=pair):
                            base = pair * 96
                            v.memset(v1_sb[:, :, 64:65], 1.0)
                            for blk in range(16):
                                for tt in range(4):
                                    vt = 4 * blk + tt
                                    pb = (pp[0], pp[1])[vt % 2]
                                    v.wait_ge(sPj, base + 6 * blk + 2 + tt + 1)
                                    v.tensor_copy(v1_sb[:, vt, 0:64],
                                                  pb[:, 0:64]).then_inc(sVe, 1)
                    else:
                        @block.tensor
                        def _(t, hl=hl, b=b, pair=pair, hrow=hrow):
                            if pair == 0:
                                t.wait_ge(ld, 128)
                                t.wait_ge(ldx, 32)
                            k = 0
                            for part, wc, _bc in ((0, 0, 0), (64, 64, 1)):
                                for gq in range(16):
                                    pb = (pst0, pst1)[k % 2]
                                    if k >= 2:
                                        t.wait_ge(sQKe, Pl["qke"][(pair, (k - 2) // 16,
                                                                  (k - 2) % 16)])
                                    inst = t.matmul(
                                        pb[0:64, :],
                                        wqkv_sb[hrow:hrow + 64, wc:wc + 64],
                                        xh_sb[hrow:hrow + 64, b,
                                              512 * gq:512 * gq + 512],
                                        start=True, stop=True)
                                    inst.then_inc(sPj, 1)
                                    k += 1
                            for vt in range(NTB):
                                pb = (pp[0], pp[1])[vt % 2]
                                if vt >= 2:
                                    t.wait_ge(sVe, Pl["ve"][(pair, vt - 2)])
                                t.matmul(pb[:, 0:64],
                                         xh_sb[hrow:hrow + 64, b,
                                               128 * vt:128 * vt + 128],
                                         wqkv_sb[hrow:hrow + 64, 128:192],
                                         start=True, stop=False)
                                inst = t.matmul(pb[:, 0:64], ones1_sb[:, :],
                                                bv_sb[:, :],
                                                start=False, stop=True)
                                inst.then_inc(sPj, 1)

                        @block.scalar
                        def _(s, pair=pair):
                            base = pair * 96
                            for half, bcol in ((0, 0), (1, 1)):
                                for gq in range(16):
                                    k = half * 16 + gq
                                    pb = (pst0, pst1)[k % 2]
                                    s.wait_ge(sPj, base + k + 1)
                                    s.activation(qk_sb[0:64, SEQB * half + 512 * gq:
                                                       SEQB * half + 512 * gq + 512],
                                                 pb[0:64, :], AF.Identity,
                                                 bias=bqkv_sb[:, bcol:bcol + 1]
                                                 ).then_inc(sQKe, 1)

                        @block.vector
                        def _(v, pair=pair):
                            base = pair * 96
                            v.memset(v1_sb[:, :, 64:65], 1.0)
                            for vt in range(NTB):
                                pb = (pp[0], pp[1])[vt % 2]
                                v.wait_ge(sPj, base + 32 + vt + 1)
                                v.tensor_copy(v1_sb[:, vt, 0:64],
                                              pb[:, 0:64]).then_inc(sVe, 1)

                with nc.Block() as block:
                    @block.tensor
                    def _(t, hl=hl, b=b, pair=pair):
                        def emit_st(g, j):
                            pb = pst[j % 3]
                            me = Pl["st"][(pair, g, j)]
                            if me >= 4:
                                t.wait_ge(sEx, me - 3)
                            t.matmul(
                                pb[:, :],
                                qk_sb[0:64, SEQB + 128 * j:SEQB + 128 * j + 128],
                                qk_sb[0:64, 512 * g:512 * g + 512],
                                start=True, stop=True).then_inc(sSt, 1)

                        for g in range(NGR - 1, -1, -1):
                            inst = None
                            for r, j in enumerate(range(4 * g + 3, 4 * g - 1, -1)):
                                inst = t.matmul(pT[0:1, 65 * r:65 * r + 65],
                                                onec_sb[:, :], v1_sb[:, j, :],
                                                start=True, stop=True)
                            inst.then_inc(sT, 1)
                            if g < NGR - 1:
                                t.wait_ge(sPr, Pl["pr"][(pair, g + 1, 3)])
                            t.wait_ge(sSu, Pl["su"][(pair, g)])
                            for r in range(4):
                                srow = suf_sb[0:1, (4, 3, 2) [r] if r < 3 else 5, :]
                                t.matmul(pp[r][:, 0:65], ones1f_sb[:, :], srow,
                                         start=True, stop=False)
                            emit_st(g, 0)
                            for j in range(0, 4 * g + 4):
                                if j + 1 <= 4 * g + 3:
                                    emit_st(g, j + 1)
                                me = Pl["st"][(pair, g, j)]
                                t.wait_ge(sEx, me)
                                et = (et0_sb, et1_sb)[j % 2]
                                last = None
                                for r in range(4):
                                    if 4 * g + r > j:
                                        last = t.matmul(pp[r][:, 0:65],
                                                        et[:, 128 * r:128 * r + 128],
                                                        v1_sb[:, j, :],
                                                        start=False, stop=False)
                                diag = j >= 4 * g
                                if (pair, g, j) in Pl["et_t"] and not diag:
                                    last.then_inc(sEt, 1)
                                if diag:
                                    r = j - 4 * g
                                    if last is not None:
                                        last.then_inc(sEt, 1)
                                    t.wait_ge(sDg, Pl["dg"][(pair, g, r)])
                                    t.matmul(pp[r][:, 0:65], etd_sb[:, :],
                                             v1_sb[:, j, :], start=False,
                                             stop=True).then_inc(sPv, 1)

                    @block.scalar
                    def _(s, pair=pair):
                        for g in range(NGR - 1, -1, -1):
                            for j in range(0, 4 * g + 4):
                                me = Pl["st"][(pair, g, j)]
                                s.wait_ge(sSt, me)
                                if me >= 3:
                                    opair, og, oj, odiag, otcons = Pl["steps"][me - 2]
                                    if otcons:
                                        s.wait_ge(sEt, Pl["et_t"][(opair, og, oj)])
                                    if odiag:
                                        s.wait_ge(sDg, Pl["dg"][(opair, og,
                                                                oj - 4 * og)])
                                pb = pst[j % 3]
                                s.activation((et0_sb, et1_sb)[j % 2][:, :], pb[:, :],
                                             AF.Exp, scale=0.125).then_inc(sEx, 1)

                    @block.vector
                    def _(v, hl=hl, b=b, pair=pair):
                        ssync = SelfSync(v, sVx, dummy=sdum_sb[:, 0:1])
                        ssync.n = vsync_state[0]
                        run = 0
                        if pair == 0:
                            v.memset(suf_sb[0:1, 0, :], 0.0)
                        else:
                            v.memset(suf_sb[0:1, run, :], 0.0)
                        for g in range(NGR - 1, -1, -1):
                            v.wait_ge(sT, Pl["t"][(pair, g)])
                            inst = v.tensor_copy(trow_sb[0:1, :], pT[0:1, 0:260])
                            ssync.flush(inst)
                            inst = v.tensor_add(suf_sb[0:1, 2, :],
                                                suf_sb[0:1, run, :],
                                                trow_sb[0:1, 0:65])
                            ssync.flush(inst)
                            inst = v.tensor_add(suf_sb[0:1, 3, :], suf_sb[0:1, 2, :],
                                                trow_sb[0:1, 65:130])
                            ssync.flush(inst)
                            inst = v.tensor_add(suf_sb[0:1, 4, :], suf_sb[0:1, 3, :],
                                                trow_sb[0:1, 130:195])
                            ssync.flush(inst)
                            v.tensor_copy(suf_sb[0:1, 5, :], suf_sb[0:1, run, :])
                            inst = v.tensor_add(suf_sb[0:1, 1 - run, :],
                                                suf_sb[0:1, 4, :],
                                                trow_sb[0:1, 195:260])
                            inst.then_inc(sSu, 1)
                            run = 1 - run
                            for j in range(4 * g, 4 * g + 4):
                                r = j - 4 * g
                                et = (et0_sb, et1_sb)[j % 2]
                                v.wait_ge(sEx, Pl["st"][(pair, g, j)])
                                v.select(etd_sb[:, :], msl_sb[:, :],
                                         et[:, 128 * r:128 * r + 128],
                                         onet_sb[:, :]).then_inc(sDg, 1)
                                v.wait_ge(sPv, Pl["pv"][(pair, g, r)])
                                inst = v.reciprocal(rz_sb[:, r:r + 1],
                                                    pp[r][:, 64:65])
                                ssync.flush(inst)
                                gti = NTB * b + 4 * g + r
                                v.tensor_scalar(s_sb[:, gti, 64 * hl:64 * hl + 64],
                                                pp[r][:, 0:64], rz_sb[:, r:r + 1],
                                                None, ALU.mult).then_inc(sPr, 1)
                        vsync_state[0] = ssync.n

        with nc.Block() as block:
            @block.sync
            def _(sync):
                sync.dma_start(out=s_out.ap().rearrange("(T p) c -> p T c", p=128),
                               in_=s_sb[:, :, :]).then_inc(st, 16)
                sync.wait_ge(st, 16)
    return nc


def _prep_B_inputs(xnfm_by_core, inputs):
    xn_full = np.concatenate(xnfm_by_core, axis=1)
    w1 = np.asarray(inputs["L0_attn_Wqkv"], np.float32)
    wqkv = np.ascontiguousarray(np.concatenate([w1, w1], axis=0))
    bq = np.asarray(inputs["L0_attn_bqkv"], np.float32)
    bqkvc = np.stack([bq[0:64], bq[64:128], bq[128:192]], axis=1)
    bvrow = np.ascontiguousarray(bq[None, 128:192])
    ones1 = np.ones((1, 128), np.float32)
    onec = np.ones((128, 1), np.float32)
    jj, ii = np.meshgrid(np.arange(128), np.arange(128), indexing="ij")
    msl = (jj < ii).astype(np.uint8)
    onet = np.ones((128, 128), np.float32)
    maps = []
    for c in range(NCORES):
        maps.append(dict(xh=np.ascontiguousarray(xn_full[128 * c:128 * (c + 1)]),
                         wqkv=wqkv, bqkvc=np.ascontiguousarray(bqkvc),
                         bvrow=bvrow, ones1=ones1, onec=onec, msl=msl, onet=onet))
    return maps


def build_launch_C1():
    nc = bass.Bass("TRN2", target_bir_lowering=False, debug=False,
                   detect_race_conditions=False)
    x_in = nc.dram_tensor("x", [TCORE, D], F32, kind="ExternalInput")
    sfm_in = nc.dram_tensor("sfm", [D, TCORE], BF16, kind="ExternalInput")
    g_in = nc.dram_tensor("g", [TCORE, D], BF16, kind="ExternalInput")
    wr_in = nc.dram_tensor("wr", [D, D], F32, kind="ExternalInput")
    br_in = nc.dram_tensor("br", [1, D], F32, kind="ExternalInput")
    ones1_in = nc.dram_tensor("ones1", [1, 128], F32, kind="ExternalInput")
    ident_in = nc.dram_tensor("ident", [128, 128], F32, kind="ExternalInput")
    x1_out = nc.dram_tensor("x1", [TCORE, D], F32, kind="ExternalOutput")
    xn1fm_out = nc.dram_tensor("xn1_fm", [D, TCORE], BF16, kind="ExternalOutput")

    emit_C1(nc, "C1_", x_in, sfm_in, g_in, wr_in, br_in, ones1_in, ident_in, x1_out, xn1fm_out)
    return nc


def emit_C1(nc, PFX, x_in, sfm_in, g_in, wr_in, br_in, ones1_in, ident_in, x1_out, xn1fm_out, s_t_in=None):
    with Phase(nc, PFX) as ph:
        x1_sb = ph.sbuf("x1_sb", [128, NT, D], F32)
        sfm_sb = ph.sbuf("sfm_sb", [128, NC_D, TCORE], BF16)
        g_sb = ph.sbuf("g_sb", [128, NT, D], BF16)
        xn_tm = ph.sbuf("xn_tm", [128, NT, D], BF16)
        wr_sb = ph.sbuf("wr_sb", [128, NC_D, D], BF16)
        br_sb = ph.sbuf("br_sb", [1, D], BF16)
        ones1_sb = ph.sbuf("ones1_sb", [1, 128], BF16)
        ident_sb = ph.sbuf("ident_sb", [128, 128], BF16)
        eps_sb = ph.sbuf("eps_sb", [128, 1], F32)
        sdum_sb = ph.sbuf("sdum_sb", [128, 2], F32)
        sq0_sb = ph.sbuf("sq0_sb", [128, D], F32)
        sq1_sb = ph.sbuf("sq1_sb", [128, D], F32)
        sums_sb = ph.sbuf("sums_sb", [128, NT], F32)
        sumsq_sb = ph.sbuf("sumsq_sb", [128, NT], F32)
        mean_sb = ph.sbuf("mean_sb", [128, NT], F32)
        var_sb = ph.sbuf("var_sb", [128, NT], F32)
        w0_sb = ph.sbuf("w0_sb", [128, NT], F32)
        w1_sb = ph.sbuf("w1_sb", [128, NT], F32)
        std_sb = ph.sbuf("std_sb", [128, NT], F32)
        rstd_sb = ph.sbuf("rstd_sb", [128, NT], F32)
        xt_sb0 = ph.sbuf("xt_sb0", [128, 512], F32)
        xt_sb1 = ph.sbuf("xt_sb1", [128, 512], F32)
        rg_sb0 = ph.sbuf("rg_sb0", [128, 512], F32)
        rg_sb1 = ph.sbuf("rg_sb1", [128, 512], F32)
        fmst_sb0 = ph.sbuf("fmst_sb0", [128, 512], BF16)
        fmst_sb1 = ph.sbuf("fmst_sb1", [128, 512], BF16)

        pr0 = ph.psum("pr0", [128, 512], F32)
        pr1 = ph.psum("pr1", [128, 512], F32)
        pt0 = ph.psum("pt0", [128, 1024], BF16)
        pt1 = ph.psum("pt1", [128, 1024], BF16)

        ld_x = ph.sem("ld_x")
        ld_w = ph.sem("ld_w")
        ld_t = ph.sem("ld_t")
        sR = ph.sem("sR")
        sC = ph.sem("sC")
        sSq = ph.sem("sSq")
        sVc = ph.sem("sVc")
        sVx = ph.sem("sVx")
        s2a = ph.sem("s2a")
        s2b = ph.sem("s2b")
        s3t = ph.sem("s3t")
        s3e = ph.sem("s3e")
        sFm = ph.sem("sFm")
        st = ph.sem("st")

        with nc.Block() as block:
            @block.sync
            def _(sync):
                if s_t_in is not None:
                    for i in range(8):
                        sync.dma_start(out=sfm_sb[:, i, :],
                                       in_=s_t_in[TCORE * i:TCORE * (i + 1), :],
                                       transpose=True).then_inc(ld_x, 16)
                else:
                    sync.dma_start(out=sfm_sb[:, :, :],
                                   in_=sfm_in.ap().rearrange("(c p) t -> p c t", p=128)).then_inc(ld_x, 16)
                sync.dma_start(out=g_sb[:, :, :],
                               in_=g_in.ap().rearrange("(T p) d -> p T d", p=128)).then_inc(ld_x, 16)

            @block.gpsimd
            def _(g):
                g.dma_start(out=wr_sb[:, :, :],
                            in_=wr_in.ap().rearrange("(c p) d -> p c d", p=128)).then_inc(ld_w, 16)
                g.dma_start(out=br_sb[:, :], in_=br_in[:, :]).then_inc(ld_w, 16)
                g.dma_start(out=ones1_sb[:, :], in_=ones1_in[:, :]).then_inc(ld_w, 16)
                g.dma_start(out=ident_sb[:, :], in_=ident_in[:, :]).then_inc(ld_w, 16)

        xr = x_in.ap().rearrange("(T p) d -> p T d", p=128)
        with nc.Block() as block:
            @block.sync
            def _(sync):
                k = 0
                for ti in range(NT):
                    for dh in range(2):
                        xtb = (xt_sb0, xt_sb1)[k % 2]
                        if k >= 2:
                            sync.wait_ge(sC, k - 1)
                        sync.dma_start(out=xtb[:, :],
                                       in_=xr[:, ti, 512 * dh:512 * dh + 512]).then_inc(ld_t, 16)
                        k += 1

            @block.tensor
            def _(t):
                t.wait_ge(ld_x, 144 if s_t_in is not None else 32)
                t.wait_ge(ld_w, 64)
                k = 0
                for ti in range(NT):
                    for dh in range(2):
                        pb = (pr0, pr1)[k % 2]
                        if k >= 2:
                            t.wait_ge(sC, k - 1)
                        for ci in range(NC_D):
                            t.matmul(pb[:, :], sfm_sb[:, ci, 128 * ti:128 * ti + 128],
                                     wr_sb[:, ci, 512 * dh:512 * dh + 512],
                                     start=(ci == 0), stop=False)
                        inst = t.matmul(pb[:, :], ones1_sb[:, :],
                                        br_sb[:, 512 * dh:512 * dh + 512],
                                        start=False, stop=True)
                        inst.then_inc(sR, 1)
                        k += 1

            @block.vector
            def _(v):
                k = 0
                for ti in range(NT):
                    for dh in range(2):
                        pb = (pr0, pr1)[k % 2]
                        rgb = (rg_sb0, rg_sb1)[k % 2]
                        xtb = (xt_sb0, xt_sb1)[k % 2]
                        v.wait_ge(sR, k + 1)
                        v.wait_ge(ld_t, 16 * (k + 1))
                        v.tensor_mul(rgb[:, :], pb[:, :],
                                     g_sb[:, ti, 512 * dh:512 * dh + 512])
                        v.tensor_add(x1_sb[:, ti, 512 * dh:512 * dh + 512],
                                     xtb[:, :], rgb[:, :]).then_inc(sC, 1)
                        k += 1

        with nc.Block() as block:
            @block.vector
            def _(v):
                ssync = SelfSync(v, sVx, dummy=sdum_sb[:, 0:1])
                v.memset(eps_sb[:, :], 1e-5)
                inst = _ln_stats(v, None, x1_sb, sq0_sb, sq1_sb, sums_sb, sumsq_sb,
                                 mean_sb, var_sb, w0_sb, w1_sb, eps_sb, sSq, sVc, ssync)
                inst.then_inc(s2a, 1)
                v.wait_ge(s2b, 1)
                inst = v.reciprocal(rstd_sb[:, :], std_sb[:, :])
                ssync.flush(inst)
                for ti in range(NT):
                    v.tensor_scalar(xn_tm[:, ti, :], x1_sb[:, ti, :],
                                    mean_sb[:, ti:ti + 1], rstd_sb[:, ti:ti + 1],
                                    ALU.subtract, ALU.mult)

            @block.scalar
            def _(s):
                _ln_stats_scalar(s, x1_sb, sq0_sb, sq1_sb, sSq, sVc)
                s.wait_ge(s2a, 1)
                s.activation(std_sb[:, :], var_sb[:, :], AF.Sqrt,
                             bias=eps_sb[:, 0:1]).then_inc(s2b, 1)

        with nc.Block() as block:
            @block.tensor
            def _(t):
                k = 0
                for ci in range(NC_D):
                    for tg in range(NT // 4):
                        pb = (pt0, pt1)[k % 2]
                        if k >= 2:
                            t.wait_ge(s3e, k - 1)
                        inst = None
                        for j in range(4):
                            ti = 4 * tg + j
                            inst = t.transpose(pb[:, 128 * j:128 * j + 128],
                                               xn_tm[:, ti, 128 * ci:128 * ci + 128],
                                               ident_sb[:, :])
                        inst.then_inc(s3t, 1)
                        k += 1

            @block.scalar
            def _(s):
                k = 0
                for ci in range(NC_D):
                    for tg in range(NT // 4):
                        pb = (pt0, pt1)[k % 2]
                        fst = (fmst_sb0, fmst_sb1)[k % 2]
                        s.wait_ge(s3t, k + 1)
                        if k >= 2:
                            s.wait_ge(sFm, 16 * (k - 1))
                        s.copy(fst[:, :], pb[:, 0:512]).then_inc(s3e, 1)
                        k += 1

            @block.sync
            def _(sync):
                k = 0
                for ci in range(NC_D):
                    for tg in range(NT // 4):
                        fst = (fmst_sb0, fmst_sb1)[k % 2]
                        sync.wait_ge(s3e, k + 1)
                        sync.dma_start(out=xn1fm_out[128 * ci:128 * ci + 128,
                                                     512 * tg:512 * tg + 512],
                                       in_=fst[:, :]).then_inc(sFm, 16)
                        k += 1

        with nc.Block() as block:
            @block.sync
            def _(sync):
                sync.dma_start(out=x1_out.ap().rearrange("(T p) d -> p T d", p=128),
                               in_=x1_sb[:, :, :]).then_inc(st, 16)
                sync.wait_ge(st, 16)
    return nc


LN16 = 2.772588722239781


def build_launch_C2():
    nc = bass.Bass("TRN2", target_bir_lowering=False, debug=False,
                   detect_race_conditions=False)
    xn1_in = nc.dram_tensor("xn1f", [D, TTOK], BF16, kind="ExternalInput")
    wk_in = nc.dram_tensor("wk2h", [D, 128], F32, kind="ExternalInput")
    wv_in = nc.dram_tensor("wv2h", [D, 128], F32, kind="ExternalInput")
    wt_in = nc.dram_tensor("wT", [128, 256], F32, kind="ExternalInput")
    onec_in = nc.dram_tensor("onec", [128, 1], F32, kind="ExternalInput")
    kv_out = nc.dram_tensor("kv4", [1024, 65], F32, kind="ExternalOutput")

    NSTEP = 32

    emit_C2(nc, "C2_", xn1_in, wk_in, wv_in, wt_in, onec_in, kv_out)
    return nc


def emit_C2(nc, PFX, xn1_in, wk_in, wv_in, wt_in, onec_in, kv_out, gathered=False, dup_out=False):
    NSTEP = 32
    with Phase(nc, PFX) as ph:
        xb_sb0 = ph.sbuf("xb_sb0", [128, NC_D, 512], BF16)
        xb_sb1 = ph.sbuf("xb_sb1", [128, NC_D, 512], BF16)
        wk_sb = ph.sbuf("wk_sb", [128, NC_D, 128], BF16)
        wv_sb = ph.sbuf("wv_sb", [128, NC_D, 128], BF16)
        wt_sb = ph.sbuf("wt_sb", [128, 256], BF16)
        onec_sb = ph.sbuf("onec_sb", [128, 1], BF16)
        onecf_sb = ph.sbuf("onecf_sb", [128, 1], F32)
        kf_sb = ph.sbuf("kf_sb", [64, 512], BF16)
        sqk_sb = ph.sbuf("sqk_sb", [64, 512], F32)
        nk_sb = ph.sbuf("nk_sb", [128, 4], F32)
        mln_sb = ph.sbuf("mln_sb", [128, 1], F32)
        pe_sb = ph.sbuf("pe_sb", [128, 2, 256], F32)
        pk_sb = ph.sbuf("pk_sb", [128, 2, 256], BF16)
        v1g_sb = ph.sbuf("v1g_sb", [128, 2, 65], BF16)
        kvo_sb = ph.sbuf("kvo_sb", [128, 8, 65], F32)

        pkv0 = ph.psum("pkv0", [128, 512], F32)
        pkv1 = ph.psum("pkv1", [128, 512], F32)
        pK = ph.psum("pK", [128, 512], F32)
        pP0 = ph.psum("pP0", [128, 512], F32)
        pP1 = ph.psum("pP1", [128, 512], F32)
        pN = ph.psum("pN", [128, 512], F32)
        pV = ph.psum("pV", [128, 512], F32)

        ld_w = ph.sem("ld_w")
        ld_t = ph.sem("ld_t")
        sK = ph.sem("sK")
        sSq2 = ph.sem("sSq2")
        sNm = ph.sem("sNm")
        sNe = ph.sem("sNe")
        sPk = ph.sem("sPk")
        sCl = ph.sem("sCl")
        sEx = ph.sem("sEx")
        sVm = ph.sem("sVm")
        sVe = ph.sem("sVe")
        sKv = ph.sem("sKv")
        sKvE = ph.sem("sKvE")
        st = ph.sem("st")

        with nc.Block() as block:
            @block.gpsimd
            def _(g):
                g.dma_start(out=wk_sb[:, :, :],
                            in_=wk_in.ap().rearrange("(c p) d -> p c d", p=128)).then_inc(ld_w, 16)
                g.dma_start(out=wv_sb[:, :, :],
                            in_=wv_in.ap().rearrange("(c p) d -> p c d", p=128)).then_inc(ld_w, 16)
                g.dma_start(out=wt_sb[:, :], in_=wt_in[:, :]).then_inc(ld_w, 16)
                g.dma_start(out=onec_sb[:, :], in_=onec_in[:, :]).then_inc(ld_w, 16)
                g.dma_start(out=onecf_sb[:, :], in_=onec_in[:, :]).then_inc(ld_w, 16)

        if not gathered:
            xr = xn1_in.ap().rearrange("(c p) t -> p c t", p=128)
        with nc.Block() as block:
            @block.sync
            def _(sync):
                for n in range(2 * NSTEP):
                    hl, g = divmod(n, NSTEP)
                    xb = (xb_sb0, xb_sb1)[n % 2]
                    if n >= 2:
                        sync.wait_ge(sVm, 4 * (n - 1))
                    if gathered:
                        r, co = g // 4, 512 * (g % 4)
                        ap = xn1_in[1024 * r:1024 * (r + 1),
                                    co:co + 512].rearrange("(c p) t -> p c t", p=128)
                    else:
                        ap = xr[:, :, 512 * g:512 * g + 512]
                    sync.dma_start(out=xb[:, :, :], in_=ap).then_inc(ld_t, 16)

            @block.tensor
            def _(t):
                t.wait_ge(ld_w, 80)
                for n in range(2 * NSTEP):
                    hl, g = divmod(n, NSTEP)
                    b = g // 16
                    xb = (xb_sb0, xb_sb1)[n % 2]
                    t.wait_ge(ld_t, 16 * (n + 1))
                    if n >= 1:
                        t.wait_ge(sSq2, n)
                    inst = None
                    for ci in range(NC_D):
                        inst = t.matmul(pK[0:64, :], wk_sb[:, ci, 64 * hl:64 * hl + 64],
                                        xb[:, ci, :], start=(ci == 0), stop=(ci == NC_D - 1))
                    inst.then_inc(sK, 1)
                    t.wait_ge(sSq2, n + 1)
                    if n >= 1:
                        t.wait_ge(sNe, n)
                    inst = None
                    for tt in range(4):
                        inst = t.matmul(pN[:, tt:tt + 1],
                                        sqk_sb[:, 128 * tt:128 * tt + 128],
                                        onecf_sb[0:64, :], start=True, stop=True)
                    inst.then_inc(sNm, 1)
                    for tt in range(4):
                        pp = (pP0, pP1)[tt % 2]
                        if 4 * n + tt >= 2:
                            t.wait_ge(sCl, 4 * n + tt - 1)
                        inst = t.matmul(pp[:, 0:256],
                                        kf_sb[:, 128 * tt:128 * tt + 128],
                                        wt_sb[0:64, :], start=True, stop=True)
                        inst.then_inc(sPk, 1)
                        if 4 * n + tt >= 1:
                            t.wait_ge(sVe, 4 * n + tt)
                        inst = None
                        for ci in range(NC_D):
                            inst = t.matmul(pV[:, 0:64], xb[:, ci, 128 * tt:128 * tt + 128],
                                            wv_sb[:, ci, 64 * hl:64 * hl + 64],
                                            start=(ci == 0), stop=(ci == NC_D - 1))
                        inst.then_inc(sVm, 1)
                        t.wait_ge(sEx, 4 * n + tt + 1)
                        t.wait_ge(sVe, 4 * n + tt + 1)
                        first = (g % 16 == 0 and tt == 0)
                        lastt = (g % 16 == 15 and tt == 3)
                        if first and n > 0:
                            t.wait_ge(sKvE, 2 * (n // 16))
                        for mc in range(2):
                            inst = t.matmul((pkv0, pkv1)[mc][:, 0:65],
                                            pk_sb[:, tt % 2, 128 * mc:128 * mc + 128],
                                            v1g_sb[:, tt % 2, :],
                                            start=first, stop=lastt)
                            inst.then_inc(sKv, 1)

            @block.scalar
            def _(s):
                for n in range(2 * NSTEP):
                    s.wait_ge(sK, n + 1)
                    if n >= 1:
                        s.wait_ge(sPk, 4 * n)
                        s.wait_ge(sNm, n)
                    s.copy(kf_sb[:, :], pK[0:64, :])
                    s.activation(sqk_sb[:, :], kf_sb[:, :], AF.Square).then_inc(sSq2, 1)
                    s.wait_ge(sNm, n + 1)
                    if n >= 1:
                        s.wait_ge(sCl, 4 * n)
                    s.activation(nk_sb[:, 0:4], pN[:, 0:4], AF.Identity,
                                 bias=mln_sb[:, 0:1], scale=-0.5).then_inc(sNe, 1)
                    for tt in range(4):
                        s.wait_ge(sCl, 4 * n + tt + 1)
                        s.activation(pk_sb[:, tt % 2, :], pe_sb[:, tt % 2, :],
                                     AF.Exp).then_inc(sEx, 1)

            @block.vector
            def _(v):
                v.memset(mln_sb[:, :], -LN16)
                v.memset(v1g_sb[:, :, 64:65], 1.0)
                for n in range(2 * NSTEP):
                    hl, g = divmod(n, NSTEP)
                    for tt in range(4):
                        pp = (pP0, pP1)[tt % 2]
                        v.wait_ge(sPk, 4 * n + tt + 1)
                        v.wait_ge(sNe, n + 1)
                        if 4 * n + tt >= 2:
                            v.wait_ge(sEx, 4 * n + tt - 1)
                        v.tensor_scalar(pe_sb[:, tt % 2, :], pp[:, 0:256],
                                        nk_sb[:, tt:tt + 1], -60.0,
                                        ALU.add, ALU.max).then_inc(sCl, 1)
                        v.wait_ge(sVm, 4 * n + tt + 1)
                        if 4 * n + tt >= 2:
                            v.wait_ge(sKv, 2 * (4 * n + tt - 1))
                        v.tensor_copy(v1g_sb[:, tt % 2, 0:64],
                                      pV[:, 0:64]).then_inc(sVe, 1)
                    if g % 16 == 15:
                        bidx = 2 * hl + g // 16
                        v.wait_ge(sKv, 2 * (4 * n + 4))
                        v.tensor_copy(kvo_sb[:, 4 * hl + 2 * (g // 16), :],
                                      pkv0[:, 0:65])
                        v.tensor_copy(kvo_sb[:, 4 * hl + 2 * (g // 16) + 1, :],
                                      pkv1[:, 0:65]).then_inc(sKvE, 2)

        with nc.Block() as block:
            @block.sync
            def _(sync):
                if dup_out:
                    nst = 0
                    kvr = kv_out.ap().rearrange("(R p) c -> p R c", p=128)
                    for j in range(8):
                        b = j // 4
                        for hl in range(2):
                            sync.dma_start(
                                out=kvr[:, 4 * j + 2 * hl:4 * j + 2 * hl + 2, :],
                                in_=kvo_sb[:, 4 * hl + 2 * b:4 * hl + 2 * b + 2, :]
                            ).then_inc(st, 16)
                            nst += 16
                    sync.wait_ge(st, nst)
                else:
                    sync.dma_start(out=kv_out.ap().rearrange("(r p) c -> p r c",
                                                             p=128),
                                   in_=kvo_sb[:, :, :]).then_inc(st, 16)
                    sync.wait_ge(st, 16)
    return nc


def build_launch_C3a():
    nc = bass.Bass("TRN2", target_bir_lowering=False, debug=False,
                   detect_race_conditions=False)
    xn1_in = nc.dram_tensor("xn1fm", [D, TCORE], BF16, kind="ExternalInput")
    kvb_in = nc.dram_tensor("kvb", [4096, 65], F32, kind="ExternalInput")
    wq_in = nc.dram_tensor("wq", [D, D], F32, kind="ExternalInput")
    wt_in = nc.dram_tensor("wT", [128, 256], F32, kind="ExternalInput")
    wrt_in = nc.dram_tensor("wrt", [D, 8], F32, kind="ExternalInput")
    wgt_in = nc.dram_tensor("wgt", [D, 8], F32, kind="ExternalInput")
    onec_in = nc.dram_tensor("onec", [128, 1], F32, kind="ExternalInput")
    ones1_in = nc.dram_tensor("ones1", [1, 128], F32, kind="ExternalInput")
    s1_out = nc.dram_tensor("s1", [TCORE, D], BF16, kind="ExternalOutput")
    cw_out = nc.dram_tensor("cw", [TCORE, 8], F32, kind="ExternalOutput")

    emit_C3a(nc, "C3a_", xn1_in, kvb_in, wq_in, wt_in, wrt_in, wgt_in, onec_in, ones1_in, s1_out, cw_out)
    return nc


def emit_C3a(nc, PFX, xn1_in, kvb_in, wq_in, wt_in, wrt_in, wgt_in, onec_in, ones1_in, s1_out, cw_out, kv_a2a=False):
    with Phase(nc, PFX) as ph:
        xn1_sb = ph.sbuf("xn1_sb", [128, NC_D, TCORE], BF16)
        q_sb = ph.sbuf("q_sb", [128, NC_D, TCORE], BF16)
        s1_sb = ph.sbuf("s1_sb", [128, NT, D], BF16)
        kv_sb = ph.sbuf("kv_sb", [128, 32, 65], BF16)
        wq_sb = ph.sbuf("wq_sb", [128, NC_D, D], BF16)
        wt_sb = ph.sbuf("wt_sb", [128, 256], BF16)
        wrg_sb = ph.sbuf("wrg_sb", [128, NC_D, 16], BF16)
        onecf_sb = ph.sbuf("onecf_sb", [128, 1], F32)
        ones1f_sb = ph.sbuf("ones1f_sb", [1, 128], F32)
        sqq_sb = ph.sbuf("sqq_sb", [128, 2048], F32)
        nq_sb = ph.sbuf("nq_sb", [1, 2048], F32)
        pqe_sb = ph.sbuf("pqe_sb", [128, 2, 512], F32)
        pq_sb = ph.sbuf("pq_sb", [128, 2, 2048], BF16)
        rz_sb = ph.sbuf("rz_sb", [128, NT], F32)
        zc_sb = ph.sbuf("zc_sb", [128, NT], F32)
        lgr_sb = ph.sbuf("lgr_sb", [128, NT, 8], F32)
        lgg_sb = ph.sbuf("lgg_sb", [128, NT, 8], F32)
        er_sb = ph.sbuf("er_sb", [128, NT, 8], F32)
        eg_sb = ph.sbuf("eg_sb", [128, NT, 8], F32)
        cw_sb = ph.sbuf("cw_sb", [128, NT, 8], F32)
        rw_sb = ph.sbuf("rw_sb", [128, NT, 12], F32)
        sdum_sb = ph.sbuf("sdum_sb", [128, 2], F32)

        pq0 = ph.psum("pq0", [128, 512], F32)
        pq1 = ph.psum("pq1", [128, 512], F32)
        pn0 = ph.psum("pn0", [128, 512], F32)
        pn1 = ph.psum("pn1", [128, 512], F32)
        po0 = ph.psum("po0", [128, 512], F32)
        po1 = ph.psum("po1", [128, 512], F32)
        plg = ph.psum("plg", [128, 512], F32)

        ld_x = ph.sem("ld_x")
        ld_w = ph.sem("ld_w")
        sQm = ph.sem("sQm")
        sQe = ph.sem("sQe")
        sSqm = ph.sem("sSqm")
        sNm = ph.sem("sNm")
        sNe = ph.sem("sNe")
        sPm = ph.sem("sPm")
        sCl = ph.sem("sCl")
        sEx = ph.sem("sEx")
        sOm = ph.sem("sOm")
        sOe = ph.sem("sOe")
        sLm = ph.sem("sLm")
        sLv = ph.sem("sLv")
        sVx = ph.sem("sVx")
        sSc = ph.sem("sSc")
        st = ph.sem("st")

        with nc.Block() as block:
            @block.sync
            def _(sync):
                sync.dma_start(out=xn1_sb[:, :, :],
                               in_=xn1_in.ap().rearrange("(c p) t -> p c t", p=128)).then_inc(ld_x, 16)

            @block.gpsimd
            def _(g):
                g.dma_start(out=kv_sb[:, :, :],
                            in_=kvb_in.ap().rearrange("(r p) c -> p r c", p=128)).then_inc(ld_w, 16)
                g.dma_start(out=wq_sb[:, :, :],
                            in_=wq_in.ap().rearrange("(c p) d -> p c d", p=128)).then_inc(ld_w, 16)
                g.dma_start(out=wt_sb[:, :], in_=wt_in[:, :]).then_inc(ld_w, 16)
                g.dma_start(out=wrg_sb[:, :, 0:8],
                            in_=wrt_in.ap().rearrange("(c p) d -> p c d", p=128)).then_inc(ld_w, 16)
                g.dma_start(out=wrg_sb[:, :, 8:16],
                            in_=wgt_in.ap().rearrange("(c p) d -> p c d", p=128)).then_inc(ld_w, 16)
                g.dma_start(out=onecf_sb[:, :], in_=onec_in[:, :]).then_inc(ld_w, 16)
                g.dma_start(out=ones1f_sb[:, :], in_=ones1_in[:, :]).then_inc(ld_w, 16)

        with nc.Block() as block:
            @block.tensor
            def _(t):
                t.wait_ge(ld_x, 16)
                t.wait_ge(ld_w, 112)
                k = 0
                for dc in range(NC_D):
                    for tg in range(4):
                        pb = (pq0, pq1)[k % 2]
                        if k >= 2:
                            t.wait_ge(sQe, k - 1)
                        inst = None
                        for ci in range(NC_D):
                            inst = t.matmul(pb[:, :],
                                            wq_sb[:, ci, 128 * dc:128 * dc + 128],
                                            xn1_sb[:, ci, 512 * tg:512 * tg + 512],
                                            start=(ci == 0), stop=(ci == NC_D - 1))
                        inst.then_inc(sQm, 1)
                        k += 1

            @block.scalar
            def _(s):
                k = 0
                for dc in range(NC_D):
                    for tg in range(4):
                        pb = (pq0, pq1)[k % 2]
                        s.wait_ge(sQm, k + 1)
                        s.copy(q_sb[:, dc, 512 * tg:512 * tg + 512],
                               pb[:, :]).then_inc(sQe, 1)
                        k += 1

        for h in range(16):
            dc = h // 2
            base = 64 * (h % 2)
            with nc.Block() as block:
                @block.tensor
                def _(t, h=h, dc=dc, base=base):
                    t.wait_ge(sSqm, h + 1)
                    for tg in range(4):
                        pnb = (pn0, pn1)[tg % 2]
                        if 4 * h + tg >= 2:
                            t.wait_ge(sNe, 4 * h + tg - 1)
                        inst = t.matmul(pnb[0:1, 0:512],
                                        onecf_sb[base:base + 64, :],
                                        sqq_sb[base:base + 64, 512 * tg:512 * tg + 512],
                                        start=True, stop=True)
                        inst.then_inc(sNm, 1)
                    k = 0
                    for mc in range(2):
                        for tg in range(4):
                            pb = (pq0, pq1)[k % 2]
                            if 8 * h + k >= 2:
                                t.wait_ge(sCl, 8 * h + k - 1)
                            t.wait_ge(sNe, 4 * h + tg + 1)
                            t.matmul(pb[:, :],
                                     wt_sb[base:base + 64, 128 * mc:128 * mc + 128],
                                     q_sb[base:base + 64, dc, 512 * tg:512 * tg + 512],
                                     start=True, stop=False)
                            inst = t.matmul(pb[:, :], ones1f_sb[:, :],
                                            nq_sb[:, 512 * tg:512 * tg + 512],
                                            start=False, stop=True)
                            inst.then_inc(sPm, 1)
                            k += 1
                    for tt in range(NT):
                        pb = (po0, po1)[tt % 2]
                        if 16 * h + tt >= 2:
                            t.wait_ge(sOe, 16 * h + tt - 1)
                        inst = None
                        for mc in range(2):
                            t.wait_ge(sEx, 8 * h + 4 * mc + tt // 4 + 1)
                            kvr = (4 * (h // 2) + 2 * (h % 2) + mc) if kv_a2a \
                                else (2 * h + mc)
                            inst = t.matmul(pb[:, 0:65],
                                            pq_sb[:, mc, 128 * tt:128 * tt + 128],
                                            kv_sb[:, kvr, :],
                                            start=(mc == 0), stop=(mc == 1))
                        inst.then_inc(sOm, 1)

                @block.scalar
                def _(s, h=h, dc=dc, base=base):
                    if h >= 1:
                        s.wait_ge(sNm, 4 * h)
                    s.activation(sqq_sb[base:base + 64, :],
                                 q_sb[base:base + 64, dc, :],
                                 AF.Square).then_inc(sSqm, 1)
                    for tg in range(4):
                        pnb = (pn0, pn1)[tg % 2]
                        s.wait_ge(sNm, 4 * h + tg + 1)
                        if h >= 1:
                            s.wait_ge(sPm, 8 * (h - 1) + 4 + tg + 1)
                        s.activation(nq_sb[0:1, 512 * tg:512 * tg + 512],
                                     pnb[0:1, 0:512], AF.Identity,
                                     scale=-0.5).then_inc(sNe, 1)
                    if h >= 1:
                        s.wait_ge(sOm, 16 * h)
                    for k in range(8):
                        mc, tg = divmod(k, 4)
                        s.wait_ge(sCl, 8 * h + k + 1)
                        s.activation(pq_sb[:, mc, 512 * tg:512 * tg + 512],
                                     pqe_sb[:, k % 2, :], AF.Exp).then_inc(sEx, 1)

                @block.vector
                def _(v, h=h, dc=dc, base=base):
                    for k in range(8):
                        pb = (pq0, pq1)[k % 2]
                        v.wait_ge(sPm, 8 * h + k + 1)
                        if 8 * h + k >= 2:
                            v.wait_ge(sEx, 8 * h + k - 1)
                        v.tensor_scalar(pqe_sb[:, k % 2, :], pb[:, :],
                                        -LN16, -60.0,
                                        ALU.add, ALU.max).then_inc(sCl, 1)
                    ssync = SelfSync(v, sVx, dummy=sdum_sb[:, 0:1])
                    ssync.n = 2 * 16 * h
                    for tt in range(NT):
                        pb = (po0, po1)[tt % 2]
                        v.wait_ge(sOm, 16 * h + tt + 1)
                        inst = v.tensor_scalar(zc_sb[:, tt:tt + 1], pb[:, 64:65],
                                               1e-6, None, ALU.add)
                        ssync.flush(inst)
                        inst = v.reciprocal(rz_sb[:, tt:tt + 1], zc_sb[:, tt:tt + 1])
                        ssync.flush(inst)
                        v.tensor_scalar(s1_sb[:, tt, 64 * h:64 * h + 64],
                                        pb[:, 0:64], rz_sb[:, tt:tt + 1],
                                        None, ALU.mult).then_inc(sOe, 1)

        with nc.Block() as block:
            @block.tensor
            def _(t):
                for ti in range(NT):
                    if ti >= 1:
                        t.wait_ge(sLv, ti)
                    inst = None
                    for ci in range(NC_D):
                        inst = t.matmul(plg[:, 0:16],
                                        xn1_sb[:, ci, 128 * ti:128 * ti + 128],
                                        wrg_sb[:, ci, :], start=(ci == 0),
                                        stop=(ci == NC_D - 1))
                    inst.then_inc(sLm, 1)

            @block.vector
            def _(v):
                ssync = SelfSync(v, sVx, dummy=sdum_sb[:, 1:2])
                ssync.n = 2 * 16 * 16
                for ti in range(NT):
                    v.wait_ge(sLm, ti + 1)
                    v.tensor_copy(lgr_sb[:, ti, :], plg[:, 0:8])
                    v.tensor_copy(lgg_sb[:, ti, :], plg[:, 8:16]).then_inc(sLv, 1)
                ssync.barrier()
                v.reduce_max(rw_sb[:, :, 0], lgr_sb[:, :, :], AX.X, negate=True)
                inst = v.reduce_max(rw_sb[:, :, 1], lgg_sb[:, :, :], AX.X, negate=True)
                ssync.flush(inst)
                inst = v.tensor_scalar(rw_sb[:, :, 2], rw_sb[:, :, 1], 10.0, None,
                                       ALU.mult)
                ssync.flush(inst)
                v.nop().then_inc(sLv, 1)
                v.wait_ge(sSc, 32)
                inst = v.reduce_sum(rw_sb[:, :, 3], er_sb[:, :, :], AX.X)
                inst2 = v.reduce_sum(rw_sb[:, :, 4], eg_sb[:, :, :], AX.X)
                ssync.flush(inst2)
                v.reciprocal(rw_sb[:, :, 5], rw_sb[:, :, 3])
                inst = v.reciprocal(rw_sb[:, :, 6], rw_sb[:, :, 4])
                ssync.flush(inst)
                for ti in range(NT):
                    v.tensor_scalar(er_sb[:, ti, :], er_sb[:, ti, :],
                                    rw_sb[:, ti, 5:6], None, ALU.mult)
                    v.tensor_scalar(eg_sb[:, ti, :], eg_sb[:, ti, :],
                                    rw_sb[:, ti, 6:7], None, ALU.mult)
                ssync.barrier()
                inst = v.tensor_mul(cw_sb[:, :, :], er_sb[:, :, :], eg_sb[:, :, :])
                ssync.flush(inst)
                inst = v.reduce_sum(rw_sb[:, :, 7], cw_sb[:, :, :], AX.X)
                ssync.flush(inst)
                inst = v.tensor_scalar(rw_sb[:, :, 8], rw_sb[:, :, 7], 1e-9, None,
                                       ALU.add)
                ssync.flush(inst)
                inst = v.reciprocal(rw_sb[:, :, 9], rw_sb[:, :, 8])
                ssync.flush(inst)
                for ti in range(NT):
                    v.tensor_scalar(cw_sb[:, ti, :], cw_sb[:, ti, :],
                                    rw_sb[:, ti, 9:10], None, ALU.mult)

            @block.scalar
            def _(s):
                s.wait_ge(sLv, NT + 1)
                for ti in range(NT):
                    s.activation(er_sb[:, ti, :], lgr_sb[:, ti, :], AF.Exp,
                                 bias=rw_sb[:, ti, 0:1]).then_inc(sSc, 1)
                for ti in range(NT):
                    s.activation(eg_sb[:, ti, :], lgg_sb[:, ti, :], AF.Exp,
                                 bias=rw_sb[:, ti, 2:3], scale=10.0).then_inc(sSc, 1)

        with nc.Block() as block:
            @block.sync
            def _(sync):
                sync.dma_start(out=s1_out.ap().rearrange("(T p) d -> p T d", p=128),
                               in_=s1_sb[:, :, :]).then_inc(st, 16)
                sync.dma_start(out=cw_out.ap().rearrange("(T p) d -> p T d", p=128),
                               in_=cw_sb[:, :, :]).then_inc(st, 16)
                sync.wait_ge(st, 32)
    return nc


def build_launch_C3b():
    nc = bass.Bass("TRN2", target_bir_lowering=False, debug=False,
                   detect_race_conditions=False)
    xn1_in = nc.dram_tensor("xn1fm", [D, TCORE], BF16, kind="ExternalInput")
    x1_in = nc.dram_tensor("x1", [TCORE, D], F32, kind="ExternalInput")
    s1_in = nc.dram_tensor("s1", [TCORE, D], BF16, kind="ExternalInput")
    cw_in = nc.dram_tensor("cw", [TCORE, 8], F32, kind="ExternalInput")
    lin_in = nc.dram_tensor("lin", [8, 32, 128, 3072], F32, kind="ExternalInput")
    x2_out = nc.dram_tensor("x2", [TCORE, D], F32, kind="ExternalOutput")

    HT = 1024
    NTT = HT // 128

    emit_C3b(nc, "C3b_", xn1_in, x1_in, s1_in, cw_in, lin_in, x2_out)
    return nc


def emit_C3b(nc, PFX, xn1_in, x1_in, s1_in, cw_in, lin_in, x2_out):
    HT = 1024
    NTT = HT // 128
    with Phase(nc, PFX) as ph:
        xn1_sb = ph.sbuf("xn1_sb", [128, NC_D, TCORE], BF16)
        s1t_sb0 = ph.sbuf("s1t_sb0", [128, D], BF16)
        s1t_sb1 = ph.sbuf("s1t_sb1", [128, D], BF16)
        cw_sb = ph.sbuf("cw_sb", [128, NT, 8], F32)
        h_sb = ph.sbuf("h_sb", [128, 32, HT], BF16)
        acc_sb = ph.sbuf("acc_sb", [128, NTT, D], F32)
        wbuf_sb = ph.sbuf("wbuf_sb", [128, 2, 4, 2048], BF16)
        wdb_sb = ph.sbuf("wdb_sb", [128, 2, 4, 512], BF16)
        sg_sb0 = ph.sbuf("sg_sb0", [128, 512], BF16)
        sg_sb1 = ph.sbuf("sg_sb1", [128, 512], BF16)
        g1_sb0 = ph.sbuf("g1_sb0", [128, D], BF16)
        g1_sb1 = ph.sbuf("g1_sb1", [128, D], BF16)
        x1t_sb0 = ph.sbuf("x1t_sb0", [128, D], F32)
        x1t_sb1 = ph.sbuf("x1t_sb1", [128, D], F32)
        sg1_sb0 = ph.sbuf("sg1_sb0", [128, D], F32)
        sg1_sb1 = ph.sbuf("sg1_sb1", [128, D], F32)
        out_sb0 = ph.sbuf("out_sb0", [128, D], F32)
        out_sb1 = ph.sbuf("out_sb1", [128, D], F32)

        p8 = [ph.psum("p8_%d" % i, [128, 512], F32) for i in range(8)]

        ld_x = ph.sem("ld_x")
        ld_t = ph.sem("ld_t")
        ld_d = ph.sem("ld_d")
        ld_1 = ph.sem("ld_1")
        sGm = ph.sem("sGm")
        sUm = ph.sem("sUm")
        sSi = ph.sem("sSi")
        sGU = ph.sem("sGU")
        sDm = ph.sem("sDm")
        sDe = ph.sem("sDe")
        sG1 = ph.sem("sG1")
        sCv = ph.sem("sCv")
        sCm = ph.sem("sCm")
        st = ph.sem("st")

        with nc.Block() as block:
            @block.sync
            def _(sync):
                sync.dma_start(out=xn1_sb[:, :, :],
                               in_=xn1_in.ap().rearrange("(c p) t -> p c t", p=128)).then_inc(ld_x, 16)
                sync.dma_start(out=cw_sb[:, :, :],
                               in_=cw_in.ap().rearrange("(T p) d -> p T d", p=128)).then_inc(ld_x, 16)

        nqs = [0]
        nwd = [0]
        nx1 = [0]

        for hf in range(2):
            q = 8 * hf
            with nc.Block() as block:
                @block.gpsimd
                def _(g, hf=hf):
                    for e in range(8):
                        q = 8 * hf + e
                        for fg in range(8):
                            glk = (32 * q) // 4 + fg
                            if glk >= 2:
                                g.wait_ge(sGm, 8 * glk - 8)
                            g.dma_start(out=wbuf_sb[:, glk % 2, :, :],
                                        in_=lin_in[e, 4 * fg:4 * fg + 4, :,
                                                   0:2048].rearrange("s p d -> p s d")
                                        ).then_inc(ld_t, 16)
                        for dh in range(2):
                            for wg4 in range(8):
                                G = (64 * q + 32 * dh) // 4 + wg4
                                if G >= 2:
                                    g.wait_ge(sDm, 4 * G - 4)
                                g.dma_start(out=wdb_sb[:, G % 2, :, :],
                                            in_=lin_in[e, 4 * wg4:4 * wg4 + 4, :,
                                                       2048 + 512 * dh:2560 + 512 * dh
                                                       ].rearrange("s p d -> p s d")
                                            ).then_inc(ld_d, 16)

                @block.sync
                def _(sync, hf=hf):
                    for tt in range(NTT):
                        k = NTT * hf + tt
                        if k >= 2:
                            sync.wait_ge(sCv, k - 1)
                            tiP = k - 2
                            sync.dma_start(
                                out=x2_out.ap().rearrange("(T p) d -> p T d",
                                                          p=128)[:, tiP, :],
                                in_=(out_sb0, out_sb1)[tiP % 2][:, :]).then_inc(st, 16)
                        sync.dma_start(
                            out=(x1t_sb0, x1t_sb1)[tt % 2][:, :],
                            in_=x1_in.ap().rearrange("(T p) d -> p T d", p=128)[:, k, :]
                        ).then_inc(ld_1, 16)
                        sync.dma_start(
                            out=(s1t_sb0, s1t_sb1)[tt % 2][:, :],
                            in_=s1_in.ap().rearrange("(T p) d -> p T d", p=128)[:, k, :]
                        ).then_inc(ld_1, 16)
                    for tt in (NTT - 2, NTT - 1):
                        k = NTT * hf + tt
                        sync.wait_ge(sCv, k + 1)
                        sync.dma_start(
                            out=x2_out.ap().rearrange("(T p) d -> p T d", p=128)[:, k, :],
                            in_=(out_sb0, out_sb1)[tt % 2][:, :]).then_inc(st, 16)
                    sync.wait_ge(st, 16 * NTT * (hf + 1))

                @block.tensor
                def _(t, hf=hf):
                    if hf == 0:
                        t.wait_ge(ld_x, 32)
                    for e in range(8):
                        q = 8 * hf + e
                        for fs in range(32):
                            k = 32 * q + fs
                            glk = k // 4
                            t.wait_ge(ld_t, 16 * (glk + 1))
                            for tg in range(2):
                                kk = 2 * k + tg
                                pg = p8[(kk % 2) * 2]
                                pu = p8[(kk % 2) * 2 + 1]
                                if kk >= 2:
                                    t.wait_ge(sSi, kk - 1)
                                    t.wait_ge(sGU, kk - 1)
                                if fs == 0 and tg == 0 and q > 0:
                                    t.wait_ge(sDe, 16 * q)
                                for ci in range(NC_D):
                                    t.matmul(pg[:, :],
                                             wbuf_sb[:, glk % 2, fs % 4,
                                                     128 * ci:128 * ci + 128],
                                             xn1_sb[:, ci, HT * hf + 512 * tg:
                                                    HT * hf + 512 * tg + 512],
                                             start=(ci == 0), stop=(ci == NC_D - 1))
                                inst = None
                                for ci in range(NC_D):
                                    inst = t.matmul(pu[:, :],
                                                    wbuf_sb[:, glk % 2, fs % 4,
                                                            1024 + 128 * ci:1152 + 128 * ci],
                                                    xn1_sb[:, ci, HT * hf + 512 * tg:
                                                           HT * hf + 512 * tg + 512],
                                                    start=(ci == 0), stop=(ci == NC_D - 1))
                                inst.then_inc(sGm, 1)
                        t.wait_ge(sGU, 64 * (q + 1))
                        t.wait_ge(sSi, 64 * (q + 1))
                        for dh in range(2):
                            if dh == 1:
                                t.wait_ge(sDe, 16 * q + 8)
                            for fs in range(32):
                                kd = 64 * q + 32 * dh + fs
                                G = kd // 4
                                t.wait_ge(ld_d, 16 * (G + 1))
                                inst = None
                                for tt in range(NTT):
                                    inst = t.matmul(p8[tt][:, :],
                                                    h_sb[:, fs, 128 * tt:128 * tt + 128],
                                                    wdb_sb[:, G % 2, fs % 4, :],
                                                    start=(fs == 0), stop=(fs == 31))
                                inst.then_inc(sDm, 1)

                @block.scalar
                def _(s, hf=hf):
                    for e in range(8):
                        q = 8 * hf + e
                        for fs in range(32):
                            for tg in range(2):
                                kk = 2 * (32 * q + fs) + tg
                                pg = p8[(kk % 2) * 2]
                                sg = (sg_sb0, sg_sb1)[kk % 2]
                                s.wait_ge(sGm, kk + 1)
                                if kk >= 2:
                                    s.wait_ge(sGU, kk - 1)
                                s.activation(sg[:, :], pg[:, :],
                                             AF.Silu).then_inc(sSi, 1)
                    for tt in range(NTT):
                        k = NTT * hf + tt
                        g1 = (g1_sb0, g1_sb1)[tt % 2]
                        s.wait_ge(sDe, 16 * 8 * (hf + 1))
                        if k >= 2:
                            s.wait_ge(sCv, k - 1)
                        s.activation(g1[:, :], acc_sb[:, tt, :],
                                     AF.Silu).then_inc(sG1, 1)

                @block.vector
                def _(v, hf=hf):
                    if hf == 0:
                        v.wait_ge(ld_x, 32)
                    for e in range(8):
                        q = 8 * hf + e
                        for fs in range(32):
                            for tg in range(2):
                                kk = 2 * (32 * q + fs) + tg
                                pu = p8[(kk % 2) * 2 + 1]
                                sg = (sg_sb0, sg_sb1)[kk % 2]
                                v.wait_ge(sSi, kk + 1)
                                v.tensor_mul(h_sb[:, fs, 512 * tg:512 * tg + 512],
                                             sg[:, :], pu[:, :]).then_inc(sGU, 1)
                        for dh in range(2):
                            for tt in range(NTT):
                                v.wait_ge(sDm, 64 * q + 32 * dh + 32)
                                ti = NTT * hf + tt
                                if e == 0:
                                    v.tensor_scalar(acc_sb[:, tt, 512 * dh:512 * dh + 512],
                                                    p8[tt][:, :], cw_sb[:, ti, 0:1],
                                                    None, ALU.mult).then_inc(sDe, 1)
                                else:
                                    v.scalar_tensor_tensor(
                                        acc_sb[:, tt, 512 * dh:512 * dh + 512],
                                        p8[tt][:, :], cw_sb[:, ti, e:e + 1],
                                        acc_sb[:, tt, 512 * dh:512 * dh + 512],
                                        ALU.mult, ALU.add).then_inc(sDe, 1)
                    for tt in range(NTT):
                        k = NTT * hf + tt
                        ti = NTT * hf + tt
                        g1 = (g1_sb0, g1_sb1)[tt % 2]
                        sg1 = (sg1_sb0, sg1_sb1)[tt % 2]
                        x1t = (x1t_sb0, x1t_sb1)[tt % 2]
                        outb = (out_sb0, out_sb1)[tt % 2]
                        s1t = (s1t_sb0, s1t_sb1)[tt % 2]
                        v.wait_ge(sG1, k + 1)
                        v.wait_ge(ld_1, 32 * (k + 1))
                        if k >= 2:
                            v.wait_ge(st, 16 * (k - 1))
                        inst = v.tensor_mul(sg1[:, :], s1t[:, :], g1[:, :])
                        inst.then_inc(sCm, 1)
                        v.wait_ge(sCm, k + 1)
                        v.tensor_add(outb[:, :], x1t[:, :], sg1[:, :]).then_inc(sCv, 1)

    return nc




_NC_CACHE = {}


def _nc(name, builder):
    if name not in _NC_CACHE:
        _NC_CACHE[name] = builder()
    return _NC_CACHE[name]


def _pack_lin(wg, wu, wd):
    lin = np.empty((8, 32, 128, 3072), np.float32)
    for e in range(8):
        ge = np.asarray(wg[e], np.float32).reshape(8, 128, 4096)
        ue = np.asarray(wu[e], np.float32).reshape(8, 128, 4096)
        de = np.asarray(wd[e], np.float32)
        for fs in range(32):
            lin[e, fs, :, 0:1024] = ge[:, :, 128 * fs:128 * fs + 128].transpose(
                1, 0, 2).reshape(128, 1024)
            lin[e, fs, :, 1024:2048] = ue[:, :, 128 * fs:128 * fs + 128].transpose(
                1, 0, 2).reshape(128, 1024)
            lin[e, fs, :, 2048:3072] = de[128 * fs:128 * fs + 128, :]
    return lin


def kernel(**inputs):
    return kernel_fused(**inputs)


def kernel_unfused(**inputs):
    cores = list(range(NCORES))
    ones1 = np.ones((1, 128), np.float32)
    onec = np.ones((128, 1), np.float32)
    ident = np.eye(128, dtype=np.float32)

    resA = run_bass_kernel_spmd(_nc("A", build_launch_A), _prep_A_inputs(inputs),
                                core_ids=cores)
    xnfm = [r["xn_fm"] for r in resA.results]
    g0 = [r["g"] for r in resA.results]

    resB = run_bass_kernel_spmd(_nc("B", build_launch_B),
                                _prep_B_inputs(xnfm, inputs), core_ids=cores)
    sB = [r["s"] for r in resB.results]

    x = np.asarray(inputs["L0_attn_x"], np.float32).reshape(TTOK, D)
    wr = np.ascontiguousarray(np.asarray(inputs["L0_attn_Wr"], np.float32))
    br = np.ascontiguousarray(np.asarray(inputs["L0_attn_br"], np.float32)[None, :])
    mapsC1 = []
    for c in cores:
        s_own = np.concatenate([sB[i][TCORE * c:TCORE * (c + 1), :] for i in cores],
                               axis=1)
        mapsC1.append(dict(x=np.ascontiguousarray(x[TCORE * c:TCORE * (c + 1)]),
                           sfm=np.ascontiguousarray(s_own.T), g=g0[c], wr=wr, br=br,
                           ones1=ones1, ident=ident))
    resC1 = run_bass_kernel_spmd(_nc("C1", build_launch_C1), mapsC1, core_ids=cores)
    x1 = [r["x1"] for r in resC1.results]
    xn1fm = [r["xn1_fm"] for r in resC1.results]

    xn1_full = np.ascontiguousarray(np.concatenate(xn1fm, axis=1))
    wk = np.asarray(inputs["L1_attn_Wk"], np.float32)
    wv = np.asarray(inputs["L1_attn_Wv"], np.float32)
    omega = np.asarray(inputs["L1_attn_omega"], np.float32)
    wTd = np.ascontiguousarray(np.concatenate([omega.T, omega.T], 0))
    mapsC2 = []
    for c in cores:
        mapsC2.append(dict(xn1f=xn1_full,
                           wk2h=np.ascontiguousarray(wk[:, 128 * c:128 * c + 128]),
                           wv2h=np.ascontiguousarray(wv[:, 128 * c:128 * c + 128]),
                           wT=wTd, onec=onec))
    resC2 = run_bass_kernel_spmd(_nc("C2", build_launch_C2), mapsC2, core_ids=cores)
    kv4 = [r["kv4"] for r in resC2.results]

    wq = np.ascontiguousarray(np.asarray(inputs["L1_attn_Wq"], np.float32))
    wrt = np.ascontiguousarray(np.asarray(inputs["L1_ffn_w_router"], np.float32))
    wgt = np.ascontiguousarray(np.asarray(inputs["L1_ffn_w_gating"], np.float32))
    mapsC3a = []
    for c in cores:
        b = c // 4
        kvb = np.concatenate(
            [kv4[h // 2][512 * (h % 2) + 256 * b + 128 * mc:
                         512 * (h % 2) + 256 * b + 128 * mc + 128, :]
             for h in range(16) for mc in range(2)], axis=0)
        mapsC3a.append(dict(xn1fm=xn1fm[c], kvb=np.ascontiguousarray(kvb), wq=wq,
                            wT=wTd, wrt=wrt, wgt=wgt, onec=onec, ones1=ones1))
    resC3a = run_bass_kernel_spmd(_nc("C3a", build_launch_C3a), mapsC3a,
                                  core_ids=cores)
    s1 = [r["s1"] for r in resC3a.results]
    cw = [r["cw"] for r in resC3a.results]

    lin = _pack_lin(inputs["L1_ffn_w_gate"], inputs["L1_ffn_w_up"],
                    inputs["L1_ffn_w_down"])
    mapsC3b = [dict(xn1fm=xn1fm[c], x1=x1[c], s1=s1[c], cw=cw[c], lin=lin)
               for c in cores]
    resC3b = run_bass_kernel_spmd(_nc("C3b", build_launch_C3b), mapsC3b,
                                  core_ids=cores)
    out = np.concatenate([r["x2"] for r in resC3b.results], axis=0)
    return np.ascontiguousarray(out.reshape(2, SEQB, D), dtype=np.float32)


from concourse.aot_env import allow_ici_all_to_all


def _emit_cc(nc, PFX, kind, src_t, dst_t):
    with Phase(nc, PFX) as ph:
        cc = ph.sem("cc")
        with nc.Block() as block:
            @block.gpsimd
            def _(g):
                if kind == "AllToAll":
                    with allow_ici_all_to_all():
                        g.collective_compute(kind, ALU.bypass,
                                             replica_groups=[list(range(NCORES))],
                                             ins=[src_t.ap()],
                                             outs=[dst_t.ap()]).then_inc(cc)
                else:
                    g.collective_compute(kind, ALU.bypass,
                                         replica_groups=[list(range(NCORES))],
                                         ins=[src_t.ap()],
                                         outs=[dst_t.ap()]).then_inc(cc)
                g.wait_ge(cc, 1)


def build_fused():
    nc = bass.Bass("TRN2", target_bir_lowering=False, debug=False,
                   detect_race_conditions=False)
    x_in = nc.dram_tensor("x", [TCORE, D], F32, kind="ExternalInput")
    wgate_in = nc.dram_tensor("wgate", [D, 7], F32, kind="ExternalInput")
    wst_in = nc.dram_tensor("wst", [D, D], F32, kind="ExternalInput")
    wet_in = nc.dram_tensor("wet", [D, D], F32, kind="ExternalInput")
    bias_in = nc.dram_tensor("bias", [1, D], F32, kind="ExternalInput")
    ones1_in = nc.dram_tensor("ones1", [1, 128], F32, kind="ExternalInput")
    ident_in = nc.dram_tensor("ident", [128, 128], F32, kind="ExternalInput")
    wqkvL_in = nc.dram_tensor("wqkvL", [D, 384], F32, kind="ExternalInput")
    bqkv_in = nc.dram_tensor("bqkvc", [64, 3], F32, kind="ExternalInput")
    bv_in = nc.dram_tensor("bvrow", [1, 64], F32, kind="ExternalInput")
    onec_in = nc.dram_tensor("onec", [128, 1], F32, kind="ExternalInput")
    msl_in = nc.dram_tensor("msl", [128, 128], mybir.dt.uint8, kind="ExternalInput")
    onet_in = nc.dram_tensor("onet", [128, 128], F32, kind="ExternalInput")
    wr_in = nc.dram_tensor("wr", [D, D], F32, kind="ExternalInput")
    br_in = nc.dram_tensor("br", [1, D], F32, kind="ExternalInput")
    wk_in = nc.dram_tensor("wk2h", [D, 128], F32, kind="ExternalInput")
    wv_in = nc.dram_tensor("wv2h", [D, 128], F32, kind="ExternalInput")
    wt_in = nc.dram_tensor("wT", [128, 256], F32, kind="ExternalInput")
    wq_in = nc.dram_tensor("wq", [D, D], F32, kind="ExternalInput")
    wrt_in = nc.dram_tensor("wrt", [D, 8], F32, kind="ExternalInput")
    wgt_in = nc.dram_tensor("wgt", [D, 8], F32, kind="ExternalInput")
    lin_in = nc.dram_tensor("lin", [8, 32, 128, 3072], F32, kind="ExternalInput")
    x2_out = nc.dram_tensor("x2", [TCORE, D], F32, kind="ExternalOutput")
    xn_c = nc.dram_tensor("xn_c", [D, TCORE], BF16)
    xng = nc.dram_tensor("xng", [8 * D, TCORE], BF16)
    gd = nc.dram_tensor("gd", [TCORE, D], BF16)
    s_c = nc.dram_tensor("s_c", [TTOK, 128], BF16)
    s_t = nc.dram_tensor("s_t", [TTOK, 128], BF16)
    x1_d = nc.dram_tensor("x1_d", [TCORE, D], F32)
    xn1_c = nc.dram_tensor("xn1_c", [D, TCORE], BF16)
    xn1g = nc.dram_tensor("xn1g", [8 * D, TCORE], BF16)
    kv_c = nc.dram_tensor("kv_c", [4096, 65], F32)
    kv_t = nc.dram_tensor("kv_t", [4096, 65], F32)
    s1_d = nc.dram_tensor("s1_d", [TCORE, D], BF16)
    cw_d = nc.dram_tensor("cw_d", [TCORE, 8], F32)

    emit_A(nc, "A_", x_in, wgate_in, wst_in, wet_in, bias_in, ones1_in, ident_in,
           xn_c, gd)
    _emit_cc(nc, "G1_", "AllGather", xn_c, xng)
    emit_B(nc, "B_", None, None, bqkv_in, bv_in, ones1_in, onec_in, msl_in,
           onet_in, s_c, xng=xng, wqkvL_in=wqkvL_in)
    _emit_cc(nc, "T1_", "AllToAll", s_c, s_t)
    emit_C1(nc, "C1_", x_in, None, gd, wr_in, br_in, ones1_in, ident_in,
            x1_d, xn1_c, s_t_in=s_t)
    _emit_cc(nc, "G2_", "AllGather", xn1_c, xn1g)
    emit_C2(nc, "C2_", xn1g, wk_in, wv_in, wt_in, onec_in, kv_c,
            gathered=True, dup_out=True)
    _emit_cc(nc, "T2_", "AllToAll", kv_c, kv_t)
    emit_C3a(nc, "C3a_", xn1_c, kv_t, wq_in, wt_in, wrt_in, wgt_in, onec_in,
             ones1_in, s1_d, cw_d, kv_a2a=True)
    emit_C3b(nc, "C3b_", xn1_c, x1_d, s1_d, cw_d, lin_in, x2_out)
    return nc


def _prep_fused_inputs(inputs):
    x = np.asarray(inputs["L0_attn_x"], np.float32).reshape(TTOK, D)
    wst = np.ascontiguousarray(np.asarray(inputs["L0_ffn_W_shared"], np.float32).T)
    wet = np.ascontiguousarray(np.asarray(inputs["L0_ffn_W_expert"], np.float32).T)
    wgate = np.ascontiguousarray(np.asarray(inputs["L0_ffn_W_gate"], np.float32))
    bias = np.ascontiguousarray(np.asarray(inputs["L0_ffn_bias"], np.float32)[None, :])
    ones1 = np.ones((1, 128), np.float32)
    onec = np.ones((128, 1), np.float32)
    ident = np.eye(128, dtype=np.float32)
    jj, ii = np.meshgrid(np.arange(128), np.arange(128), indexing="ij")
    msl = (jj < ii).astype(np.uint8)
    onet = np.ones((128, 128), np.float32)
    wqkv = np.asarray(inputs["L0_attn_Wqkv"], np.float32)
    bq = np.asarray(inputs["L0_attn_bqkv"], np.float32)
    bqkvc = np.ascontiguousarray(np.stack([bq[0:64], bq[64:128], bq[128:192]], 1))
    bvrow = np.ascontiguousarray(bq[None, 128:192])
    wr = np.ascontiguousarray(np.asarray(inputs["L0_attn_Wr"], np.float32))
    br = np.ascontiguousarray(np.asarray(inputs["L0_attn_br"], np.float32)[None, :])
    wk = np.asarray(inputs["L1_attn_Wk"], np.float32)
    wv = np.asarray(inputs["L1_attn_Wv"], np.float32)
    omega = np.asarray(inputs["L1_attn_omega"], np.float32)
    wTd = np.ascontiguousarray(np.concatenate([omega.T, omega.T], 0))
    wq = np.ascontiguousarray(np.asarray(inputs["L1_attn_Wq"], np.float32))
    wrt = np.ascontiguousarray(np.asarray(inputs["L1_ffn_w_router"], np.float32))
    wgt = np.ascontiguousarray(np.asarray(inputs["L1_ffn_w_gating"], np.float32))
    lin = _pack_lin(inputs["L1_ffn_w_gate"], inputs["L1_ffn_w_up"],
                    inputs["L1_ffn_w_down"])
    maps = []
    for c in range(NCORES):
        wqkvL = np.zeros((D, 384), np.float32)
        for hl in range(2):
            h = 2 * c + hl
            wqkvL[64 * h:64 * h + 64, 64 * hl:64 * hl + 64] = wqkv[:, 0:64]
            wqkvL[64 * h:64 * h + 64, 128 + 64 * hl:128 + 64 * hl + 64] = wqkv[:, 64:128]
            wqkvL[64 * h:64 * h + 64, 256 + 64 * hl:256 + 64 * hl + 64] = wqkv[:, 128:192]
        maps.append(dict(
            x=np.ascontiguousarray(x[TCORE * c:TCORE * (c + 1)]),
            wgate=wgate, wst=wst, wet=wet, bias=bias, ones1=ones1, ident=ident,
            wqkvL=wqkvL, bqkvc=bqkvc, bvrow=bvrow, onec=onec, msl=msl, onet=onet,
            wr=wr, br=br,
            wk2h=np.ascontiguousarray(wk[:, 128 * c:128 * c + 128]),
            wv2h=np.ascontiguousarray(wv[:, 128 * c:128 * c + 128]),
            wT=wTd, wq=wq, wrt=wrt, wgt=wgt, lin=lin))
    return maps


def kernel_fused(**inputs):
    res = run_bass_kernel_spmd(_nc("F", build_fused), _prep_fused_inputs(inputs),
                               core_ids=list(range(NCORES)))
    out = np.concatenate([r["x2"] for r in res.results], axis=0)
    return np.ascontiguousarray(out.reshape(2, SEQB, D), dtype=np.float32)


# version 07 (final)
# speedup vs baseline: 1.3650x; 1.0123x over previous
import contextlib
import numpy as np
import ml_dtypes

import concourse.bass as bass
import concourse.mybir as mybir
from concourse.bass_utils import run_bass_kernel_spmd

F32 = mybir.dt.float32
BF16 = mybir.dt.bfloat16
AF = mybir.ActivationFunctionType
ALU = mybir.AluOpType
AX = mybir.AxisListType
BF = ml_dtypes.bfloat16

NCORES = 8
D = 1024
TTOK = 16384
TCORE = 2048
NT = TCORE // 128
NC_D = D // 128


class SelfSync:

    def __init__(self, eng, sem, dummy=None):
        self.eng, self.sem, self.n = eng, sem, 0
        self.dummy = dummy

    def flush(self, inst):
        self.n += 1
        inst.then_inc(self.sem, 1)
        self.eng.wait_ge(self.sem, self.n)

    def barrier(self):
        self.n += 1
        self.eng.memset(self.dummy, 0.0).then_inc(self.sem, 1)
        self.eng.wait_ge(self.sem, self.n)


class Phase:

    def __init__(self, nc, prefix=""):
        self.nc, self.prefix = nc, prefix
        self.stack = contextlib.ExitStack()
        self.sems = []

    def __enter__(self):
        self.stack.__enter__()
        return self

    def sbuf(self, name, shape, dtype):
        return self.stack.enter_context(
            self.nc.sbuf_tensor(self.prefix + name, shape, dtype))

    def psum(self, name, shape, dtype):
        return self.stack.enter_context(
            self.nc.psum_tensor(self.prefix + name, shape, dtype))

    def sem(self, name):
        h = self.stack.enter_context(self.nc.semaphore(self.prefix + name))
        self.sems.append(h)
        return h

    def __exit__(self, *exc):
        if exc[0] is None and self.sems:
            sems = self.sems
            with self.nc.Block() as blk:
                @blk.gpsimd
                def _(g):
                    for h in sems:
                        g.sem_clear(range(h.num, h.num + 1))
        return self.stack.__exit__(*exc)



def _ln_stats(v, s, x_sb, sq0, sq1, sums, sumsq, mean, var, w0, w1, eps,
              sSq, sVc, ssync, n_tiles=NT, width=D):
    for ti in range(n_tiles):
        v.reduce_sum(sums[:, ti:ti + 1], x_sb[:, ti, :], AX.X)
    for ti in range(n_tiles):
        v.wait_ge(sSq, ti + 1)
        inst = v.reduce_sum(sumsq[:, ti:ti + 1], (sq0, sq1)[ti % 2][:, :], AX.X)
        inst.then_inc(sVc, 1)
    ssync.barrier()
    v.tensor_scalar(mean[:, :], sums[:, :], 1.0 / width, None, ALU.mult)
    inst = v.tensor_scalar(w0[:, :], sumsq[:, :], 1.0 / width, None, ALU.mult)
    ssync.flush(inst)
    inst = v.tensor_mul(w1[:, :], mean[:, :], mean[:, :])
    ssync.flush(inst)
    return v.tensor_sub(var[:, :], w0[:, :], w1[:, :])


def _ln_stats_scalar(s, x_sb, sq0, sq1, sSq, sVc, n_tiles=NT):
    for ti in range(n_tiles):
        if ti >= 2:
            s.wait_ge(sVc, ti - 1)
        s.activation((sq0, sq1)[ti % 2][:, :], x_sb[:, ti, :],
                     AF.Square).then_inc(sSq, 1)


def build_launch_A(dbg_xn_tm=False):
    nc = bass.Bass("TRN2", target_bir_lowering=False, debug=False,
                   detect_race_conditions=False)
    x_in = nc.dram_tensor("x", [TCORE, D], F32, kind="ExternalInput")
    wgate_in = nc.dram_tensor("wgate", [D, 7], F32, kind="ExternalInput")
    wst_in = nc.dram_tensor("wst", [D, D], F32, kind="ExternalInput")
    wet_in = nc.dram_tensor("wet", [D, D], F32, kind="ExternalInput")
    bias_in = nc.dram_tensor("bias", [1, D], F32, kind="ExternalInput")
    ones1_in = nc.dram_tensor("ones1", [1, 128], F32, kind="ExternalInput")
    ident_in = nc.dram_tensor("ident", [128, 128], F32, kind="ExternalInput")
    xnfm_out = nc.dram_tensor("xn_fm", [D, TCORE], BF16, kind="ExternalOutput")
    g_out = nc.dram_tensor("g", [TCORE, D], BF16, kind="ExternalOutput")

    emit_A(nc, "A_", x_in, wgate_in, wst_in, wet_in, bias_in, ones1_in, ident_in, xnfm_out, g_out, dbg_xn_tm=dbg_xn_tm)
    return nc


def emit_A(nc, PFX, x_in, wgate_in, wst_in, wet_in, bias_in, ones1_in, ident_in, xnfm_out, g_out, dbg_xn_tm=False, ag=None):
    with Phase(nc, PFX) as ph:
        x_sb = ph.sbuf("x_sb", [128, NT, D], F32)
        xn_tm = ph.sbuf("xn_tm", [128, NT, D], BF16)
        xn_fm = ph.sbuf("xnf_sb", [128, NC_D, TCORE], BF16)
        g_sb = ph.sbuf("g_sb", [128, NT, D], BF16)
        wgate_sb = ph.sbuf("wgate_sb", [128, NC_D, 7], BF16)
        wst_sb = ph.sbuf("wst_sb", [128, NC_D, D], BF16)
        wet_sb = ph.sbuf("wet_sb", [128, NC_D, D], BF16)
        bias_sb = ph.sbuf("bias_sb", [1, D], BF16)
        ones1_sb = ph.sbuf("ones1_sb", [1, 128], BF16)
        ident_sb = ph.sbuf("ident_sb", [128, 128], BF16)
        eps_sb = ph.sbuf("eps_sb", [128, 1], F32)
        sdum_sb = ph.sbuf("sdum_sb", [128, 2], F32)
        sq0_sb = ph.sbuf("sq0_sb", [128, D], F32)
        sq1_sb = ph.sbuf("sq1_sb", [128, D], F32)
        sums_sb = ph.sbuf("sums_sb", [128, NT], F32)
        sumsq_sb = ph.sbuf("sumsq_sb", [128, NT], F32)
        mean_sb = ph.sbuf("mean_sb", [128, NT], F32)
        var_sb = ph.sbuf("var_sb", [128, NT], F32)
        w0_sb = ph.sbuf("w0_sb", [128, NT], F32)
        w1_sb = ph.sbuf("w1_sb", [128, NT], F32)
        std_sb = ph.sbuf("std_sb", [128, NT], F32)
        rstd_sb = ph.sbuf("rstd_sb", [128, NT], F32)
        lg_sb = ph.sbuf("lg_sb", [128, NT, 7], F32)
        fw_sb = ph.sbuf("fw_sb", [128, NT, 12], F32)
        e2_sb = ph.sbuf("e2_sb", [128, NT, 2], F32)
        fd_sb = ph.sbuf("fd_sb", [128, NT], F32)
        fr_sb = ph.sbuf("fr_sb", [128, NT], F32)
        fac_sb = ph.sbuf("fac_sb", [128, NT], F32)
        t_sb0 = sq0_sb[:, 0:512]
        t_sb1 = sq0_sb[:, 512:1024]
        s_sb0 = sq1_sb[:, 0:512]
        s_sb1 = sq1_sb[:, 512:1024]

        pt0 = ph.psum("pt0", [128, 1024], BF16)
        pt1 = ph.psum("pt1", [128, 1024], BF16)
        pl0 = ph.psum("pl0", [128, 512], F32)
        pl1 = ph.psum("pl1", [128, 512], F32)
        ps0 = ph.psum("ps0", [128, 512], F32)
        ps1 = ph.psum("ps1", [128, 512], F32)
        pe0 = ph.psum("pe0", [128, 512], F32)
        pe1 = ph.psum("pe1", [128, 512], F32)

        ld_x = ph.sem("ld_x")
        ld_w = ph.sem("ld_w")
        sSq = ph.sem("sSq")
        sSc = ph.sem("sSc")
        sVc = ph.sem("sVc")
        sVx = ph.sem("sVx")
        s2a = ph.sem("s2a")
        s2b = ph.sem("s2b")
        s3t = ph.sem("s3t")
        s3e = ph.sem("s3e")
        s4t = ph.sem("s4t")
        s4v = ph.sem("s4v")
        s4e = ph.sem("s4e")
        s5t = ph.sem("s5t")
        s5s = ph.sem("s5s")
        s5v = ph.sem("s5v")
        s5e = ph.sem("s5e")
        st = ph.sem("st")

        with nc.Block() as block:
            @block.sync
            def _(sync):
                sync.dma_start(out=x_sb[:, :, :],
                               in_=x_in.ap().rearrange("(T p) d -> p T d", p=128)).then_inc(ld_x, 16)

            @block.gpsimd
            def _(g):
                g.dma_start(out=wgate_sb[:, :, :],
                            in_=wgate_in.ap().rearrange("(c p) d -> p c d", p=128)).then_inc(ld_w, 16)
                g.dma_start(out=wst_sb[:, :, :],
                            in_=wst_in.ap().rearrange("(c p) d -> p c d", p=128)).then_inc(ld_w, 16)
                g.dma_start(out=wet_sb[:, :, :],
                            in_=wet_in.ap().rearrange("(c p) d -> p c d", p=128)).then_inc(ld_w, 16)
                g.dma_start(out=bias_sb[:, :], in_=bias_in[:, :]).then_inc(ld_w, 16)
                g.dma_start(out=ones1_sb[:, :], in_=ones1_in[:, :]).then_inc(ld_w, 16)
                g.dma_start(out=ident_sb[:, :], in_=ident_in[:, :]).then_inc(ld_w, 16)

        with nc.Block() as block:
            @block.vector
            def _(v):
                ssync = SelfSync(v, sVx, dummy=sdum_sb[:, 0:1])
                v.memset(eps_sb[:, :], 1e-5)
                v.wait_ge(ld_x, 16)
                inst = _ln_stats(v, None, x_sb, sq0_sb, sq1_sb, sums_sb, sumsq_sb,
                                 mean_sb, var_sb, w0_sb, w1_sb, eps_sb, sSq, sVc, ssync)
                inst.then_inc(s2a, 1)
                v.wait_ge(s2b, 1)
                inst = v.reciprocal(rstd_sb[:, :], std_sb[:, :])
                ssync.flush(inst)
                for ti in range(NT):
                    v.tensor_scalar(xn_tm[:, ti, :], x_sb[:, ti, :],
                                    mean_sb[:, ti:ti + 1], rstd_sb[:, ti:ti + 1],
                                    ALU.subtract, ALU.mult)

            @block.scalar
            def _(s):
                s.wait_ge(ld_x, 16)
                _ln_stats_scalar(s, x_sb, sq0_sb, sq1_sb, sSq, sVc)
                s.wait_ge(s2a, 1)
                s.activation(std_sb[:, :], var_sb[:, :], AF.Sqrt,
                             bias=eps_sb[:, 0:1]).then_inc(s2b, 1)

        with nc.Block() as block:
            @block.tensor
            def _(t):
                t.wait_ge(ld_w, 96)
                k = 0
                for ci in range(NC_D):
                    for tg in range(NT // 4):
                        pb = (pt0, pt1)[k % 2]
                        if k >= 2:
                            t.wait_ge(s3e, k - 1)
                        inst = None
                        for j in range(4):
                            ti = 4 * tg + j
                            inst = t.transpose(pb[:, 128 * j:128 * j + 128],
                                               xn_tm[:, ti, 128 * ci:128 * ci + 128],
                                               ident_sb[:, :])
                        inst.then_inc(s3t, 1)
                        k += 1

            @block.scalar
            def _(s):
                k = 0
                for ci in range(NC_D):
                    for tg in range(NT // 4):
                        pb = (pt0, pt1)[k % 2]
                        s.wait_ge(s3t, k + 1)
                        s.copy(xn_fm[:, ci, 512 * tg:512 * tg + 512],
                               pb[:, 0:512]).then_inc(s3e, 1)
                        k += 1

        if ag is not None:
            xng_t, ccA = ag
            with nc.Block() as block:
                @block.sync
                def _(sync):
                    sync.dma_start(out=xnfm_out.ap().rearrange("(c p) t -> p c t",
                                                               p=128),
                                   in_=xn_fm[:, :, :]).then_inc(st, 16)

                @block.gpsimd
                def _(g):
                    g.wait_ge(st, 16)
                    g.collective_compute("AllGather", ALU.bypass,
                                         replica_groups=[list(range(NCORES))],
                                         ins=[xnfm_out.ap()],
                                         outs=[xng_t.ap()]).then_inc(ccA)

        with nc.Block() as block:
            @block.tensor
            def _(t):
                for ti in range(NT):
                    pb = (pl0, pl1)[ti % 2]
                    if ti >= 2:
                        t.wait_ge(s4v, ti - 1)
                    inst = None
                    for ci in range(NC_D):
                        inst = t.matmul(pb[:, 0:7], xn_fm[:, ci, 128 * ti:128 * ti + 128],
                                        wgate_sb[:, ci, :], start=(ci == 0),
                                        stop=(ci == NC_D - 1))
                    inst.then_inc(s4t, 1)

            @block.vector
            def _(v):
                ssync = SelfSync(v, s4e, dummy=sdum_sb[:, 1:2])
                inst = None
                for ti in range(NT):
                    pb = (pl0, pl1)[ti % 2]
                    v.wait_ge(s4t, ti + 1)
                    inst = v.tensor_copy(lg_sb[:, ti, :], pb[:, 0:7])
                    inst.then_inc(s4v, 1)
                ssync.barrier()
                c = lambda j: lg_sb[:, :, j]
                v.tensor_tensor(fw_sb[:, :, 0], c(0), c(1), ALU.max)
                inst = v.tensor_tensor(fw_sb[:, :, 1], c(0), c(1), ALU.min)
                a, b = 0, 1
                nxt = 2
                for j in range(2, 7):
                    tm = nxt; na = nxt + 1; nb = nxt + 2
                    nxt += 3 if j < 6 else 0
                    if nxt > 9:
                        nxt = 2
                    ssync.flush(inst)
                    v.tensor_tensor(fw_sb[:, :, tm], fw_sb[:, :, a], c(j), ALU.min)
                    inst = v.tensor_tensor(fw_sb[:, :, na], fw_sb[:, :, a], c(j),
                                           ALU.max)
                    ssync.flush(inst)
                    inst = v.tensor_tensor(fw_sb[:, :, nb], fw_sb[:, :, b],
                                           fw_sb[:, :, tm], ALU.max)
                    a, b = na, nb
                ssync.flush(inst)
                v.tensor_copy(fw_sb[:, :, 10], fw_sb[:, :, a])
                v.tensor_copy(fw_sb[:, :, 11],
                              fw_sb[:, :, b]).then_inc(s4v, 1)
                v.wait_ge(sSq, NT + 1)
                inst = v.tensor_add(fd_sb[:, :], e2_sb[:, :, 0], e2_sb[:, :, 1])
                ssync.flush(inst)
                inst = v.reciprocal(fr_sb[:, :], fd_sb[:, :])
                ssync.flush(inst)
                v.tensor_mul(fac_sb[:, :], fd_sb[:, :], fr_sb[:, :])

            @block.scalar
            def _(s):
                s.wait_ge(s4v, NT + 1)
                s.activation(e2_sb[:, :, :], fw_sb[:, :, 10:12],
                             AF.Exp).then_inc(sSq, 1)

        with nc.Block() as block:
            @block.tensor
            def _(t):
                k = 0
                for ti in range(NT):
                    for dh in range(2):
                        psb = (ps0, ps1)[k % 2]
                        peb = (pe0, pe1)[k % 2]
                        if k >= 2:
                            t.wait_ge(s5e, k - 1)
                        for ci in range(NC_D):
                            t.matmul(psb[:, :], xn_fm[:, ci, 128 * ti:128 * ti + 128],
                                     wst_sb[:, ci, 512 * dh:512 * dh + 512],
                                     start=(ci == 0), stop=False)
                        t.matmul(psb[:, :], ones1_sb[:, :], bias_sb[:, 512 * dh:512 * dh + 512],
                                 start=False, stop=True)
                        inst = None
                        for ci in range(NC_D):
                            inst = t.matmul(peb[:, :], xn_fm[:, ci, 128 * ti:128 * ti + 128],
                                            wet_sb[:, ci, 512 * dh:512 * dh + 512],
                                            start=(ci == 0), stop=(ci == NC_D - 1))
                        inst.then_inc(s5t, 1)
                        k += 1

            @block.scalar
            def _(s):
                k = 0
                for ti in range(NT):
                    for dh in range(2):
                        psb = (ps0, ps1)[k % 2]
                        ssb = (s_sb0, s_sb1)[k % 2]
                        tsb = (t_sb0, t_sb1)[k % 2]
                        s.wait_ge(s5t, k + 1)
                        s.copy(ssb, psb[:, :]).then_inc(s5s, 1)
                        s.wait_ge(s5v, k + 1)
                        s.activation(g_sb[:, ti, 512 * dh:512 * dh + 512], tsb,
                                     AF.Silu).then_inc(s5e, 1)
                        k += 1

            @block.vector
            def _(v):
                k = 0
                for ti in range(NT):
                    for dh in range(2):
                        peb = (pe0, pe1)[k % 2]
                        ssb = (s_sb0, s_sb1)[k % 2]
                        tsb = (t_sb0, t_sb1)[k % 2]
                        v.wait_ge(s5s, k + 1)
                        v.scalar_tensor_tensor(tsb, peb[:, :], fac_sb[:, ti:ti + 1],
                                               ssb, ALU.mult, ALU.add).then_inc(s5v, 1)
                        k += 1

        with nc.Block() as block:
            @block.sync
            def _(sync):
                sync.dma_start(out=xnfm_out.ap().rearrange("(c p) t -> p c t", p=128),
                               in_=xn_fm[:, :, :]).then_inc(st, 16)
                sync.dma_start(out=g_out.ap().rearrange("(T p) d -> p T d", p=128),
                               in_=(xn_tm if dbg_xn_tm else g_sb)[:, :, :]).then_inc(st, 16)
                sync.wait_ge(st, 33 if ag is not None else 32)
    return nc


def _prep_A_inputs(inputs):
    x = np.asarray(inputs["L0_attn_x"], np.float32).reshape(TTOK, D)
    wst = np.ascontiguousarray(np.asarray(inputs["L0_ffn_W_shared"], np.float32).T)
    wet = np.ascontiguousarray(np.asarray(inputs["L0_ffn_W_expert"], np.float32).T)
    wgate = np.ascontiguousarray(np.asarray(inputs["L0_ffn_W_gate"], np.float32))
    bias = np.ascontiguousarray(np.asarray(inputs["L0_ffn_bias"], np.float32)[None, :])
    ones1 = np.ones((1, 128), np.float32)
    ident = np.eye(128, dtype=np.float32)
    maps = []
    for c in range(NCORES):
        maps.append(dict(x=np.ascontiguousarray(x[TCORE * c:TCORE * (c + 1)]),
                         wgate=wgate, wst=wst, wet=wet, bias=bias,
                         ones1=ones1, ident=ident))
    return maps





SEQB = 8192
NTB = SEQB // 128
NGR = NTB // 4


def _plan_B():
    p = dict(st={}, dg={}, pv={}, t={}, su={}, pr={}, qke={}, ve={},
             et_t={}, steps={})
    nSt = nDg = nPv = nT = nSu = nPr = nQKe = nVe = nEtT = 0
    for pair in range(4):
        for part in range(2):
            for gq in range(16):
                nQKe += 1
                p["qke"][(pair, part, gq)] = nQKe
        for vt in range(NTB):
            nVe += 1
            p["ve"][(pair, vt)] = nVe
        for g in range(NGR - 1, -1, -1):
            nT += 1
            p["t"][(pair, g)] = nT
            nSu += 1
            p["su"][(pair, g)] = nSu
            for j in range(0, 4 * g + 4):
                nSt += 1
                p["st"][(pair, g, j)] = nSt
                diag = j >= 4 * g
                tcons = any(4 * g + r > j for r in range(4))
                if tcons:
                    nEtT += 1
                    p["et_t"][(pair, g, j)] = nEtT
                if diag:
                    r = j - 4 * g
                    nDg += 1
                    p["dg"][(pair, g, r)] = nDg
                    nPv += 1
                    p["pv"][(pair, g, r)] = nPv
                p["steps"][nSt] = (pair, g, j, diag, tcons)
            for r in range(4):
                nPr += 1
                p["pr"][(pair, g, r)] = nPr
    p["nSt"] = nSt
    return p


def build_launch_B():
    nc = bass.Bass("TRN2", target_bir_lowering=False, debug=False,
                   detect_race_conditions=False)
    xh_in = nc.dram_tensor("xh", [128, TTOK], BF16, kind="ExternalInput")
    wqkv_in = nc.dram_tensor("wqkv", [128, 192], F32, kind="ExternalInput")
    bqkv_in = nc.dram_tensor("bqkvc", [64, 3], F32, kind="ExternalInput")
    bv_in = nc.dram_tensor("bvrow", [1, 64], F32, kind="ExternalInput")
    ones1_in = nc.dram_tensor("ones1", [1, 128], F32, kind="ExternalInput")
    onec_in = nc.dram_tensor("onec", [128, 1], F32, kind="ExternalInput")
    msl_in = nc.dram_tensor("msl", [128, 128], mybir.dt.uint8, kind="ExternalInput")
    onet_in = nc.dram_tensor("onet", [128, 128], F32, kind="ExternalInput")
    s_out = nc.dram_tensor("s", [TTOK, 128], BF16, kind="ExternalOutput")


    emit_B(nc, "B_", xh_in, wqkv_in, bqkv_in, bv_in, ones1_in, onec_in, msl_in, onet_in, s_out)
    return nc


def emit_B(nc, PFX, xh_in, wqkv_in, bqkv_in, bv_in, ones1_in, onec_in, msl_in, onet_in, s_out,
           xng=None, wqkvL_in=None, ccA=None):
    Pl = _plan_B()
    gathered = xng is not None
    with Phase(nc, PFX) as ph:
        if gathered:
            xb_sb0 = ph.sbuf("xb_sb0", [128, NC_D, 512], BF16)
            xb_sb1 = ph.sbuf("xb_sb1", [128, NC_D, 512], BF16)
            wL_sb = ph.sbuf("wL_sb", [128, NC_D, 384], BF16)
            ld_b = ph.sem("ld_b")
        else:
            xh_sb = ph.sbuf("xh_sb", [128, 2, SEQB], BF16)
        qk_sb = ph.sbuf("qk_sb", [64, 2 * SEQB], BF16)
        v1_sb = ph.sbuf("v1_sb", [128, NTB, 65], BF16)
        et0_sb = ph.sbuf("et0_sb", [128, 512], BF16)
        et1_sb = ph.sbuf("et1_sb", [128, 512], BF16)
        etd_sb = ph.sbuf("etd_sb", [128, 128], BF16)
        s_sb = ph.sbuf("s_sb", [128, 128, 128], BF16)
        wqkv_sb = ph.sbuf("wqkv_sb", [128, 192], BF16)
        bqkv_sb = ph.sbuf("bqkv_sb", [64, 3], F32)
        bv_sb = ph.sbuf("bv_sb", [1, 64], BF16)
        ones1_sb = ph.sbuf("ones1_sb", [1, 128], BF16)
        ones1f_sb = ph.sbuf("ones1f_sb", [1, 128], F32)
        onec_sb = ph.sbuf("onec_sb", [128, 1], BF16)
        msl_sb = ph.sbuf("msl_sb", [128, 128], mybir.dt.uint8)
        onet_sb = ph.sbuf("onet_sb", [128, 128], BF16)
        suf_sb = ph.sbuf("suf_sb", [1, 8, 65], F32)
        trow_sb = ph.sbuf("trow_sb", [1, 260], F32)
        rz_sb = ph.sbuf("rz_sb", [128, 8], F32)
        sdum_sb = ph.sbuf("sdumB_sb", [128, 2], F32)

        pst = [ph.psum("pst%d" % i, [128, 512], F32) for i in range(3)]
        pst0, pst1 = pst[0], pst[1]
        pp = [ph.psum("pp%d" % i, [128, 512], F32) for i in range(4)]
        pT = ph.psum("pT", [128, 512], F32)

        ld = ph.sem("ld")
        ldx = ph.sem("ldx")
        sQKe = ph.sem("sQKe")
        sVe = ph.sem("sVe")
        sPj = ph.sem("sPj")
        sSt = ph.sem("sSt")
        sEx = ph.sem("sEx")
        sEt = ph.sem("sEt")
        sDg = ph.sem("sDg")
        sPv = ph.sem("sPv")
        sSu = ph.sem("sSu")
        sT = ph.sem("sT")
        sPr = ph.sem("sPr")
        sVx = ph.sem("sVxB")
        st = ph.sem("st")

        with nc.Block() as block:
            @block.sync
            def _(sync):
                if not gathered:
                    sync.dma_start(out=xh_sb[:, 0, :], in_=xh_in[:, 0:SEQB]).then_inc(ldx, 16)
                    sync.dma_start(out=xh_sb[:, 1, :], in_=xh_in[:, SEQB:TTOK]).then_inc(ldx, 16)

            @block.gpsimd
            def _(g):
                if gathered:
                    g.dma_start(out=wL_sb[:, :, :],
                                in_=wqkvL_in.ap().rearrange("(c p) d -> p c d", p=128)).then_inc(ld, 16)
                else:
                    g.dma_start(out=wqkv_sb[:, :], in_=wqkv_in[:, :]).then_inc(ld, 16)
                g.dma_start(out=bqkv_sb[:, :], in_=bqkv_in[:, :]).then_inc(ld, 16)
                g.dma_start(out=bv_sb[:, :], in_=bv_in[:, :]).then_inc(ld, 16)
                g.dma_start(out=ones1_sb[:, :], in_=ones1_in[:, :]).then_inc(ld, 16)
                g.dma_start(out=ones1f_sb[:, :], in_=ones1_in[:, :]).then_inc(ld, 16)
                g.dma_start(out=onec_sb[:, :], in_=onec_in[:, :]).then_inc(ld, 16)
                g.dma_start(out=msl_sb[:, :], in_=msl_in[:, :]).then_inc(ld, 16)
                g.dma_start(out=onet_sb[:, :], in_=onet_in[:, :]).then_inc(ld, 16)

        vsync_state = [0]
        for hl in range(2):
            for b in range(2):
                pair = 2 * hl + b
                hrow = 64 * hl

                with nc.Block() as block:
                    if gathered:
                        @block.sync
                        def _(sync, hl=hl, b=b, pair=pair):
                            if pair == 0 and ccA is not None:
                                sync.wait_ge(ccA, 1)
                            for blk in range(16):
                                n = 16 * pair + blk
                                r = 4 * b + blk // 4
                                co = 512 * (blk % 4)
                                xb = (xb_sb0, xb_sb1)[n % 2]
                                if n >= 2:
                                    pm2 = (n - 2) // 16
                                    bm2 = (n - 2) % 16
                                    sync.wait_ge(sPj, 96 * pm2 + 6 * (bm2 + 1))
                                sync.dma_start(
                                    out=xb[:, :, :],
                                    in_=xng[1024 * r:1024 * (r + 1),
                                            co:co + 512].rearrange(
                                                "(c p) t -> p c t", p=128)
                                ).then_inc(ld_b, 16)

                        @block.tensor
                        def _(t, hl=hl, b=b, pair=pair):
                            if pair == 0:
                                t.wait_ge(ld, 128)
                            k = 0
                            for blk in range(16):
                                n = 16 * pair + blk
                                xb = (xb_sb0, xb_sb1)[n % 2]
                                t.wait_ge(ld_b, 16 * (n + 1))
                                for wc in (0, 128):
                                    pb = (pst0, pst1)[k % 2]
                                    if k >= 2 or pair > 0:
                                        t.wait_ge(sQKe, 32 * pair
                                                  + (k - 1 if k >= 2 else 0))
                                    inst = None
                                    for ci in range(NC_D):
                                        inst = t.matmul(
                                            pb[0:64, :],
                                            wL_sb[:, ci,
                                                  wc + 64 * hl:wc + 64 * hl + 64],
                                            xb[:, ci, :],
                                            start=(ci == 0), stop=(ci == NC_D - 1))
                                    inst.then_inc(sPj, 1)
                                    k += 1
                                for tt in range(4):
                                    vt = 4 * blk + tt
                                    pb = (pp[0], pp[1])[vt % 2]
                                    if vt >= 2:
                                        t.wait_ge(sVe, Pl["ve"][(pair, vt - 2)])
                                    for ci in range(NC_D):
                                        t.matmul(pb[:, 0:64],
                                                 xb[:, ci, 128 * tt:128 * tt + 128],
                                                 wL_sb[:, ci, 256 + 64 * hl:
                                                       320 + 64 * hl],
                                                 start=(ci == 0), stop=False)
                                    inst = t.matmul(pb[:, 0:64], ones1_sb[:, :],
                                                    bv_sb[:, :], start=False,
                                                    stop=True)
                                    inst.then_inc(sPj, 1)

                        @block.scalar
                        def _(s, pair=pair):
                            base = pair * 96
                            for blk in range(16):
                                for half, bcol in ((0, 0), (1, 1)):
                                    k = 2 * blk + half
                                    pb = (pst0, pst1)[k % 2]
                                    s.wait_ge(sPj, base + 6 * blk + half + 1)
                                    s.activation(
                                        qk_sb[0:64, SEQB * half + 512 * blk:
                                              SEQB * half + 512 * blk + 512],
                                        pb[0:64, :], AF.Identity,
                                        bias=bqkv_sb[:, bcol:bcol + 1]
                                    ).then_inc(sQKe, 1)

                        @block.vector
                        def _(v, pair=pair):
                            base = pair * 96
                            v.memset(v1_sb[:, :, 64:65], 1.0)
                            for blk in range(16):
                                for tt in range(4):
                                    vt = 4 * blk + tt
                                    pb = (pp[0], pp[1])[vt % 2]
                                    v.wait_ge(sPj, base + 6 * blk + 2 + tt + 1)
                                    v.tensor_copy(v1_sb[:, vt, 0:64],
                                                  pb[:, 0:64]).then_inc(sVe, 1)
                    else:
                        @block.tensor
                        def _(t, hl=hl, b=b, pair=pair, hrow=hrow):
                            if pair == 0:
                                t.wait_ge(ld, 128)
                                t.wait_ge(ldx, 32)
                            k = 0
                            for part, wc, _bc in ((0, 0, 0), (64, 64, 1)):
                                for gq in range(16):
                                    pb = (pst0, pst1)[k % 2]
                                    if k >= 2:
                                        t.wait_ge(sQKe, Pl["qke"][(pair, (k - 2) // 16,
                                                                  (k - 2) % 16)])
                                    inst = t.matmul(
                                        pb[0:64, :],
                                        wqkv_sb[hrow:hrow + 64, wc:wc + 64],
                                        xh_sb[hrow:hrow + 64, b,
                                              512 * gq:512 * gq + 512],
                                        start=True, stop=True)
                                    inst.then_inc(sPj, 1)
                                    k += 1
                            for vt in range(NTB):
                                pb = (pp[0], pp[1])[vt % 2]
                                if vt >= 2:
                                    t.wait_ge(sVe, Pl["ve"][(pair, vt - 2)])
                                t.matmul(pb[:, 0:64],
                                         xh_sb[hrow:hrow + 64, b,
                                               128 * vt:128 * vt + 128],
                                         wqkv_sb[hrow:hrow + 64, 128:192],
                                         start=True, stop=False)
                                inst = t.matmul(pb[:, 0:64], ones1_sb[:, :],
                                                bv_sb[:, :],
                                                start=False, stop=True)
                                inst.then_inc(sPj, 1)

                        @block.scalar
                        def _(s, pair=pair):
                            base = pair * 96
                            for half, bcol in ((0, 0), (1, 1)):
                                for gq in range(16):
                                    k = half * 16 + gq
                                    pb = (pst0, pst1)[k % 2]
                                    s.wait_ge(sPj, base + k + 1)
                                    s.activation(qk_sb[0:64, SEQB * half + 512 * gq:
                                                       SEQB * half + 512 * gq + 512],
                                                 pb[0:64, :], AF.Identity,
                                                 bias=bqkv_sb[:, bcol:bcol + 1]
                                                 ).then_inc(sQKe, 1)

                        @block.vector
                        def _(v, pair=pair):
                            base = pair * 96
                            v.memset(v1_sb[:, :, 64:65], 1.0)
                            for vt in range(NTB):
                                pb = (pp[0], pp[1])[vt % 2]
                                v.wait_ge(sPj, base + 32 + vt + 1)
                                v.tensor_copy(v1_sb[:, vt, 0:64],
                                              pb[:, 0:64]).then_inc(sVe, 1)

                with nc.Block() as block:
                    @block.tensor
                    def _(t, hl=hl, b=b, pair=pair):
                        def emit_st(g, j):
                            pb = pst[j % 3]
                            me = Pl["st"][(pair, g, j)]
                            if me >= 4:
                                t.wait_ge(sEx, me - 3)
                            t.matmul(
                                pb[:, :],
                                qk_sb[0:64, SEQB + 128 * j:SEQB + 128 * j + 128],
                                qk_sb[0:64, 512 * g:512 * g + 512],
                                start=True, stop=True).then_inc(sSt, 1)

                        for g in range(NGR - 1, -1, -1):
                            inst = None
                            for r, j in enumerate(range(4 * g + 3, 4 * g - 1, -1)):
                                inst = t.matmul(pT[0:1, 65 * r:65 * r + 65],
                                                onec_sb[:, :], v1_sb[:, j, :],
                                                start=True, stop=True)
                            inst.then_inc(sT, 1)
                            if g < NGR - 1:
                                t.wait_ge(sPr, Pl["pr"][(pair, g + 1, 3)])
                            t.wait_ge(sSu, Pl["su"][(pair, g)])
                            for r in range(4):
                                srow = suf_sb[0:1, (4, 3, 2) [r] if r < 3 else 5, :]
                                t.matmul(pp[r][:, 0:65], ones1f_sb[:, :], srow,
                                         start=True, stop=False)
                            emit_st(g, 0)
                            for j in range(0, 4 * g + 4):
                                if j + 1 <= 4 * g + 3:
                                    emit_st(g, j + 1)
                                me = Pl["st"][(pair, g, j)]
                                t.wait_ge(sEx, me)
                                et = (et0_sb, et1_sb)[j % 2]
                                last = None
                                for r in range(4):
                                    if 4 * g + r > j:
                                        last = t.matmul(pp[r][:, 0:65],
                                                        et[:, 128 * r:128 * r + 128],
                                                        v1_sb[:, j, :],
                                                        start=False, stop=False)
                                diag = j >= 4 * g
                                if (pair, g, j) in Pl["et_t"] and not diag:
                                    last.then_inc(sEt, 1)
                                if diag:
                                    r = j - 4 * g
                                    if last is not None:
                                        last.then_inc(sEt, 1)
                                    t.wait_ge(sDg, Pl["dg"][(pair, g, r)])
                                    t.matmul(pp[r][:, 0:65], etd_sb[:, :],
                                             v1_sb[:, j, :], start=False,
                                             stop=True).then_inc(sPv, 1)

                    @block.scalar
                    def _(s, pair=pair):
                        for g in range(NGR - 1, -1, -1):
                            for j in range(0, 4 * g + 4):
                                me = Pl["st"][(pair, g, j)]
                                s.wait_ge(sSt, me)
                                if me >= 3:
                                    opair, og, oj, odiag, otcons = Pl["steps"][me - 2]
                                    if otcons:
                                        s.wait_ge(sEt, Pl["et_t"][(opair, og, oj)])
                                    if odiag:
                                        s.wait_ge(sDg, Pl["dg"][(opair, og,
                                                                oj - 4 * og)])
                                pb = pst[j % 3]
                                s.activation((et0_sb, et1_sb)[j % 2][:, :], pb[:, :],
                                             AF.Exp, scale=0.125).then_inc(sEx, 1)

                    @block.vector
                    def _(v, hl=hl, b=b, pair=pair):
                        ssync = SelfSync(v, sVx, dummy=sdum_sb[:, 0:1])
                        ssync.n = vsync_state[0]
                        run = 0
                        if pair == 0:
                            v.memset(suf_sb[0:1, 0, :], 0.0)
                        else:
                            v.memset(suf_sb[0:1, run, :], 0.0)
                        for g in range(NGR - 1, -1, -1):
                            v.wait_ge(sT, Pl["t"][(pair, g)])
                            inst = v.tensor_copy(trow_sb[0:1, :], pT[0:1, 0:260])
                            ssync.flush(inst)
                            inst = v.tensor_add(suf_sb[0:1, 2, :],
                                                suf_sb[0:1, run, :],
                                                trow_sb[0:1, 0:65])
                            ssync.flush(inst)
                            inst = v.tensor_add(suf_sb[0:1, 3, :], suf_sb[0:1, 2, :],
                                                trow_sb[0:1, 65:130])
                            ssync.flush(inst)
                            inst = v.tensor_add(suf_sb[0:1, 4, :], suf_sb[0:1, 3, :],
                                                trow_sb[0:1, 130:195])
                            ssync.flush(inst)
                            v.tensor_copy(suf_sb[0:1, 5, :], suf_sb[0:1, run, :])
                            inst = v.tensor_add(suf_sb[0:1, 1 - run, :],
                                                suf_sb[0:1, 4, :],
                                                trow_sb[0:1, 195:260])
                            inst.then_inc(sSu, 1)
                            run = 1 - run
                            for j in range(4 * g, 4 * g + 4):
                                r = j - 4 * g
                                et = (et0_sb, et1_sb)[j % 2]
                                v.wait_ge(sEx, Pl["st"][(pair, g, j)])
                                v.select(etd_sb[:, :], msl_sb[:, :],
                                         et[:, 128 * r:128 * r + 128],
                                         onet_sb[:, :]).then_inc(sDg, 1)
                                v.wait_ge(sPv, Pl["pv"][(pair, g, r)])
                                inst = v.reciprocal(rz_sb[:, r:r + 1],
                                                    pp[r][:, 64:65])
                                ssync.flush(inst)
                                gti = NTB * b + 4 * g + r
                                v.tensor_scalar(s_sb[:, gti, 64 * hl:64 * hl + 64],
                                                pp[r][:, 0:64], rz_sb[:, r:r + 1],
                                                None, ALU.mult).then_inc(sPr, 1)
                        vsync_state[0] = ssync.n

        with nc.Block() as block:
            @block.sync
            def _(sync):
                sync.dma_start(out=s_out.ap().rearrange("(T p) c -> p T c", p=128),
                               in_=s_sb[:, :, :]).then_inc(st, 16)
                sync.wait_ge(st, 16)
    return nc


def _prep_B_inputs(xnfm_by_core, inputs):
    xn_full = np.concatenate(xnfm_by_core, axis=1)
    w1 = np.asarray(inputs["L0_attn_Wqkv"], np.float32)
    wqkv = np.ascontiguousarray(np.concatenate([w1, w1], axis=0))
    bq = np.asarray(inputs["L0_attn_bqkv"], np.float32)
    bqkvc = np.stack([bq[0:64], bq[64:128], bq[128:192]], axis=1)
    bvrow = np.ascontiguousarray(bq[None, 128:192])
    ones1 = np.ones((1, 128), np.float32)
    onec = np.ones((128, 1), np.float32)
    jj, ii = np.meshgrid(np.arange(128), np.arange(128), indexing="ij")
    msl = (jj < ii).astype(np.uint8)
    onet = np.ones((128, 128), np.float32)
    maps = []
    for c in range(NCORES):
        maps.append(dict(xh=np.ascontiguousarray(xn_full[128 * c:128 * (c + 1)]),
                         wqkv=wqkv, bqkvc=np.ascontiguousarray(bqkvc),
                         bvrow=bvrow, ones1=ones1, onec=onec, msl=msl, onet=onet))
    return maps


def build_launch_C1():
    nc = bass.Bass("TRN2", target_bir_lowering=False, debug=False,
                   detect_race_conditions=False)
    x_in = nc.dram_tensor("x", [TCORE, D], F32, kind="ExternalInput")
    sfm_in = nc.dram_tensor("sfm", [D, TCORE], BF16, kind="ExternalInput")
    g_in = nc.dram_tensor("g", [TCORE, D], BF16, kind="ExternalInput")
    wr_in = nc.dram_tensor("wr", [D, D], F32, kind="ExternalInput")
    br_in = nc.dram_tensor("br", [1, D], F32, kind="ExternalInput")
    ones1_in = nc.dram_tensor("ones1", [1, 128], F32, kind="ExternalInput")
    ident_in = nc.dram_tensor("ident", [128, 128], F32, kind="ExternalInput")
    x1_out = nc.dram_tensor("x1", [TCORE, D], F32, kind="ExternalOutput")
    xn1fm_out = nc.dram_tensor("xn1_fm", [D, TCORE], BF16, kind="ExternalOutput")

    emit_C1(nc, "C1_", x_in, sfm_in, g_in, wr_in, br_in, ones1_in, ident_in, x1_out, xn1fm_out)
    return nc


def emit_C1(nc, PFX, x_in, sfm_in, g_in, wr_in, br_in, ones1_in, ident_in, x1_out, xn1fm_out, s_t_in=None):
    with Phase(nc, PFX) as ph:
        x1_sb = ph.sbuf("x1_sb", [128, NT, D], F32)
        sfm_sb = ph.sbuf("sfm_sb", [128, NC_D, TCORE], BF16)
        g_sb = ph.sbuf("g_sb", [128, NT, D], BF16)
        xn_tm = ph.sbuf("xn_tm", [128, NT, D], BF16)
        wr_sb = ph.sbuf("wr_sb", [128, NC_D, D], BF16)
        br_sb = ph.sbuf("br_sb", [1, D], BF16)
        ones1_sb = ph.sbuf("ones1_sb", [1, 128], BF16)
        ident_sb = ph.sbuf("ident_sb", [128, 128], BF16)
        eps_sb = ph.sbuf("eps_sb", [128, 1], F32)
        sdum_sb = ph.sbuf("sdum_sb", [128, 2], F32)
        sq0_sb = ph.sbuf("sq0_sb", [128, D], F32)
        sq1_sb = ph.sbuf("sq1_sb", [128, D], F32)
        sums_sb = ph.sbuf("sums_sb", [128, NT], F32)
        sumsq_sb = ph.sbuf("sumsq_sb", [128, NT], F32)
        mean_sb = ph.sbuf("mean_sb", [128, NT], F32)
        var_sb = ph.sbuf("var_sb", [128, NT], F32)
        w0_sb = ph.sbuf("w0_sb", [128, NT], F32)
        w1_sb = ph.sbuf("w1_sb", [128, NT], F32)
        std_sb = ph.sbuf("std_sb", [128, NT], F32)
        rstd_sb = ph.sbuf("rstd_sb", [128, NT], F32)
        xt_sb0 = ph.sbuf("xt_sb0", [128, 512], F32)
        xt_sb1 = ph.sbuf("xt_sb1", [128, 512], F32)
        rg_sb0 = ph.sbuf("rg_sb0", [128, 512], F32)
        rg_sb1 = ph.sbuf("rg_sb1", [128, 512], F32)
        fmst_sb0 = ph.sbuf("fmst_sb0", [128, 512], BF16)
        fmst_sb1 = ph.sbuf("fmst_sb1", [128, 512], BF16)

        pr0 = ph.psum("pr0", [128, 512], F32)
        pr1 = ph.psum("pr1", [128, 512], F32)
        pt0 = ph.psum("pt0", [128, 1024], BF16)
        pt1 = ph.psum("pt1", [128, 1024], BF16)

        ld_x = ph.sem("ld_x")
        ld_w = ph.sem("ld_w")
        ld_t = ph.sem("ld_t")
        sR = ph.sem("sR")
        sC = ph.sem("sC")
        sSq = ph.sem("sSq")
        sVc = ph.sem("sVc")
        sVx = ph.sem("sVx")
        s2a = ph.sem("s2a")
        s2b = ph.sem("s2b")
        s3t = ph.sem("s3t")
        s3e = ph.sem("s3e")
        sFm = ph.sem("sFm")
        st = ph.sem("st")

        with nc.Block() as block:
            @block.sync
            def _(sync):
                if s_t_in is not None:
                    for i in range(8):
                        sync.dma_start(out=sfm_sb[:, i, :],
                                       in_=s_t_in[TCORE * i:TCORE * (i + 1), :],
                                       transpose=True).then_inc(ld_x, 16)
                else:
                    sync.dma_start(out=sfm_sb[:, :, :],
                                   in_=sfm_in.ap().rearrange("(c p) t -> p c t", p=128)).then_inc(ld_x, 16)
                sync.dma_start(out=g_sb[:, :, :],
                               in_=g_in.ap().rearrange("(T p) d -> p T d", p=128)).then_inc(ld_x, 16)

            @block.gpsimd
            def _(g):
                g.dma_start(out=wr_sb[:, :, :],
                            in_=wr_in.ap().rearrange("(c p) d -> p c d", p=128)).then_inc(ld_w, 16)
                g.dma_start(out=br_sb[:, :], in_=br_in[:, :]).then_inc(ld_w, 16)
                g.dma_start(out=ones1_sb[:, :], in_=ones1_in[:, :]).then_inc(ld_w, 16)
                g.dma_start(out=ident_sb[:, :], in_=ident_in[:, :]).then_inc(ld_w, 16)

        xr = x_in.ap().rearrange("(T p) d -> p T d", p=128)
        with nc.Block() as block:
            @block.sync
            def _(sync):
                k = 0
                for ti in range(NT):
                    for dh in range(2):
                        xtb = (xt_sb0, xt_sb1)[k % 2]
                        if k >= 2:
                            sync.wait_ge(sC, k - 1)
                        sync.dma_start(out=xtb[:, :],
                                       in_=xr[:, ti, 512 * dh:512 * dh + 512]).then_inc(ld_t, 16)
                        k += 1

            @block.tensor
            def _(t):
                t.wait_ge(ld_x, 144 if s_t_in is not None else 32)
                t.wait_ge(ld_w, 64)
                k = 0
                for ti in range(NT):
                    for dh in range(2):
                        pb = (pr0, pr1)[k % 2]
                        if k >= 2:
                            t.wait_ge(sC, k - 1)
                        for ci in range(NC_D):
                            t.matmul(pb[:, :], sfm_sb[:, ci, 128 * ti:128 * ti + 128],
                                     wr_sb[:, ci, 512 * dh:512 * dh + 512],
                                     start=(ci == 0), stop=False)
                        inst = t.matmul(pb[:, :], ones1_sb[:, :],
                                        br_sb[:, 512 * dh:512 * dh + 512],
                                        start=False, stop=True)
                        inst.then_inc(sR, 1)
                        k += 1

            @block.vector
            def _(v):
                k = 0
                for ti in range(NT):
                    for dh in range(2):
                        pb = (pr0, pr1)[k % 2]
                        rgb = (rg_sb0, rg_sb1)[k % 2]
                        xtb = (xt_sb0, xt_sb1)[k % 2]
                        v.wait_ge(sR, k + 1)
                        v.wait_ge(ld_t, 16 * (k + 1))
                        v.tensor_mul(rgb[:, :], pb[:, :],
                                     g_sb[:, ti, 512 * dh:512 * dh + 512])
                        v.tensor_add(x1_sb[:, ti, 512 * dh:512 * dh + 512],
                                     xtb[:, :], rgb[:, :]).then_inc(sC, 1)
                        k += 1

        with nc.Block() as block:
            @block.vector
            def _(v):
                ssync = SelfSync(v, sVx, dummy=sdum_sb[:, 0:1])
                v.memset(eps_sb[:, :], 1e-5)
                inst = _ln_stats(v, None, x1_sb, sq0_sb, sq1_sb, sums_sb, sumsq_sb,
                                 mean_sb, var_sb, w0_sb, w1_sb, eps_sb, sSq, sVc, ssync)
                inst.then_inc(s2a, 1)
                v.wait_ge(s2b, 1)
                inst = v.reciprocal(rstd_sb[:, :], std_sb[:, :])
                ssync.flush(inst)
                for ti in range(NT):
                    v.tensor_scalar(xn_tm[:, ti, :], x1_sb[:, ti, :],
                                    mean_sb[:, ti:ti + 1], rstd_sb[:, ti:ti + 1],
                                    ALU.subtract, ALU.mult)

            @block.scalar
            def _(s):
                _ln_stats_scalar(s, x1_sb, sq0_sb, sq1_sb, sSq, sVc)
                s.wait_ge(s2a, 1)
                s.activation(std_sb[:, :], var_sb[:, :], AF.Sqrt,
                             bias=eps_sb[:, 0:1]).then_inc(s2b, 1)

        with nc.Block() as block:
            @block.tensor
            def _(t):
                k = 0
                for ci in range(NC_D):
                    for tg in range(NT // 4):
                        pb = (pt0, pt1)[k % 2]
                        if k >= 2:
                            t.wait_ge(s3e, k - 1)
                        inst = None
                        for j in range(4):
                            ti = 4 * tg + j
                            inst = t.transpose(pb[:, 128 * j:128 * j + 128],
                                               xn_tm[:, ti, 128 * ci:128 * ci + 128],
                                               ident_sb[:, :])
                        inst.then_inc(s3t, 1)
                        k += 1

            @block.scalar
            def _(s):
                k = 0
                for ci in range(NC_D):
                    for tg in range(NT // 4):
                        pb = (pt0, pt1)[k % 2]
                        fst = (fmst_sb0, fmst_sb1)[k % 2]
                        s.wait_ge(s3t, k + 1)
                        if k >= 2:
                            s.wait_ge(sFm, 16 * (k - 1))
                        s.copy(fst[:, :], pb[:, 0:512]).then_inc(s3e, 1)
                        k += 1

            @block.sync
            def _(sync):
                k = 0
                for ci in range(NC_D):
                    for tg in range(NT // 4):
                        fst = (fmst_sb0, fmst_sb1)[k % 2]
                        sync.wait_ge(s3e, k + 1)
                        sync.dma_start(out=xn1fm_out[128 * ci:128 * ci + 128,
                                                     512 * tg:512 * tg + 512],
                                       in_=fst[:, :]).then_inc(sFm, 16)
                        k += 1

        with nc.Block() as block:
            @block.sync
            def _(sync):
                sync.dma_start(out=x1_out.ap().rearrange("(T p) d -> p T d", p=128),
                               in_=x1_sb[:, :, :]).then_inc(st, 16)
                sync.wait_ge(st, 16)
    return nc


LN16 = 2.772588722239781


def build_launch_C2():
    nc = bass.Bass("TRN2", target_bir_lowering=False, debug=False,
                   detect_race_conditions=False)
    xn1_in = nc.dram_tensor("xn1f", [D, TTOK], BF16, kind="ExternalInput")
    wk_in = nc.dram_tensor("wk2h", [D, 128], F32, kind="ExternalInput")
    wv_in = nc.dram_tensor("wv2h", [D, 128], F32, kind="ExternalInput")
    wt_in = nc.dram_tensor("wT", [128, 256], F32, kind="ExternalInput")
    onec_in = nc.dram_tensor("onec", [128, 1], F32, kind="ExternalInput")
    kv_out = nc.dram_tensor("kv4", [1024, 65], F32, kind="ExternalOutput")

    NSTEP = 32

    emit_C2(nc, "C2_", xn1_in, wk_in, wv_in, wt_in, onec_in, kv_out)
    return nc


def emit_C2(nc, PFX, xn1_in, wk_in, wv_in, wt_in, onec_in, kv_out, gathered=False, dup_out=False):
    NSTEP = 32
    with Phase(nc, PFX) as ph:
        xb_sb0 = ph.sbuf("xb_sb0", [128, NC_D, 512], BF16)
        xb_sb1 = ph.sbuf("xb_sb1", [128, NC_D, 512], BF16)
        wk_sb = ph.sbuf("wk_sb", [128, NC_D, 128], BF16)
        wv_sb = ph.sbuf("wv_sb", [128, NC_D, 128], BF16)
        wt_sb = ph.sbuf("wt_sb", [128, 256], BF16)
        onec_sb = ph.sbuf("onec_sb", [128, 1], BF16)
        onecf_sb = ph.sbuf("onecf_sb", [128, 1], F32)
        kf_sb = ph.sbuf("kf_sb", [64, 512], BF16)
        sqk_sb = ph.sbuf("sqk_sb", [64, 512], F32)
        nk_sb = ph.sbuf("nk_sb", [128, 4], F32)
        mln_sb = ph.sbuf("mln_sb", [128, 1], F32)
        pe_sb = ph.sbuf("pe_sb", [128, 2, 256], F32)
        pk_sb = ph.sbuf("pk_sb", [128, 2, 256], BF16)
        v1g_sb = ph.sbuf("v1g_sb", [128, 2, 65], BF16)
        kvo_sb = ph.sbuf("kvo_sb", [128, 8, 65], F32)

        pkv0 = ph.psum("pkv0", [128, 512], F32)
        pkv1 = ph.psum("pkv1", [128, 512], F32)
        pK = ph.psum("pK", [128, 512], F32)
        pP0 = ph.psum("pP0", [128, 512], F32)
        pP1 = ph.psum("pP1", [128, 512], F32)
        pN = ph.psum("pN", [128, 512], F32)
        pV = ph.psum("pV", [128, 512], F32)

        ld_w = ph.sem("ld_w")
        ld_t = ph.sem("ld_t")
        sK = ph.sem("sK")
        sSq2 = ph.sem("sSq2")
        sNm = ph.sem("sNm")
        sNe = ph.sem("sNe")
        sPk = ph.sem("sPk")
        sCl = ph.sem("sCl")
        sEx = ph.sem("sEx")
        sVm = ph.sem("sVm")
        sVe = ph.sem("sVe")
        sKv = ph.sem("sKv")
        sKvE = ph.sem("sKvE")
        st = ph.sem("st")

        with nc.Block() as block:
            @block.gpsimd
            def _(g):
                g.dma_start(out=wk_sb[:, :, :],
                            in_=wk_in.ap().rearrange("(c p) d -> p c d", p=128)).then_inc(ld_w, 16)
                g.dma_start(out=wv_sb[:, :, :],
                            in_=wv_in.ap().rearrange("(c p) d -> p c d", p=128)).then_inc(ld_w, 16)
                g.dma_start(out=wt_sb[:, :], in_=wt_in[:, :]).then_inc(ld_w, 16)
                g.dma_start(out=onec_sb[:, :], in_=onec_in[:, :]).then_inc(ld_w, 16)
                g.dma_start(out=onecf_sb[:, :], in_=onec_in[:, :]).then_inc(ld_w, 16)

        if not gathered:
            xr = xn1_in.ap().rearrange("(c p) t -> p c t", p=128)
        with nc.Block() as block:
            @block.sync
            def _(sync):
                for n in range(2 * NSTEP):
                    hl, g = divmod(n, NSTEP)
                    xb = (xb_sb0, xb_sb1)[n % 2]
                    if n >= 2:
                        sync.wait_ge(sVm, 4 * (n - 1))
                    if gathered:
                        r, co = g // 4, 512 * (g % 4)
                        ap = xn1_in[1024 * r:1024 * (r + 1),
                                    co:co + 512].rearrange("(c p) t -> p c t", p=128)
                    else:
                        ap = xr[:, :, 512 * g:512 * g + 512]
                    sync.dma_start(out=xb[:, :, :], in_=ap).then_inc(ld_t, 16)

            @block.tensor
            def _(t):
                t.wait_ge(ld_w, 80)
                for n in range(2 * NSTEP):
                    hl, g = divmod(n, NSTEP)
                    b = g // 16
                    xb = (xb_sb0, xb_sb1)[n % 2]
                    t.wait_ge(ld_t, 16 * (n + 1))
                    if n >= 1:
                        t.wait_ge(sSq2, n)
                    inst = None
                    for ci in range(NC_D):
                        inst = t.matmul(pK[0:64, :], wk_sb[:, ci, 64 * hl:64 * hl + 64],
                                        xb[:, ci, :], start=(ci == 0), stop=(ci == NC_D - 1))
                    inst.then_inc(sK, 1)
                    t.wait_ge(sSq2, n + 1)
                    if n >= 1:
                        t.wait_ge(sNe, n)
                    inst = None
                    for tt in range(4):
                        inst = t.matmul(pN[:, tt:tt + 1],
                                        sqk_sb[:, 128 * tt:128 * tt + 128],
                                        onecf_sb[0:64, :], start=True, stop=True)
                    inst.then_inc(sNm, 1)
                    for tt in range(4):
                        pp = (pP0, pP1)[tt % 2]
                        if 4 * n + tt >= 2:
                            t.wait_ge(sCl, 4 * n + tt - 1)
                        inst = t.matmul(pp[:, 0:256],
                                        kf_sb[:, 128 * tt:128 * tt + 128],
                                        wt_sb[0:64, :], start=True, stop=True)
                        inst.then_inc(sPk, 1)
                        if 4 * n + tt >= 1:
                            t.wait_ge(sVe, 4 * n + tt)
                        inst = None
                        for ci in range(NC_D):
                            inst = t.matmul(pV[:, 0:64], xb[:, ci, 128 * tt:128 * tt + 128],
                                            wv_sb[:, ci, 64 * hl:64 * hl + 64],
                                            start=(ci == 0), stop=(ci == NC_D - 1))
                        inst.then_inc(sVm, 1)
                        t.wait_ge(sEx, 4 * n + tt + 1)
                        t.wait_ge(sVe, 4 * n + tt + 1)
                        first = (g % 16 == 0 and tt == 0)
                        lastt = (g % 16 == 15 and tt == 3)
                        if first and n > 0:
                            t.wait_ge(sKvE, 2 * (n // 16))
                        for mc in range(2):
                            inst = t.matmul((pkv0, pkv1)[mc][:, 0:65],
                                            pk_sb[:, tt % 2, 128 * mc:128 * mc + 128],
                                            v1g_sb[:, tt % 2, :],
                                            start=first, stop=lastt)
                            inst.then_inc(sKv, 1)

            @block.scalar
            def _(s):
                for n in range(2 * NSTEP):
                    s.wait_ge(sK, n + 1)
                    if n >= 1:
                        s.wait_ge(sPk, 4 * n)
                        s.wait_ge(sNm, n)
                    s.copy(kf_sb[:, :], pK[0:64, :])
                    s.activation(sqk_sb[:, :], kf_sb[:, :], AF.Square).then_inc(sSq2, 1)
                    s.wait_ge(sNm, n + 1)
                    if n >= 1:
                        s.wait_ge(sCl, 4 * n)
                    s.activation(nk_sb[:, 0:4], pN[:, 0:4], AF.Identity,
                                 bias=mln_sb[:, 0:1], scale=-0.5).then_inc(sNe, 1)
                    for tt in range(4):
                        s.wait_ge(sCl, 4 * n + tt + 1)
                        s.activation(pk_sb[:, tt % 2, :], pe_sb[:, tt % 2, :],
                                     AF.Exp).then_inc(sEx, 1)

            @block.vector
            def _(v):
                v.memset(mln_sb[:, :], -LN16)
                v.memset(v1g_sb[:, :, 64:65], 1.0)
                for n in range(2 * NSTEP):
                    hl, g = divmod(n, NSTEP)
                    for tt in range(4):
                        pp = (pP0, pP1)[tt % 2]
                        v.wait_ge(sPk, 4 * n + tt + 1)
                        v.wait_ge(sNe, n + 1)
                        if 4 * n + tt >= 2:
                            v.wait_ge(sEx, 4 * n + tt - 1)
                        v.tensor_scalar(pe_sb[:, tt % 2, :], pp[:, 0:256],
                                        nk_sb[:, tt:tt + 1], -60.0,
                                        ALU.add, ALU.max).then_inc(sCl, 1)
                        v.wait_ge(sVm, 4 * n + tt + 1)
                        if 4 * n + tt >= 2:
                            v.wait_ge(sKv, 2 * (4 * n + tt - 1))
                        v.tensor_copy(v1g_sb[:, tt % 2, 0:64],
                                      pV[:, 0:64]).then_inc(sVe, 1)
                    if g % 16 == 15:
                        bidx = 2 * hl + g // 16
                        v.wait_ge(sKv, 2 * (4 * n + 4))
                        v.tensor_copy(kvo_sb[:, 4 * hl + 2 * (g // 16), :],
                                      pkv0[:, 0:65])
                        v.tensor_copy(kvo_sb[:, 4 * hl + 2 * (g // 16) + 1, :],
                                      pkv1[:, 0:65]).then_inc(sKvE, 2)

        with nc.Block() as block:
            @block.sync
            def _(sync):
                if dup_out:
                    nst = 0
                    kvr = kv_out.ap().rearrange("(R p) c -> p R c", p=128)
                    for j in range(8):
                        b = j // 4
                        for hl in range(2):
                            sync.dma_start(
                                out=kvr[:, 4 * j + 2 * hl:4 * j + 2 * hl + 2, :],
                                in_=kvo_sb[:, 4 * hl + 2 * b:4 * hl + 2 * b + 2, :]
                            ).then_inc(st, 16)
                            nst += 16
                    sync.wait_ge(st, nst)
                else:
                    sync.dma_start(out=kv_out.ap().rearrange("(r p) c -> p r c",
                                                             p=128),
                                   in_=kvo_sb[:, :, :]).then_inc(st, 16)
                    sync.wait_ge(st, 16)
    return nc


def build_launch_C3a():
    nc = bass.Bass("TRN2", target_bir_lowering=False, debug=False,
                   detect_race_conditions=False)
    xn1_in = nc.dram_tensor("xn1fm", [D, TCORE], BF16, kind="ExternalInput")
    kvb_in = nc.dram_tensor("kvb", [4096, 65], F32, kind="ExternalInput")
    wq_in = nc.dram_tensor("wq", [D, D], F32, kind="ExternalInput")
    wt_in = nc.dram_tensor("wT", [128, 256], F32, kind="ExternalInput")
    wrt_in = nc.dram_tensor("wrt", [D, 8], F32, kind="ExternalInput")
    wgt_in = nc.dram_tensor("wgt", [D, 8], F32, kind="ExternalInput")
    onec_in = nc.dram_tensor("onec", [128, 1], F32, kind="ExternalInput")
    ones1_in = nc.dram_tensor("ones1", [1, 128], F32, kind="ExternalInput")
    s1_out = nc.dram_tensor("s1", [TCORE, D], BF16, kind="ExternalOutput")
    cw_out = nc.dram_tensor("cw", [TCORE, 8], F32, kind="ExternalOutput")

    emit_C3a(nc, "C3a_", xn1_in, kvb_in, wq_in, wt_in, wrt_in, wgt_in, onec_in, ones1_in, s1_out, cw_out)
    return nc


def emit_C3a(nc, PFX, xn1_in, kvb_in, wq_in, wt_in, wrt_in, wgt_in, onec_in, ones1_in, s1_out, cw_out, kv_a2a=False):
    with Phase(nc, PFX) as ph:
        xn1_sb = ph.sbuf("xn1_sb", [128, NC_D, TCORE], BF16)
        q_sb = ph.sbuf("q_sb", [128, NC_D, TCORE], BF16)
        s1_sb = ph.sbuf("s1_sb", [128, NT, D], BF16)
        kv_sb = ph.sbuf("kv_sb", [128, 32, 65], BF16)
        wq_sb = ph.sbuf("wq_sb", [128, NC_D, D], BF16)
        wt_sb = ph.sbuf("wt_sb", [128, 256], BF16)
        wrg_sb = ph.sbuf("wrg_sb", [128, NC_D, 16], BF16)
        onecf_sb = ph.sbuf("onecf_sb", [128, 1], F32)
        ones1f_sb = ph.sbuf("ones1f_sb", [1, 128], F32)
        sqq_sb = ph.sbuf("sqq_sb", [128, 2048], F32)
        nq_sb = ph.sbuf("nq_sb", [1, 2048], F32)
        pqe_sb = ph.sbuf("pqe_sb", [128, 2, 512], F32)
        pq_sb = ph.sbuf("pq_sb", [128, 2, 2048], BF16)
        rz_sb = ph.sbuf("rz_sb", [128, NT], F32)
        zc_sb = ph.sbuf("zc_sb", [128, NT], F32)
        lgr_sb = ph.sbuf("lgr_sb", [128, NT, 8], F32)
        lgg_sb = ph.sbuf("lgg_sb", [128, NT, 8], F32)
        er_sb = ph.sbuf("er_sb", [128, NT, 8], F32)
        eg_sb = ph.sbuf("eg_sb", [128, NT, 8], F32)
        cw_sb = ph.sbuf("cw_sb", [128, NT, 8], F32)
        rw_sb = ph.sbuf("rw_sb", [128, NT, 12], F32)
        sdum_sb = ph.sbuf("sdum_sb", [128, 2], F32)

        pq0 = ph.psum("pq0", [128, 512], F32)
        pq1 = ph.psum("pq1", [128, 512], F32)
        pn0 = ph.psum("pn0", [128, 512], F32)
        pn1 = ph.psum("pn1", [128, 512], F32)
        po0 = ph.psum("po0", [128, 512], F32)
        po1 = ph.psum("po1", [128, 512], F32)
        plg = ph.psum("plg", [128, 512], F32)

        ld_x = ph.sem("ld_x")
        ld_w = ph.sem("ld_w")
        sQm = ph.sem("sQm")
        sQe = ph.sem("sQe")
        sSqm = ph.sem("sSqm")
        sNm = ph.sem("sNm")
        sNe = ph.sem("sNe")
        sPm = ph.sem("sPm")
        sCl = ph.sem("sCl")
        sEx = ph.sem("sEx")
        sOm = ph.sem("sOm")
        sOe = ph.sem("sOe")
        sLm = ph.sem("sLm")
        sLv = ph.sem("sLv")
        sVx = ph.sem("sVx")
        sSc = ph.sem("sSc")
        st = ph.sem("st")

        with nc.Block() as block:
            @block.sync
            def _(sync):
                sync.dma_start(out=xn1_sb[:, :, :],
                               in_=xn1_in.ap().rearrange("(c p) t -> p c t", p=128)).then_inc(ld_x, 16)

            @block.gpsimd
            def _(g):
                g.dma_start(out=kv_sb[:, :, :],
                            in_=kvb_in.ap().rearrange("(r p) c -> p r c", p=128)).then_inc(ld_w, 16)
                g.dma_start(out=wq_sb[:, :, :],
                            in_=wq_in.ap().rearrange("(c p) d -> p c d", p=128)).then_inc(ld_w, 16)
                g.dma_start(out=wt_sb[:, :], in_=wt_in[:, :]).then_inc(ld_w, 16)
                g.dma_start(out=wrg_sb[:, :, 0:8],
                            in_=wrt_in.ap().rearrange("(c p) d -> p c d", p=128)).then_inc(ld_w, 16)
                g.dma_start(out=wrg_sb[:, :, 8:16],
                            in_=wgt_in.ap().rearrange("(c p) d -> p c d", p=128)).then_inc(ld_w, 16)
                g.dma_start(out=onecf_sb[:, :], in_=onec_in[:, :]).then_inc(ld_w, 16)
                g.dma_start(out=ones1f_sb[:, :], in_=ones1_in[:, :]).then_inc(ld_w, 16)

        with nc.Block() as block:
            @block.tensor
            def _(t):
                t.wait_ge(ld_x, 16)
                t.wait_ge(ld_w, 112)
                k = 0
                for dc in range(NC_D):
                    for tg in range(4):
                        pb = (pq0, pq1)[k % 2]
                        if k >= 2:
                            t.wait_ge(sQe, k - 1)
                        inst = None
                        for ci in range(NC_D):
                            inst = t.matmul(pb[:, :],
                                            wq_sb[:, ci, 128 * dc:128 * dc + 128],
                                            xn1_sb[:, ci, 512 * tg:512 * tg + 512],
                                            start=(ci == 0), stop=(ci == NC_D - 1))
                        inst.then_inc(sQm, 1)
                        k += 1

            @block.scalar
            def _(s):
                k = 0
                for dc in range(NC_D):
                    for tg in range(4):
                        pb = (pq0, pq1)[k % 2]
                        s.wait_ge(sQm, k + 1)
                        s.copy(q_sb[:, dc, 512 * tg:512 * tg + 512],
                               pb[:, :]).then_inc(sQe, 1)
                        k += 1

        for h in range(16):
            dc = h // 2
            base = 64 * (h % 2)
            with nc.Block() as block:
                @block.tensor
                def _(t, h=h, dc=dc, base=base):
                    t.wait_ge(sSqm, h + 1)
                    for tg in range(4):
                        pnb = (pn0, pn1)[tg % 2]
                        if 4 * h + tg >= 2:
                            t.wait_ge(sNe, 4 * h + tg - 1)
                        inst = t.matmul(pnb[0:1, 0:512],
                                        onecf_sb[base:base + 64, :],
                                        sqq_sb[base:base + 64, 512 * tg:512 * tg + 512],
                                        start=True, stop=True)
                        inst.then_inc(sNm, 1)
                    k = 0
                    for mc in range(2):
                        for tg in range(4):
                            pb = (pq0, pq1)[k % 2]
                            if 8 * h + k >= 2:
                                t.wait_ge(sCl, 8 * h + k - 1)
                            t.wait_ge(sNe, 4 * h + tg + 1)
                            t.matmul(pb[:, :],
                                     wt_sb[base:base + 64, 128 * mc:128 * mc + 128],
                                     q_sb[base:base + 64, dc, 512 * tg:512 * tg + 512],
                                     start=True, stop=False)
                            inst = t.matmul(pb[:, :], ones1f_sb[:, :],
                                            nq_sb[:, 512 * tg:512 * tg + 512],
                                            start=False, stop=True)
                            inst.then_inc(sPm, 1)
                            k += 1
                    for tt in range(NT):
                        pb = (po0, po1)[tt % 2]
                        if 16 * h + tt >= 2:
                            t.wait_ge(sOe, 16 * h + tt - 1)
                        inst = None
                        for mc in range(2):
                            t.wait_ge(sEx, 8 * h + 4 * mc + tt // 4 + 1)
                            kvr = (4 * (h // 2) + 2 * (h % 2) + mc) if kv_a2a \
                                else (2 * h + mc)
                            inst = t.matmul(pb[:, 0:65],
                                            pq_sb[:, mc, 128 * tt:128 * tt + 128],
                                            kv_sb[:, kvr, :],
                                            start=(mc == 0), stop=(mc == 1))
                        inst.then_inc(sOm, 1)

                @block.scalar
                def _(s, h=h, dc=dc, base=base):
                    if h >= 1:
                        s.wait_ge(sNm, 4 * h)
                    s.activation(sqq_sb[base:base + 64, :],
                                 q_sb[base:base + 64, dc, :],
                                 AF.Square).then_inc(sSqm, 1)
                    for tg in range(4):
                        pnb = (pn0, pn1)[tg % 2]
                        s.wait_ge(sNm, 4 * h + tg + 1)
                        if h >= 1:
                            s.wait_ge(sPm, 8 * (h - 1) + 4 + tg + 1)
                        s.activation(nq_sb[0:1, 512 * tg:512 * tg + 512],
                                     pnb[0:1, 0:512], AF.Identity,
                                     scale=-0.5).then_inc(sNe, 1)
                    if h >= 1:
                        s.wait_ge(sOm, 16 * h)
                    for k in range(8):
                        mc, tg = divmod(k, 4)
                        s.wait_ge(sCl, 8 * h + k + 1)
                        s.activation(pq_sb[:, mc, 512 * tg:512 * tg + 512],
                                     pqe_sb[:, k % 2, :], AF.Exp).then_inc(sEx, 1)

                @block.vector
                def _(v, h=h, dc=dc, base=base):
                    for k in range(8):
                        pb = (pq0, pq1)[k % 2]
                        v.wait_ge(sPm, 8 * h + k + 1)
                        if 8 * h + k >= 2:
                            v.wait_ge(sEx, 8 * h + k - 1)
                        v.tensor_scalar(pqe_sb[:, k % 2, :], pb[:, :],
                                        -LN16, -60.0,
                                        ALU.add, ALU.max).then_inc(sCl, 1)
                    ssync = SelfSync(v, sVx, dummy=sdum_sb[:, 0:1])
                    ssync.n = 2 * 16 * h
                    for tt in range(NT):
                        pb = (po0, po1)[tt % 2]
                        v.wait_ge(sOm, 16 * h + tt + 1)
                        inst = v.tensor_scalar(zc_sb[:, tt:tt + 1], pb[:, 64:65],
                                               1e-6, None, ALU.add)
                        ssync.flush(inst)
                        inst = v.reciprocal(rz_sb[:, tt:tt + 1], zc_sb[:, tt:tt + 1])
                        ssync.flush(inst)
                        v.tensor_scalar(s1_sb[:, tt, 64 * h:64 * h + 64],
                                        pb[:, 0:64], rz_sb[:, tt:tt + 1],
                                        None, ALU.mult).then_inc(sOe, 1)

        with nc.Block() as block:
            @block.tensor
            def _(t):
                for ti in range(NT):
                    if ti >= 1:
                        t.wait_ge(sLv, ti)
                    inst = None
                    for ci in range(NC_D):
                        inst = t.matmul(plg[:, 0:16],
                                        xn1_sb[:, ci, 128 * ti:128 * ti + 128],
                                        wrg_sb[:, ci, :], start=(ci == 0),
                                        stop=(ci == NC_D - 1))
                    inst.then_inc(sLm, 1)

            @block.vector
            def _(v):
                ssync = SelfSync(v, sVx, dummy=sdum_sb[:, 1:2])
                ssync.n = 2 * 16 * 16
                for ti in range(NT):
                    v.wait_ge(sLm, ti + 1)
                    v.tensor_copy(lgr_sb[:, ti, :], plg[:, 0:8])
                    v.tensor_copy(lgg_sb[:, ti, :], plg[:, 8:16]).then_inc(sLv, 1)
                ssync.barrier()
                v.reduce_max(rw_sb[:, :, 0], lgr_sb[:, :, :], AX.X, negate=True)
                inst = v.reduce_max(rw_sb[:, :, 1], lgg_sb[:, :, :], AX.X, negate=True)
                ssync.flush(inst)
                inst = v.tensor_scalar(rw_sb[:, :, 2], rw_sb[:, :, 1], 10.0, None,
                                       ALU.mult)
                ssync.flush(inst)
                v.nop().then_inc(sLv, 1)
                v.wait_ge(sSc, 32)
                inst = v.reduce_sum(rw_sb[:, :, 3], er_sb[:, :, :], AX.X)
                inst2 = v.reduce_sum(rw_sb[:, :, 4], eg_sb[:, :, :], AX.X)
                ssync.flush(inst2)
                v.reciprocal(rw_sb[:, :, 5], rw_sb[:, :, 3])
                inst = v.reciprocal(rw_sb[:, :, 6], rw_sb[:, :, 4])
                ssync.flush(inst)
                for ti in range(NT):
                    v.tensor_scalar(er_sb[:, ti, :], er_sb[:, ti, :],
                                    rw_sb[:, ti, 5:6], None, ALU.mult)
                    v.tensor_scalar(eg_sb[:, ti, :], eg_sb[:, ti, :],
                                    rw_sb[:, ti, 6:7], None, ALU.mult)
                ssync.barrier()
                inst = v.tensor_mul(cw_sb[:, :, :], er_sb[:, :, :], eg_sb[:, :, :])
                ssync.flush(inst)
                inst = v.reduce_sum(rw_sb[:, :, 7], cw_sb[:, :, :], AX.X)
                ssync.flush(inst)
                inst = v.tensor_scalar(rw_sb[:, :, 8], rw_sb[:, :, 7], 1e-9, None,
                                       ALU.add)
                ssync.flush(inst)
                inst = v.reciprocal(rw_sb[:, :, 9], rw_sb[:, :, 8])
                ssync.flush(inst)
                for ti in range(NT):
                    v.tensor_scalar(cw_sb[:, ti, :], cw_sb[:, ti, :],
                                    rw_sb[:, ti, 9:10], None, ALU.mult)

            @block.scalar
            def _(s):
                s.wait_ge(sLv, NT + 1)
                for ti in range(NT):
                    s.activation(er_sb[:, ti, :], lgr_sb[:, ti, :], AF.Exp,
                                 bias=rw_sb[:, ti, 0:1]).then_inc(sSc, 1)
                for ti in range(NT):
                    s.activation(eg_sb[:, ti, :], lgg_sb[:, ti, :], AF.Exp,
                                 bias=rw_sb[:, ti, 2:3], scale=10.0).then_inc(sSc, 1)

        with nc.Block() as block:
            @block.sync
            def _(sync):
                sync.dma_start(out=s1_out.ap().rearrange("(T p) d -> p T d", p=128),
                               in_=s1_sb[:, :, :]).then_inc(st, 16)
                sync.dma_start(out=cw_out.ap().rearrange("(T p) d -> p T d", p=128),
                               in_=cw_sb[:, :, :]).then_inc(st, 16)
                sync.wait_ge(st, 32)
    return nc


def build_launch_C3b():
    nc = bass.Bass("TRN2", target_bir_lowering=False, debug=False,
                   detect_race_conditions=False)
    xn1_in = nc.dram_tensor("xn1fm", [D, TCORE], BF16, kind="ExternalInput")
    x1_in = nc.dram_tensor("x1", [TCORE, D], F32, kind="ExternalInput")
    s1_in = nc.dram_tensor("s1", [TCORE, D], BF16, kind="ExternalInput")
    cw_in = nc.dram_tensor("cw", [TCORE, 8], F32, kind="ExternalInput")
    lin_in = nc.dram_tensor("lin", [8, 32, 128, 3072], F32, kind="ExternalInput")
    x2_out = nc.dram_tensor("x2", [TCORE, D], F32, kind="ExternalOutput")

    HT = 1024
    NTT = HT // 128

    emit_C3b(nc, "C3b_", xn1_in, x1_in, s1_in, cw_in, lin_in, x2_out)
    return nc


def emit_C3b(nc, PFX, xn1_in, x1_in, s1_in, cw_in, lin_in, x2_out):
    HT = 1024
    NTT = HT // 128
    with Phase(nc, PFX) as ph:
        xn1_sb = ph.sbuf("xn1_sb", [128, NC_D, TCORE], BF16)
        s1t_sb0 = ph.sbuf("s1t_sb0", [128, D], BF16)
        s1t_sb1 = ph.sbuf("s1t_sb1", [128, D], BF16)
        cw_sb = ph.sbuf("cw_sb", [128, NT, 8], F32)
        h_sb = ph.sbuf("h_sb", [128, 32, HT], BF16)
        acc_sb = ph.sbuf("acc_sb", [128, NTT, D], F32)
        wbuf_sb = ph.sbuf("wbuf_sb", [128, 2, 4, 2048], BF16)
        wdb_sb = ph.sbuf("wdb_sb", [128, 2, 4, 512], BF16)
        sg_sb0 = ph.sbuf("sg_sb0", [128, 512], BF16)
        sg_sb1 = ph.sbuf("sg_sb1", [128, 512], BF16)
        g1_sb0 = ph.sbuf("g1_sb0", [128, D], BF16)
        g1_sb1 = ph.sbuf("g1_sb1", [128, D], BF16)
        x1t_sb0 = ph.sbuf("x1t_sb0", [128, D], F32)
        x1t_sb1 = ph.sbuf("x1t_sb1", [128, D], F32)
        sg1_sb0 = ph.sbuf("sg1_sb0", [128, D], F32)
        sg1_sb1 = ph.sbuf("sg1_sb1", [128, D], F32)
        out_sb0 = ph.sbuf("out_sb0", [128, D], F32)
        out_sb1 = ph.sbuf("out_sb1", [128, D], F32)

        p8 = [ph.psum("p8_%d" % i, [128, 512], F32) for i in range(8)]

        ld_x = ph.sem("ld_x")
        ld_t = ph.sem("ld_t")
        ld_d = ph.sem("ld_d")
        ld_1 = ph.sem("ld_1")
        sGm = ph.sem("sGm")
        sUm = ph.sem("sUm")
        sSi = ph.sem("sSi")
        sGU = ph.sem("sGU")
        sDm = ph.sem("sDm")
        sDe = ph.sem("sDe")
        sG1 = ph.sem("sG1")
        sCv = ph.sem("sCv")
        sCm = ph.sem("sCm")
        st = ph.sem("st")

        with nc.Block() as block:
            @block.sync
            def _(sync):
                sync.dma_start(out=xn1_sb[:, :, :],
                               in_=xn1_in.ap().rearrange("(c p) t -> p c t", p=128)).then_inc(ld_x, 16)
                sync.dma_start(out=cw_sb[:, :, :],
                               in_=cw_in.ap().rearrange("(T p) d -> p T d", p=128)).then_inc(ld_x, 16)

        nqs = [0]
        nwd = [0]
        nx1 = [0]

        for hf in range(2):
            q = 8 * hf
            with nc.Block() as block:
                @block.gpsimd
                def _(g, hf=hf):
                    for e in range(8):
                        q = 8 * hf + e
                        for fg in range(8):
                            glk = (32 * q) // 4 + fg
                            if glk >= 2:
                                g.wait_ge(sGm, 8 * glk - 8)
                            g.dma_start(out=wbuf_sb[:, glk % 2, :, :],
                                        in_=lin_in[e, 4 * fg:4 * fg + 4, :,
                                                   0:2048].rearrange("s p d -> p s d")
                                        ).then_inc(ld_t, 16)
                        for dh in range(2):
                            for wg4 in range(8):
                                G = (64 * q + 32 * dh) // 4 + wg4
                                if G >= 2:
                                    g.wait_ge(sDm, 4 * G - 4)
                                g.dma_start(out=wdb_sb[:, G % 2, :, :],
                                            in_=lin_in[e, 4 * wg4:4 * wg4 + 4, :,
                                                       2048 + 512 * dh:2560 + 512 * dh
                                                       ].rearrange("s p d -> p s d")
                                            ).then_inc(ld_d, 16)

                @block.sync
                def _(sync, hf=hf):
                    for tt in range(NTT):
                        k = NTT * hf + tt
                        if k >= 2:
                            sync.wait_ge(sCv, k - 1)
                            tiP = k - 2
                            sync.dma_start(
                                out=x2_out.ap().rearrange("(T p) d -> p T d",
                                                          p=128)[:, tiP, :],
                                in_=(out_sb0, out_sb1)[tiP % 2][:, :]).then_inc(st, 16)
                        sync.dma_start(
                            out=(x1t_sb0, x1t_sb1)[tt % 2][:, :],
                            in_=x1_in.ap().rearrange("(T p) d -> p T d", p=128)[:, k, :]
                        ).then_inc(ld_1, 16)
                        sync.dma_start(
                            out=(s1t_sb0, s1t_sb1)[tt % 2][:, :],
                            in_=s1_in.ap().rearrange("(T p) d -> p T d", p=128)[:, k, :]
                        ).then_inc(ld_1, 16)
                    for tt in (NTT - 2, NTT - 1):
                        k = NTT * hf + tt
                        sync.wait_ge(sCv, k + 1)
                        sync.dma_start(
                            out=x2_out.ap().rearrange("(T p) d -> p T d", p=128)[:, k, :],
                            in_=(out_sb0, out_sb1)[tt % 2][:, :]).then_inc(st, 16)
                    sync.wait_ge(st, 16 * NTT * (hf + 1))

                @block.tensor
                def _(t, hf=hf):
                    if hf == 0:
                        t.wait_ge(ld_x, 32)
                    for e in range(8):
                        q = 8 * hf + e
                        for fs in range(32):
                            k = 32 * q + fs
                            glk = k // 4
                            t.wait_ge(ld_t, 16 * (glk + 1))
                            for tg in range(2):
                                kk = 2 * k + tg
                                pg = p8[(kk % 2) * 2]
                                pu = p8[(kk % 2) * 2 + 1]
                                if kk >= 2:
                                    t.wait_ge(sSi, kk - 1)
                                    t.wait_ge(sGU, kk - 1)
                                if fs == 0 and tg == 0 and q > 0:
                                    t.wait_ge(sDe, 16 * q)
                                for ci in range(NC_D):
                                    t.matmul(pg[:, :],
                                             wbuf_sb[:, glk % 2, fs % 4,
                                                     128 * ci:128 * ci + 128],
                                             xn1_sb[:, ci, HT * hf + 512 * tg:
                                                    HT * hf + 512 * tg + 512],
                                             start=(ci == 0), stop=(ci == NC_D - 1))
                                inst = None
                                for ci in range(NC_D):
                                    inst = t.matmul(pu[:, :],
                                                    wbuf_sb[:, glk % 2, fs % 4,
                                                            1024 + 128 * ci:1152 + 128 * ci],
                                                    xn1_sb[:, ci, HT * hf + 512 * tg:
                                                           HT * hf + 512 * tg + 512],
                                                    start=(ci == 0), stop=(ci == NC_D - 1))
                                inst.then_inc(sGm, 1)
                        t.wait_ge(sGU, 64 * (q + 1))
                        t.wait_ge(sSi, 64 * (q + 1))
                        for dh in range(2):
                            if dh == 1:
                                t.wait_ge(sDe, 16 * q + 8)
                            for fs in range(32):
                                kd = 64 * q + 32 * dh + fs
                                G = kd // 4
                                t.wait_ge(ld_d, 16 * (G + 1))
                                inst = None
                                for tt in range(NTT):
                                    inst = t.matmul(p8[tt][:, :],
                                                    h_sb[:, fs, 128 * tt:128 * tt + 128],
                                                    wdb_sb[:, G % 2, fs % 4, :],
                                                    start=(fs == 0), stop=(fs == 31))
                                inst.then_inc(sDm, 1)

                @block.scalar
                def _(s, hf=hf):
                    for e in range(8):
                        q = 8 * hf + e
                        for fs in range(32):
                            for tg in range(2):
                                kk = 2 * (32 * q + fs) + tg
                                pg = p8[(kk % 2) * 2]
                                sg = (sg_sb0, sg_sb1)[kk % 2]
                                s.wait_ge(sGm, kk + 1)
                                if kk >= 2:
                                    s.wait_ge(sGU, kk - 1)
                                s.activation(sg[:, :], pg[:, :],
                                             AF.Silu).then_inc(sSi, 1)
                    for tt in range(NTT):
                        k = NTT * hf + tt
                        g1 = (g1_sb0, g1_sb1)[tt % 2]
                        s.wait_ge(sDe, 16 * 8 * (hf + 1))
                        if k >= 2:
                            s.wait_ge(sCv, k - 1)
                        s.activation(g1[:, :], acc_sb[:, tt, :],
                                     AF.Silu).then_inc(sG1, 1)

                @block.vector
                def _(v, hf=hf):
                    if hf == 0:
                        v.wait_ge(ld_x, 32)
                    for e in range(8):
                        q = 8 * hf + e
                        for fs in range(32):
                            for tg in range(2):
                                kk = 2 * (32 * q + fs) + tg
                                pu = p8[(kk % 2) * 2 + 1]
                                sg = (sg_sb0, sg_sb1)[kk % 2]
                                v.wait_ge(sSi, kk + 1)
                                v.tensor_mul(h_sb[:, fs, 512 * tg:512 * tg + 512],
                                             sg[:, :], pu[:, :]).then_inc(sGU, 1)
                        for dh in range(2):
                            for tt in range(NTT):
                                v.wait_ge(sDm, 64 * q + 32 * dh + 32)
                                ti = NTT * hf + tt
                                if e == 0:
                                    v.tensor_scalar(acc_sb[:, tt, 512 * dh:512 * dh + 512],
                                                    p8[tt][:, :], cw_sb[:, ti, 0:1],
                                                    None, ALU.mult).then_inc(sDe, 1)
                                else:
                                    v.scalar_tensor_tensor(
                                        acc_sb[:, tt, 512 * dh:512 * dh + 512],
                                        p8[tt][:, :], cw_sb[:, ti, e:e + 1],
                                        acc_sb[:, tt, 512 * dh:512 * dh + 512],
                                        ALU.mult, ALU.add).then_inc(sDe, 1)
                    for tt in range(NTT):
                        k = NTT * hf + tt
                        ti = NTT * hf + tt
                        g1 = (g1_sb0, g1_sb1)[tt % 2]
                        sg1 = (sg1_sb0, sg1_sb1)[tt % 2]
                        x1t = (x1t_sb0, x1t_sb1)[tt % 2]
                        outb = (out_sb0, out_sb1)[tt % 2]
                        s1t = (s1t_sb0, s1t_sb1)[tt % 2]
                        v.wait_ge(sG1, k + 1)
                        v.wait_ge(ld_1, 32 * (k + 1))
                        if k >= 2:
                            v.wait_ge(st, 16 * (k - 1))
                        inst = v.tensor_mul(sg1[:, :], s1t[:, :], g1[:, :])
                        inst.then_inc(sCm, 1)
                        v.wait_ge(sCm, k + 1)
                        v.tensor_add(outb[:, :], x1t[:, :], sg1[:, :]).then_inc(sCv, 1)

    return nc




_NC_CACHE = {}


def _nc(name, builder):
    if name not in _NC_CACHE:
        _NC_CACHE[name] = builder()
    return _NC_CACHE[name]


def _pack_lin(wg, wu, wd):
    lin = np.empty((8, 32, 128, 3072), np.float32)
    for e in range(8):
        ge = np.asarray(wg[e], np.float32).reshape(8, 128, 4096)
        ue = np.asarray(wu[e], np.float32).reshape(8, 128, 4096)
        de = np.asarray(wd[e], np.float32)
        for fs in range(32):
            lin[e, fs, :, 0:1024] = ge[:, :, 128 * fs:128 * fs + 128].transpose(
                1, 0, 2).reshape(128, 1024)
            lin[e, fs, :, 1024:2048] = ue[:, :, 128 * fs:128 * fs + 128].transpose(
                1, 0, 2).reshape(128, 1024)
            lin[e, fs, :, 2048:3072] = de[128 * fs:128 * fs + 128, :]
    return lin


def kernel(**inputs):
    return kernel_fused(**inputs)


def kernel_unfused(**inputs):
    cores = list(range(NCORES))
    ones1 = np.ones((1, 128), np.float32)
    onec = np.ones((128, 1), np.float32)
    ident = np.eye(128, dtype=np.float32)

    resA = run_bass_kernel_spmd(_nc("A", build_launch_A), _prep_A_inputs(inputs),
                                core_ids=cores)
    xnfm = [r["xn_fm"] for r in resA.results]
    g0 = [r["g"] for r in resA.results]

    resB = run_bass_kernel_spmd(_nc("B", build_launch_B),
                                _prep_B_inputs(xnfm, inputs), core_ids=cores)
    sB = [r["s"] for r in resB.results]

    x = np.asarray(inputs["L0_attn_x"], np.float32).reshape(TTOK, D)
    wr = np.ascontiguousarray(np.asarray(inputs["L0_attn_Wr"], np.float32))
    br = np.ascontiguousarray(np.asarray(inputs["L0_attn_br"], np.float32)[None, :])
    mapsC1 = []
    for c in cores:
        s_own = np.concatenate([sB[i][TCORE * c:TCORE * (c + 1), :] for i in cores],
                               axis=1)
        mapsC1.append(dict(x=np.ascontiguousarray(x[TCORE * c:TCORE * (c + 1)]),
                           sfm=np.ascontiguousarray(s_own.T), g=g0[c], wr=wr, br=br,
                           ones1=ones1, ident=ident))
    resC1 = run_bass_kernel_spmd(_nc("C1", build_launch_C1), mapsC1, core_ids=cores)
    x1 = [r["x1"] for r in resC1.results]
    xn1fm = [r["xn1_fm"] for r in resC1.results]

    xn1_full = np.ascontiguousarray(np.concatenate(xn1fm, axis=1))
    wk = np.asarray(inputs["L1_attn_Wk"], np.float32)
    wv = np.asarray(inputs["L1_attn_Wv"], np.float32)
    omega = np.asarray(inputs["L1_attn_omega"], np.float32)
    wTd = np.ascontiguousarray(np.concatenate([omega.T, omega.T], 0))
    mapsC2 = []
    for c in cores:
        mapsC2.append(dict(xn1f=xn1_full,
                           wk2h=np.ascontiguousarray(wk[:, 128 * c:128 * c + 128]),
                           wv2h=np.ascontiguousarray(wv[:, 128 * c:128 * c + 128]),
                           wT=wTd, onec=onec))
    resC2 = run_bass_kernel_spmd(_nc("C2", build_launch_C2), mapsC2, core_ids=cores)
    kv4 = [r["kv4"] for r in resC2.results]

    wq = np.ascontiguousarray(np.asarray(inputs["L1_attn_Wq"], np.float32))
    wrt = np.ascontiguousarray(np.asarray(inputs["L1_ffn_w_router"], np.float32))
    wgt = np.ascontiguousarray(np.asarray(inputs["L1_ffn_w_gating"], np.float32))
    mapsC3a = []
    for c in cores:
        b = c // 4
        kvb = np.concatenate(
            [kv4[h // 2][512 * (h % 2) + 256 * b + 128 * mc:
                         512 * (h % 2) + 256 * b + 128 * mc + 128, :]
             for h in range(16) for mc in range(2)], axis=0)
        mapsC3a.append(dict(xn1fm=xn1fm[c], kvb=np.ascontiguousarray(kvb), wq=wq,
                            wT=wTd, wrt=wrt, wgt=wgt, onec=onec, ones1=ones1))
    resC3a = run_bass_kernel_spmd(_nc("C3a", build_launch_C3a), mapsC3a,
                                  core_ids=cores)
    s1 = [r["s1"] for r in resC3a.results]
    cw = [r["cw"] for r in resC3a.results]

    lin = _pack_lin(inputs["L1_ffn_w_gate"], inputs["L1_ffn_w_up"],
                    inputs["L1_ffn_w_down"])
    mapsC3b = [dict(xn1fm=xn1fm[c], x1=x1[c], s1=s1[c], cw=cw[c], lin=lin)
               for c in cores]
    resC3b = run_bass_kernel_spmd(_nc("C3b", build_launch_C3b), mapsC3b,
                                  core_ids=cores)
    out = np.concatenate([r["x2"] for r in resC3b.results], axis=0)
    return np.ascontiguousarray(out.reshape(2, SEQB, D), dtype=np.float32)


from concourse.aot_env import allow_ici_all_to_all


def _emit_cc(nc, PFX, kind, src_t, dst_t):
    with Phase(nc, PFX) as ph:
        cc = ph.sem("cc")
        with nc.Block() as block:
            @block.gpsimd
            def _(g):
                if kind == "AllToAll":
                    with allow_ici_all_to_all():
                        g.collective_compute(kind, ALU.bypass,
                                             replica_groups=[list(range(NCORES))],
                                             ins=[src_t.ap()],
                                             outs=[dst_t.ap()]).then_inc(cc)
                else:
                    g.collective_compute(kind, ALU.bypass,
                                         replica_groups=[list(range(NCORES))],
                                         ins=[src_t.ap()],
                                         outs=[dst_t.ap()]).then_inc(cc)
                g.wait_ge(cc, 1)


def build_fused():
    nc = bass.Bass("TRN2", target_bir_lowering=False, debug=False,
                   detect_race_conditions=False)
    x_in = nc.dram_tensor("x", [TCORE, D], F32, kind="ExternalInput")
    wgate_in = nc.dram_tensor("wgate", [D, 7], F32, kind="ExternalInput")
    wst_in = nc.dram_tensor("wst", [D, D], F32, kind="ExternalInput")
    wet_in = nc.dram_tensor("wet", [D, D], F32, kind="ExternalInput")
    bias_in = nc.dram_tensor("bias", [1, D], F32, kind="ExternalInput")
    ones1_in = nc.dram_tensor("ones1", [1, 128], F32, kind="ExternalInput")
    ident_in = nc.dram_tensor("ident", [128, 128], F32, kind="ExternalInput")
    wqkvL_in = nc.dram_tensor("wqkvL", [D, 384], F32, kind="ExternalInput")
    bqkv_in = nc.dram_tensor("bqkvc", [64, 3], F32, kind="ExternalInput")
    bv_in = nc.dram_tensor("bvrow", [1, 64], F32, kind="ExternalInput")
    onec_in = nc.dram_tensor("onec", [128, 1], F32, kind="ExternalInput")
    msl_in = nc.dram_tensor("msl", [128, 128], mybir.dt.uint8, kind="ExternalInput")
    onet_in = nc.dram_tensor("onet", [128, 128], F32, kind="ExternalInput")
    wr_in = nc.dram_tensor("wr", [D, D], F32, kind="ExternalInput")
    br_in = nc.dram_tensor("br", [1, D], F32, kind="ExternalInput")
    wk_in = nc.dram_tensor("wk2h", [D, 128], F32, kind="ExternalInput")
    wv_in = nc.dram_tensor("wv2h", [D, 128], F32, kind="ExternalInput")
    wt_in = nc.dram_tensor("wT", [128, 256], F32, kind="ExternalInput")
    wq_in = nc.dram_tensor("wq", [D, D], F32, kind="ExternalInput")
    wrt_in = nc.dram_tensor("wrt", [D, 8], F32, kind="ExternalInput")
    wgt_in = nc.dram_tensor("wgt", [D, 8], F32, kind="ExternalInput")
    lin_in = nc.dram_tensor("lin", [8, 32, 128, 3072], F32, kind="ExternalInput")
    x2_out = nc.dram_tensor("x2", [TCORE, D], F32, kind="ExternalOutput")
    xn_c = nc.dram_tensor("xn_c", [D, TCORE], BF16)
    xng = nc.dram_tensor("xng", [8 * D, TCORE], BF16)
    gd = nc.dram_tensor("gd", [TCORE, D], BF16)
    s_c = nc.dram_tensor("s_c", [TTOK, 128], BF16)
    s_t = nc.dram_tensor("s_t", [TTOK, 128], BF16)
    x1_d = nc.dram_tensor("x1_d", [TCORE, D], F32)
    xn1_c = nc.dram_tensor("xn1_c", [D, TCORE], BF16)
    xn1g = nc.dram_tensor("xn1g", [8 * D, TCORE], BF16)
    kv_c = nc.dram_tensor("kv_c", [4096, 65], F32)
    kv_t = nc.dram_tensor("kv_t", [4096, 65], F32)
    s1_d = nc.dram_tensor("s1_d", [TCORE, D], BF16)
    cw_d = nc.dram_tensor("cw_d", [TCORE, 8], F32)

    ccA_cm = nc.semaphore("ccA")
    ccA = ccA_cm.__enter__()
    emit_A(nc, "A_", x_in, wgate_in, wst_in, wet_in, bias_in, ones1_in, ident_in,
           xn_c, gd, ag=(xng, ccA))
    emit_B(nc, "B_", None, None, bqkv_in, bv_in, ones1_in, onec_in, msl_in,
           onet_in, s_c, xng=xng, wqkvL_in=wqkvL_in, ccA=ccA)
    _emit_cc(nc, "T1_", "AllToAll", s_c, s_t)
    emit_C1(nc, "C1_", x_in, None, gd, wr_in, br_in, ones1_in, ident_in,
            x1_d, xn1_c, s_t_in=s_t)
    _emit_cc(nc, "G2_", "AllGather", xn1_c, xn1g)
    emit_C2(nc, "C2_", xn1g, wk_in, wv_in, wt_in, onec_in, kv_c,
            gathered=True, dup_out=True)
    _emit_cc(nc, "T2_", "AllToAll", kv_c, kv_t)
    emit_C3a(nc, "C3a_", xn1_c, kv_t, wq_in, wt_in, wrt_in, wgt_in, onec_in,
             ones1_in, s1_d, cw_d, kv_a2a=True)
    emit_C3b(nc, "C3b_", xn1_c, x1_d, s1_d, cw_d, lin_in, x2_out)
    return nc


def _prep_fused_inputs(inputs):
    x = np.asarray(inputs["L0_attn_x"], np.float32).reshape(TTOK, D)
    wst = np.ascontiguousarray(np.asarray(inputs["L0_ffn_W_shared"], np.float32).T)
    wet = np.ascontiguousarray(np.asarray(inputs["L0_ffn_W_expert"], np.float32).T)
    wgate = np.ascontiguousarray(np.asarray(inputs["L0_ffn_W_gate"], np.float32))
    bias = np.ascontiguousarray(np.asarray(inputs["L0_ffn_bias"], np.float32)[None, :])
    ones1 = np.ones((1, 128), np.float32)
    onec = np.ones((128, 1), np.float32)
    ident = np.eye(128, dtype=np.float32)
    jj, ii = np.meshgrid(np.arange(128), np.arange(128), indexing="ij")
    msl = (jj < ii).astype(np.uint8)
    onet = np.ones((128, 128), np.float32)
    wqkv = np.asarray(inputs["L0_attn_Wqkv"], np.float32)
    bq = np.asarray(inputs["L0_attn_bqkv"], np.float32)
    bqkvc = np.ascontiguousarray(np.stack([bq[0:64], bq[64:128], bq[128:192]], 1))
    bvrow = np.ascontiguousarray(bq[None, 128:192])
    wr = np.ascontiguousarray(np.asarray(inputs["L0_attn_Wr"], np.float32))
    br = np.ascontiguousarray(np.asarray(inputs["L0_attn_br"], np.float32)[None, :])
    wk = np.asarray(inputs["L1_attn_Wk"], np.float32)
    wv = np.asarray(inputs["L1_attn_Wv"], np.float32)
    omega = np.asarray(inputs["L1_attn_omega"], np.float32)
    wTd = np.ascontiguousarray(np.concatenate([omega.T, omega.T], 0))
    wq = np.ascontiguousarray(np.asarray(inputs["L1_attn_Wq"], np.float32))
    wrt = np.ascontiguousarray(np.asarray(inputs["L1_ffn_w_router"], np.float32))
    wgt = np.ascontiguousarray(np.asarray(inputs["L1_ffn_w_gating"], np.float32))
    lin = _pack_lin(inputs["L1_ffn_w_gate"], inputs["L1_ffn_w_up"],
                    inputs["L1_ffn_w_down"])
    maps = []
    for c in range(NCORES):
        wqkvL = np.zeros((D, 384), np.float32)
        for hl in range(2):
            h = 2 * c + hl
            wqkvL[64 * h:64 * h + 64, 64 * hl:64 * hl + 64] = wqkv[:, 0:64]
            wqkvL[64 * h:64 * h + 64, 128 + 64 * hl:128 + 64 * hl + 64] = wqkv[:, 64:128]
            wqkvL[64 * h:64 * h + 64, 256 + 64 * hl:256 + 64 * hl + 64] = wqkv[:, 128:192]
        maps.append(dict(
            x=np.ascontiguousarray(x[TCORE * c:TCORE * (c + 1)]),
            wgate=wgate, wst=wst, wet=wet, bias=bias, ones1=ones1, ident=ident,
            wqkvL=wqkvL, bqkvc=bqkvc, bvrow=bvrow, onec=onec, msl=msl, onet=onet,
            wr=wr, br=br,
            wk2h=np.ascontiguousarray(wk[:, 128 * c:128 * c + 128]),
            wv2h=np.ascontiguousarray(wv[:, 128 * c:128 * c + 128]),
            wT=wTd, wq=wq, wrt=wrt, wgt=wgt, lin=lin))
    return maps


def kernel_fused(**inputs):
    res = run_bass_kernel_spmd(_nc("F", build_fused), _prep_fused_inputs(inputs),
                               core_ids=list(range(NCORES)))
    out = np.concatenate([r["x2"] for r in res.results], axis=0)
    return np.ascontiguousarray(out.reshape(2, SEQB, D), dtype=np.float32)
